# version 05
# baseline (speedup 1.0000x reference)
import numpy as np
import ml_dtypes
from contextlib import ExitStack

import concourse.bass as bass
import concourse.mybir as mybir
from concourse.bass_utils import run_bass_kernel_spmd

F32 = mybir.dt.float32
BF16 = mybir.dt.bfloat16
AF = mybir.ActivationFunctionType
ALU = mybir.AluOpType
AX = mybir.AxisListType

P = 128
T = 2048
TS = 512
NTS = 4
NCH = 32
D = 4096
C = 128
NTC = 16
NHC = 32
LNWMIN = -5.0
EPS = 1e-5
QL, QR = 611, 2047
NQ = QR - QL

BF = ml_dtypes.bfloat16


def _host_consts():
    j = np.arange(128)
    cbf = np.zeros((128, 579), dtype=np.float32)
    cbf[:, 0:128] = np.eye(128)
    cbf[:, 128:256] = np.triu(np.ones((128, 128)), 1)
    cbf[:, 256:320] = 1.0
    cbf[:64, 320] = 1.0
    cbf[64:, 321] = 1.0
    cbf[:, 322] = 1.0
    for e in range(8):
        cbf[e, 323 + e * 16:323 + (e + 1) * 16] = 1.0
    cbf[0, 451:515] = 1.0
    cbf[1, 515:579] = 1.0
    cf32 = np.zeros((128, 642), dtype=np.float32)
    cf32[:, 0:128] = np.eye(128)
    cf32[:, 128:256] = j[None, :]
    cf32[:, 256:384] = (j % 64)[None, :]
    cf32[:, 384:512] = (j % 32)[None, :]
    cf32[:, 512:640] = (j % 16)[None, :]
    cf32[:, 640] = 1e-5
    return cbf.astype(BF), cf32, np.ones((1, 512), dtype=BF)


class Sem:
    def __init__(self, es, nc, name):
        self.h = es.enter_context(nc.semaphore(name))
        self.n = 0

    def inc(self, inst, k=1):
        inst.then_inc(self.h, k)
        self.n += k
        return self.n


class KB:
    def __init__(self):
        nc = bass.Bass("TRN2", target_bir_lowering=False)
        self.nc = nc
        self.x0 = nc.dram_tensor("x0", [T, D], F32, kind="ExternalInput").ap()
        self.w = {nm: nc.dram_tensor(nm, [D, D], F32, kind="ExternalInput").ap()
                  for nm in ["wk", "wv", "wr", "wg", "wo", "wf"]}
        self.amat = nc.dram_tensor("amat", [128, D], F32, kind="ExternalInput").ap()
        self.bmat = nc.dram_tensor("bmat", [D, 128], F32, kind="ExternalInput").ap()
        self.rwm = nc.dram_tensor("rwm", [8, D], F32, kind="ExternalInput").ap()
        self.rbr = nc.dram_tensor("rbr", [1, 8], F32, kind="ExternalInput").ap()
        self.brow0 = nc.dram_tensor("brow0", [1, T], F32, kind="ExternalInput").ap()
        self.brow1 = nc.dram_tensor("brow1", [1, T], F32, kind="ExternalInput").ap()
        self.vecs = nc.dram_tensor("vecs", [128, 8 * 32], F32, kind="ExternalInput").ap()
        self.cbf = nc.dram_tensor("cbf", [128, 579], BF16, kind="ExternalInput").ap()
        self.cf32 = nc.dram_tensor("cf32", [128, 642], F32, kind="ExternalInput").ap()
        self.crow = nc.dram_tensor("crow", [1, 512], BF16, kind="ExternalInput").ap()
        self.y = nc.dram_tensor("y", [T, D], F32, kind="ExternalOutput").ap()

        self.xfm = [nc.dram_tensor(f"xfm{i}", [NCH, P, T], BF16, kind="Internal").ap()
                    for i in range(3)]
        self.kvrg = {nm: nc.dram_tensor(nm + "scr", [NCH, P, T], BF16, kind="Internal").ap()
                     for nm in "kvrg"}
        self.obscr = nc.dram_tensor("obscr", [NHC, P, T], BF16, kind="Internal").ap()
        self.wtf = nc.dram_tensor("wtf_scr", [NCH, P, D], BF16, kind="Internal").ap()
        self.atil = nc.dram_tensor("atil_scr", [NCH, P, 128], BF16, kind="Internal").ap()
        self.btil = nc.dram_tensor("btil_scr", [NCH, P, 128], BF16, kind="Internal").ap()
        self.rows_scr = nc.dram_tensor("rows_scr", [4, T], BF16, kind="Internal").ap()

    def snap(self):
        return {k: s.n for k, s in self.S.items()}

    HW_DMA_SEMS = ["qs", "wl", "wl2", "bc", "ob", "kv", "kv2", "wta", "wta2"]
    SW_DMA_SEMS = ["st", "st2", "xc", "xc2", "bt", "bt2", "ws", "ws2", "rbs",
                   "gv", "gv2", "ys", "ys2"]

    def _drain_dmas(self):
        S = self.S
        with self.nc.Block() as blk:
            @blk.sync
            def _(sync):
                for nm in self.HW_DMA_SEMS:
                    if S[nm].n:
                        sync.wait_ge(S[nm].h, S[nm].n)

            @blk.gpsimd
            def _(g_):
                for nm in self.SW_DMA_SEMS:
                    if S[nm].n:
                        g_.wait_ge(S[nm].h, S[nm].n)

    def dinc(self, name, idx, inst):
        self.S[name if idx % 2 == 0 else name + "2"].inc(inst, 16)

    def dwait(self, eng, name, idx, b):
        nm = name if idx % 2 == 0 else name + "2"
        eng.wait_ge(self.S[nm].h, b[nm] + 16 * (idx // 2 + 1))

    def uname(self, base):
        self._uc = getattr(self, "_uc", 0) + 1
        return f"{base}_{self._uc}"

    def build(self, upto=None, debug=()):
        self._upto = upto
        self._debug = tuple(debug)
        nc = self.nc
        if debug:
            self.dbg_out = {}
            scr = {"xfm0": self.xfm[0], "xfm1": self.xfm[1], "xfm2": self.xfm[2],
                   "kscr": self.kvrg["k"], "vscr": self.kvrg["v"],
                   "rscr": self.kvrg["r"], "gscr": self.kvrg["g"],
                   "obscr": self.obscr, "rows": self.rows_scr,
                   "atil": self.atil, "btil": self.btil, "wtf": self.wtf}
            for nm in debug:
                t = scr[nm]
                self.dbg_out[nm] = nc.dram_tensor(
                    "dbg_" + nm, list(t.shape), BF16, kind="ExternalOutput").ap()
        es = ExitStack()
        with es:
            sb = lambda name, shape, dt: es.enter_context(nc.sbuf_tensor(name, shape, dt))
            self.ACT_A = sb("ACT_A", [P, NCH * T], BF16)
            self.RSTD = sb("RSTD", [P, T], BF16)
            self.MURS = sb("MURS", [P, T], BF16)
            self.TSC = sb("TSC", [P, T], BF16)
            self.VECS = sb("VECS", [P, 8 * 32], F32)
            self.CBF = sb("CBF", [P, 579], BF16)
            self.CF = sb("CF", [P, 642], F32)
            self.CROW = sb("CROW", [1, 512], BF16)
            self.RWT = sb("RWT", [P, 256], BF16)
            self.BRB0 = sb("BRB0", [1, T], BF16)
            self.BRB1 = sb("BRB1", [1, T], BF16)
            self.RBR8 = sb("RBR8", [1, 8], F32)
            self.RTBC = sb("RTBC", [P, 1], F32)
            self.QSUM = sb("QSUM", [P, NCH], F32)
            self.QBF = sb("QBF", [P, NCH], BF16)
            self.SROW = sb("SROW", [1, 32], F32)
            self.PBF = sb("PBF", [1, 8], BF16)
            self.PCOL = sb("PCOL", [8, 1], BF16)
            self.RMU = sb("RMU", [2, TS], F32)
            self.RSQ = sb("RSQ", [2, TS], F32)
            self.RVA = sb("RVA", [2, TS], F32)
            self.RSB = sb("RSB", [2, TS], BF16)
            self.RMB = sb("RMB", [2, TS], BF16)

            self.ident = self.CBF[:, 0:128]
            self.masku = self.CBF[:, 128:256]
            self.ones64 = self.CBF[:, 256:320]
            self.hm2 = self.CBF[:, 320:322]
            self.onescol = self.CBF[:, 322:323]
            self.exp8 = self.CBF[0:8, 323:451]
            self.hm2T = self.CBF[0:2, 451:579]
            self.identf = self.CF[:, 0:128]
            self.iota = {128: self.CF[:, 128:256], 64: self.CF[:, 256:384],
                         32: self.CF[:, 384:512], 16: self.CF[:, 512:640]}
            self.tm = {pn: self.VECS[:, i * 32:(i + 1) * 32]
                       for i, pn in enumerate("kvrg")}
            self.gnw = self.VECS[:, 4 * 32:5 * 32]
            self.gnb = self.VECS[:, 5 * 32:6 * 32]
            self.tdv = self.VECS[:, 6 * 32:7 * 32]
            self.faaaa = self.VECS[:, 7 * 32:8 * 32]

            self.PSB = [es.enter_context(nc.psum_tensor(f"PS{i}", [P, 512], F32))
                        for i in range(8)]
            self.S = {nm: Sem(es, nc, "s_" + nm) for nm in
                      ["qs", "wl", "cv", "tpg", "sst", "wt", "mm", "ev", "xc",
                       "st", "bt", "ws", "xt", "yr", "ys", "pp", "tt", "ae",
                       "am", "om", "uf", "sbm", "kv", "pmb", "cd", "ca", "cp",
                       "cg", "ln", "gn", "aux", "ax2", "bc", "ob",
                       "wl2", "xc2", "st2", "bt2", "ys2", "ws2", "kv2",
                       "wta", "wta2", "rbs", "gv", "gv2", "dp"]}

            seq = [
                ("consts", self.p0_consts),
                ("entry", self.p0_entry),
                ("small", self.p0_small),
                ("lnq1", lambda: self.p1_lnq(self.xfm[0])),
                ("router1", self.p1_router),
                ("tsc1", self.p1_tsc),
                ("ffn1", lambda: self.p2_main("ffn", self.w["wf"],
                                              xres=self.xfm[0], xdst=self.xfm[1],
                                              save_wt=True)),
                ("stats1", lambda: self.p_stats(self.xfm[1])),
                ("bcast1", self._bcast_rows),
            ]
            for pn in "kvrg":
                seq.append(("mix_" + pn,
                            (lambda p: lambda: self.p4_mix(p, self.xfm[1]))(pn)))
                seq.append(("proj_" + pn,
                            (lambda p: lambda: self.p2_main(
                                "proj", self.w["w" + p], dst_scr=self.kvrg[p],
                                silu=(p == "g")))(pn)))
            seq += [
                ("wkv", self.p5a_wkv),
                ("gn", self.p5b_gn),
                ("outproj", lambda: self.p2_main("outproj", self.w["wo"],
                                                 xres=self.xfm[1],
                                                 xdst=self.xfm[2])),
                ("stats2", lambda: self.p_stats(self.xfm[2])),
                ("lnq2", lambda: self.p1_lnq(self.xfm[2])),
                ("router2", self.p1_router),
                ("tsc2", self.p1_tsc),
                ("ffn2", lambda: self.p2_main("ffn2", None, xres=self.xfm[2])),
            ]
            for name, fn in seq:
                fn()
                self._drain_dmas()
                if upto is not None and name == upto:
                    break
            if self._debug:
                self._emit_debug()
            self.p_final()
        return nc

    def _emit_debug(self):
        nc, S = self.nc, self.S
        scr = {"xfm0": self.xfm[0], "xfm1": self.xfm[1], "xfm2": self.xfm[2],
               "kscr": self.kvrg["k"], "vscr": self.kvrg["v"],
               "rscr": self.kvrg["r"], "gscr": self.kvrg["g"],
               "obscr": self.obscr, "rows": self.rows_scr,
               "atil": self.atil, "btil": self.btil, "wtf": self.wtf}
        with nc.Block() as blk:
            @blk.sync
            def _(sync):
                for snm in ["qs", "wl", "wl2", "st", "st2", "xc", "xc2", "bt",
                            "bt2", "ws", "ws2", "ys2", "kv", "kv2", "gv", "gv2",
                            "bc", "ob", "rbs", "wta", "wta2"]:
                    if S[snm].n:
                        sync.wait_ge(S[snm].h, S[snm].n)
                for nm in self._debug:
                    srcap = scr[nm]
                    dstap = self.dbg_out[nm]
                    if nm == "rows":
                        srcap = srcap[0:2, :]
                        dstap = dstap[0:2, :]
                    S["qs"].inc(sync.dma_start(dstap, srcap), 16)
                sync.wait_ge(S["qs"].h, S["qs"].n)

    def p0_consts(self):
        nc, S = self.nc, self.S
        with nc.Block() as blk:
            @blk.sync
            def _(sync):
                for dst, src in [(self.CBF[:], self.cbf), (self.CF[:], self.cf32),
                                 (self.CROW[:], self.crow), (self.VECS[:], self.vecs),
                                 (self.RBR8[:], self.rbr)]:
                    S["qs"].inc(sync.dma_start(dst, src), 16)
                sync.wait_ge(S["qs"].h, S["qs"].n)

    def p0_entry(self):
        nc, S, PSB = self.nc, self.S, self.PSB
        NT = 16
        with ExitStack() as ps:
            xtf = [ps.enter_context(nc.sbuf_tensor(self.uname(f"e_xtf{i}"), [P, 2048], F32))
                   for i in range(2)]
            xtb = ps.enter_context(nc.sbuf_tensor(self.uname("e_xtb"), [P, D], BF16))
            stg = [ps.enter_context(nc.sbuf_tensor(self.uname(f"e_stg{i}"), [P, 512], BF16))
                   for i in range(2)]
            sqg = [ps.enter_context(nc.sbuf_tensor(self.uname(f"e_sqg{i}"), [P, 512], BF16))
                   for i in range(2)]
            b = self.snap()
            with nc.Block() as blk:
                @blk.sync
                def _(sync):
                    for i in range(2 * NT):
                        if i >= 2:
                            sync.wait_ge(S["cv"].h, b["cv"] + i - 1)
                        tt, pc = divmod(i, 2)
                        self.dinc("wl", i, sync.dma_start(
                            xtf[i % 2][:], self.x0[tt * 128:(tt + 1) * 128,
                                                   pc * 2048:(pc + 1) * 2048]))

                @blk.scalar
                def _(act):
                    for tt in range(NT):
                        if tt >= 1:
                            act.wait_ge(S["tpg"].h, b["tpg"] + 8 * tt)
                        for pc in range(2):
                            i = tt * 2 + pc
                            self.dwait(act, "wl", i, b)
                            S["cv"].inc(act.activation(
                                xtb[:, pc * 2048:(pc + 1) * 2048],
                                xtf[i % 2][:], AF.Copy))
                        for g in range(8):
                            G = tt * 8 + g
                            act.wait_ge(S["wt"].h, b["wt"] + G + 1)
                            if G >= 2:
                                act.wait_ge(S["sst"].h, b["sst"] + G - 1)
                            S["aux"].inc(act.activation(
                                sqg[G % 2][:], stg[G % 2][:], AF.Square))
                        if tt % 4 == 3:
                            ts = tt // 4
                            act.wait_ge(S["sst"].h, b["sst"] + 8 * (tt + 1))
                            if ts >= 1:
                                act.wait_ge(S["cg"].h, b["cg"] + ts)
                                act.wait_ge(S["rbs"].h, b["rbs"] + 16 * 2 * ts)
                            S["ca"].inc(act.activation(self.RMU[0:1, :],
                                                       PSB[2][0:1, :], AF.Copy,
                                                       scale=1.0 / D))
                            act.wait_ge(S["ca"].h, S["ca"].n)
                            S["ca"].inc(act.activation(self.RSQ[0:1, :],
                                                       self.RMU[0:1, :], AF.Square))
                            act.wait_ge(S["cd"].h, b["cd"] + 1 * (ts + 1))
                            S["ca"].inc(act.activation(self.RVA[0:1, :],
                                                       self.RVA[0:1, :], AF.Ln,
                                                       bias=self.CF[0:1, 640:641]))
                            act.wait_ge(S["ca"].h, S["ca"].n)
                            S["ca"].inc(act.activation(self.RSB[0:1, :],
                                                       self.RVA[0:1, :], AF.Exp,
                                                       scale=-0.5))

                @blk.tensor
                def _(te):
                    for tt in range(NT):
                        te.wait_ge(S["cv"].h, b["cv"] + 2 * tt + 2)
                        if tt % 4 == 0 and tt > 0:
                            te.wait_ge(S["ca"].h, b["ca"] + 4 * (tt // 4))
                            te.wait_ge(S["cd"].h, b["cd"] + (tt // 4))
                        for g in range(8):
                            G = tt * 8 + g
                            if G >= 2:
                                te.wait_ge(S["wt"].h, b["wt"] + G - 1)
                            for m in range(4):
                                jj = g * 4 + m
                                inst = te.matmul(
                                    PSB[G % 2][:, m * 128:(m + 1) * 128],
                                    xtb[:, jj * 128:(jj + 1) * 128], self.ident,
                                    start=True, stop=True)
                            S["tpg"].inc(inst)
                            if g >= 1:
                                self._entry_stats(te, tt, g - 1, b, stg, sqg)
                        self._entry_stats(te, tt, 7, b, stg, sqg)

                @blk.vector
                def _(ve):
                    for tt in range(NT):
                        for g in range(8):
                            G = tt * 8 + g
                            ve.wait_ge(S["tpg"].h, b["tpg"] + G + 1)
                            if G >= 2:
                                ve.wait_ge(S["aux"].h, b["aux"] + G - 1)
                                self.dwait(ve, "st", G - 2, b)
                            S["wt"].inc(ve.tensor_copy(stg[G % 2][:], PSB[G % 2][:]))
                        if tt % 4 == 3:
                            ts = tt // 4
                            ve.wait_ge(S["ca"].h, b["ca"] + 4 * ts + 2)
                            S["cd"].inc(ve.scalar_tensor_tensor(
                                self.RVA[0:1, :], PSB[3][0:1, :], 1.0 / D,
                                self.RSQ[0:1, :], ALU.mult, ALU.subtract))
                            ve.wait_ge(S["ca"].h, b["ca"] + 4 * ts + 4)
                            S["cg"].inc(ve.tensor_mul(self.RMB[0:1, :],
                                                      self.RMU[0:1, :],
                                                      self.RSB[0:1, :]))

                @blk.gpsimd
                def _(g_):
                    for G in range(8 * NT):
                        g_.wait_ge(S["wt"].h, b["wt"] + G + 1)
                        tt, g = divmod(G, 8)
                        dst = self.xfm[0][4 * g:4 * g + 4, :,
                                          tt * 128:(tt + 1) * 128].transpose([1, 0, 2])
                        self.dinc("st", G, g_.dma_start(dst, stg[G % 2][:]))
                        if tt % 4 == 3 and g == 7:
                            ts = tt // 4
                            g_.wait_ge(S["cg"].h, b["cg"] + ts + 1)
                            S["rbs"].inc(g_.dma_start(
                                self.rows_scr[0:1, ts * TS:(ts + 1) * TS],
                                self.RSB[0:1, :]), 16)
                            S["rbs"].inc(g_.dma_start(
                                self.rows_scr[1:2, ts * TS:(ts + 1) * TS],
                                self.RMB[0:1, :]), 16)
                    g_.wait_ge(S["st"].h, S["st"].n)
                    g_.wait_ge(S["st2"].h, S["st2"].n)

    def _entry_stats(self, te, tt, g, b, stg, sqg):
        S, PSB = self.S, self.PSB
        G = tt * 8 + g
        te.wait_ge(S["wt"].h, b["wt"] + G + 1)
        te.wait_ge(S["aux"].h, b["aux"] + G + 1)
        col = (tt % 4) * 128
        first = (g == 0)
        last = (g == 7)
        for m in range(4):
            te.matmul(PSB[2][0:1, col:col + 128], self.onescol,
                      stg[G % 2][:, m * 128:(m + 1) * 128],
                      start=(first and m == 0 and tt % 4 == 0),
                      stop=(last and m == 3 and tt % 4 == 3))
        for m in range(4):
            inst = te.matmul(PSB[3][0:1, col:col + 128], self.onescol,
                             sqg[G % 2][:, m * 128:(m + 1) * 128],
                             start=(first and m == 0 and tt % 4 == 0),
                             stop=(last and m == 3 and tt % 4 == 3))
        S["sst"].inc(inst)

    def p0_small(self):
        nc, S, PSB = self.nc, self.S, self.PSB
        with ExitStack() as ps:
            stf = ps.enter_context(nc.sbuf_tensor(self.uname("s_stf"), [P, 1024], F32))
            stb = ps.enter_context(nc.sbuf_tensor(self.uname("s_stb"), [P, 1024], BF16))
            otb = ps.enter_context(nc.sbuf_tensor(self.uname("s_otb"), [P, 512], BF16))
            brf = ps.enter_context(nc.sbuf_tensor(self.uname("s_brf"), [1, T], F32))
            for strip in range(4):
                self._small_strip(self.amat[:, strip * 1024:(strip + 1) * 1024],
                                  stf, stb, None,
                                  self.atil[strip * 8:(strip + 1) * 8, :, :], 8)
            for strip in range(8):
                src = self.bmat[strip * 512:(strip + 1) * 512, :].rearrange(
                    "(a p) e -> p a e", p=128)
                self._small_strip(src, stf, stb, otb,
                                  self.btil[strip * 4:(strip + 1) * 4, :, :], 4)
            for s4 in range(4):
                with nc.Block() as blk:
                    @blk.sync
                    def _(sync):
                        S["qs"].inc(sync.dma_start(
                            stf[0:8, :], self.rwm[:, s4 * 1024:(s4 + 1) * 1024]), 16)
                        sync.wait_ge(S["qs"].h, S["qs"].n)
                with nc.Block() as blk:
                    @blk.scalar
                    def _(act):
                        act.activation(stb[0:8, :], stf[0:8, :], AF.Copy)
                with nc.Block() as blk:
                    @blk.tensor
                    def _(te):
                        for m in range(8):
                            te.matmul(PSB[0][:, m * 64:m * 64 + 8],
                                      stb[0:8, m * 128:(m + 1) * 128],
                                      self.ident[0:8, 0:8], start=True, stop=True)
                with nc.Block() as blk:
                    @blk.vector
                    def _(ve):
                        for m in range(8):
                            kc = s4 * 8 + m
                            ve.tensor_copy(self.RWT[:, kc * 8:(kc + 1) * 8],
                                           PSB[0][:, m * 64:m * 64 + 8])
            with nc.Block() as blk:
                @blk.sync
                def _(sync):
                    S["qs"].inc(sync.dma_start(brf[:], self.brow0), 16)
                    sync.wait_ge(S["qs"].h, S["qs"].n)
            with nc.Block() as blk:
                @blk.scalar
                def _(act):
                    act.activation(self.BRB0[:], brf[:], AF.Copy)
            with nc.Block() as blk:
                @blk.sync
                def _(sync):
                    S["qs"].inc(sync.dma_start(brf[:], self.brow1), 16)
                    sync.wait_ge(S["qs"].h, S["qs"].n)
            with nc.Block() as blk:
                @blk.scalar
                def _(act):
                    act.activation(self.BRB1[:], brf[:], AF.Copy)

    def _small_strip(self, src, stf, stb, otb, dst_scr, ntile):
        nc, S, PSB = self.nc, self.S, self.PSB
        w = ntile * 128
        out = stb if otb is None else otb
        with nc.Block() as blk:
            @blk.sync
            def _(sync):
                S["qs"].inc(sync.dma_start(stf[:, 0:w], src), 16)
                sync.wait_ge(S["qs"].h, S["qs"].n)
        with nc.Block() as blk:
            @blk.scalar
            def _(act):
                act.activation(stb[:, 0:w], stf[:, 0:w], AF.Copy)
        with nc.Block() as blk:
            @blk.tensor
            def _(te):
                for m in range(ntile):
                    te.matmul(PSB[m // 4 % 2][:, (m % 4) * 128:(m % 4 + 1) * 128],
                              stb[:, m * 128:(m + 1) * 128], self.ident,
                              start=True, stop=True)
        with nc.Block() as blk:
            @blk.vector
            def _(ve):
                for q in range((ntile + 3) // 4):
                    n = min(4, ntile - q * 4) * 128
                    ve.tensor_copy(out[:, q * 512:q * 512 + n], PSB[q % 2][:, 0:n])
        with nc.Block() as blk:
            @blk.sync
            def _(sync):
                S["qs"].inc(sync.dma_start(
                    dst_scr.transpose([1, 0, 2]),
                    out[:, 0:w].rearrange("p (a c) -> p a c", c=128)), 16)
                sync.wait_ge(S["qs"].h, S["qs"].n)

    def _bcast_rows(self):
        nc, S = self.nc, self.S
        with nc.Block() as blk:
            @blk.sync
            def _(sync):
                if S["rbs"].n:
                    sync.wait_ge(S["rbs"].h, S["rbs"].n)
                if S["qs"].n:
                    sync.wait_ge(S["qs"].h, S["qs"].n)
                S["bc"].inc(sync.dma_start(
                    self.RSTD[:], self.rows_scr[0:1, :].broadcast_to([P, T])), 16)
                S["bc"].inc(sync.dma_start(
                    self.MURS[:], self.rows_scr[1:2, :].broadcast_to([P, T])), 16)
                sync.wait_ge(S["bc"].h, S["bc"].n)

    def p1_lnq(self, xsrc):
        nc, S = self.nc, self.S
        self._bcast_rows()
        with ExitStack() as ps:
            xst = [ps.enter_context(nc.sbuf_tensor(self.uname(f"l_x{i}"), [P, T], BF16))
                   for i in range(2)]
            b = self.snap()
            with nc.Block() as blk:
                @blk.gpsimd
                def _(g_):
                    for ch in range(NCH):
                        if ch >= 2:
                            g_.wait_ge(S["cd"].h, b["cd"] + 3 * (ch - 1))
                        self.dinc("xc", ch, g_.dma_start(xst[ch % 2][:],
                                                         xsrc[ch, :, :]))

                @blk.vector
                def _(ve):
                    for ch in range(NCH):
                        a = self.ACT_A[:, ch * T:(ch + 1) * T]
                        self.dwait(ve, "xc", ch, b)
                        S["cd"].inc(ve.tensor_mul(a, xst[ch % 2][:], self.RSTD[:]))
                        ve.wait_ge(S["cd"].h, S["cd"].n)
                        S["cd"].inc(ve.tensor_sub(a, a, self.MURS[:]))
                        ve.wait_ge(S["cd"].h, S["cd"].n)
                        S["cd"].inc(ve.reduce_sum(
                            self.QSUM[:, ch:ch + 1],
                            self.ACT_A[:, ch * T + QL:ch * T + QR], axis=AX.X))

    def p1_router(self):
        nc, S, PSB = self.nc, self.S, self.PSB
        with nc.Block() as blk:
            @blk.scalar
            def _(act):
                act.activation(self.QBF[:], self.QSUM[:], AF.Copy)
        with nc.Block() as blk:
            @blk.tensor
            def _(te):
                for kc in range(NCH):
                    te.matmul(PSB[0][0:1, 0:8], self.QBF[:, kc:kc + 1],
                              self.RWT[:, kc * 8:(kc + 1) * 8],
                              start=(kc == 0), stop=(kc == NCH - 1))
        with nc.Block() as blk:
            @blk.vector
            def _(ve):
                S["cd"].inc(ve.scalar_tensor_tensor(
                    self.SROW[0:1, 0:8], PSB[0][0:1, 0:8], 1.0 / NQ,
                    self.RBR8[:], ALU.mult, ALU.add))
                ve.wait_ge(S["cd"].h, S["cd"].n)
                S["cd"].inc(ve.reduce_max(self.SROW[0:1, 8:9],
                                          self.SROW[0:1, 0:8], axis=AX.X))
                ve.wait_ge(S["cd"].h, S["cd"].n)
                S["cd"].inc(ve.tensor_scalar_mul(self.SROW[0:1, 9:10],
                                                 self.SROW[0:1, 8:9], -1.0))
        with nc.Block() as blk:
            @blk.scalar
            def _(act):
                act.activation(self.SROW[0:1, 16:24], self.SROW[0:1, 0:8],
                               AF.Exp, bias=self.SROW[0:1, 9:10])
        with nc.Block() as blk:
            @blk.vector
            def _(ve):
                S["cd"].inc(ve.reduce_sum(self.SROW[0:1, 10:11],
                                          self.SROW[0:1, 16:24], axis=AX.X))
                ve.wait_ge(S["cd"].h, S["cd"].n)
                S["cd"].inc(ve.reciprocal(self.SROW[0:1, 11:12],
                                          self.SROW[0:1, 10:11]))
                ve.wait_ge(S["cd"].h, S["cd"].n)
                S["cd"].inc(ve.tensor_scalar(self.SROW[0:1, 24:32],
                                             self.SROW[0:1, 16:24],
                                             self.SROW[0:1, 11:12], 2.0,
                                             ALU.mult, ALU.mult))
        with nc.Block() as blk:
            @blk.scalar
            def _(act):
                act.activation(self.PBF[:], self.SROW[0:1, 24:32], AF.Copy)
        with nc.Block() as blk:
            @blk.tensor
            def _(te):
                te.matmul(PSB[1][0:8, 0:1], self.PBF[:], self.CROW[0:1, 0:1],
                          start=True, stop=True)
        with nc.Block() as blk:
            @blk.vector
            def _(ve):
                ve.tensor_copy(self.PCOL[:], PSB[1][0:8, 0:1])
        with nc.Block() as blk:
            @blk.tensor
            def _(te):
                te.matmul(PSB[2][:, 0:1], self.exp8, self.PCOL[:],
                          start=True, stop=True)
        with nc.Block() as blk:
            @blk.vector
            def _(ve):
                ve.tensor_copy(self.RTBC[:], PSB[2][:, 0:1])

    def p1_tsc(self):
        nc, S, PSB = self.nc, self.S, self.PSB
        with ExitStack() as ps:
            atl = ps.enter_context(nc.sbuf_tensor(self.uname("t_atl"), [P, NCH * 128], BF16))
            b = self.snap()
            with nc.Block() as blk:
                @blk.sync
                def _(sync):
                    S["qs"].inc(sync.dma_start(
                        atl[:], self.atil[:, :, :].transpose([1, 0, 2])), 16)
                    sync.wait_ge(S["qs"].h, S["qs"].n)
            with nc.Block() as blk:
                @blk.tensor
                def _(te):
                    for ts in range(NTS):
                        if ts >= 2:
                            te.wait_ge(S["ev"].h, b["ev"] + ts - 1)
                        for kc in range(NCH):
                            inst = te.matmul(
                                PSB[3 + ts % 2][:],
                                atl[:, kc * 128:(kc + 1) * 128],
                                self.ACT_A[:, kc * T + ts * TS:kc * T + (ts + 1) * TS],
                                start=(kc == 0), stop=(kc == NCH - 1))
                        S["mm"].inc(inst)

                @blk.vector
                def _(ve):
                    for ts in range(NTS):
                        ve.wait_ge(S["mm"].h, b["mm"] + ts + 1)
                        S["ev"].inc(ve.tensor_scalar_mul(
                            self.TSC[:, ts * TS:(ts + 1) * TS],
                            PSB[3 + ts % 2][:], self.RTBC[:, 0:1]))

    def p_stats(self, xsrc):
        nc, S, PSB = self.nc, self.S, self.PSB
        with ExitStack() as ps:
            xst = [ps.enter_context(nc.sbuf_tensor(self.uname(f"st_x{i}"), [P, T], BF16))
                   for i in range(2)]
            sqs = [ps.enter_context(nc.sbuf_tensor(self.uname(f"st_q{i}"), [P, T], BF16))
                   for i in range(2)]
            b = self.snap()
            with nc.Block() as blk:
                @blk.gpsimd
                def _(g_):
                    for ch in range(NCH):
                        if ch >= 2:
                            g_.wait_ge(S["sst"].h, b["sst"] + ch - 1)
                        self.dinc("xc", ch, g_.dma_start(xst[ch % 2][:], xsrc[ch, :, :]))

                @blk.scalar
                def _(act):
                    for ch in range(NCH):
                        self.dwait(act, "xc", ch, b)
                        if ch >= 2:
                            act.wait_ge(S["sst"].h, b["sst"] + ch - 1)
                        S["aux"].inc(act.activation(sqs[ch % 2][:], xst[ch % 2][:],
                                                    AF.Square))
                    for ts in range(NTS):
                        act.wait_ge(S["sst"].h, b["sst"] + NCH)
                        if ts >= 1:
                            act.wait_ge(S["cd"].h, b["cd"] + 2 * ts)
                            act.wait_ge(S["qs"].h, b["qs"] + 16 * 2 * ts)
                        S["ca"].inc(act.activation(self.RMU[0:1, :],
                                                   PSB[ts][0:1, :], AF.Copy,
                                                   scale=1.0 / D))
                        act.wait_ge(S["ca"].h, S["ca"].n)
                        S["ca"].inc(act.activation(self.RSQ[0:1, :],
                                                   self.RMU[0:1, :], AF.Square))
                        act.wait_ge(S["cd"].h, b["cd"] + 2 * ts + 1)
                        S["ca"].inc(act.activation(self.RVA[0:1, :],
                                                   self.RVA[0:1, :], AF.Ln,
                                                   bias=self.CF[0:1, 640:641]))
                        act.wait_ge(S["ca"].h, S["ca"].n)
                        S["ca"].inc(act.activation(self.RSB[0:1, :],
                                                   self.RVA[0:1, :], AF.Exp,
                                                   scale=-0.5))

                @blk.tensor
                def _(te):
                    for ch in range(NCH):
                        self.dwait(te, "xc", ch, b)
                        te.wait_ge(S["aux"].h, b["aux"] + ch + 1)
                        for ts in range(NTS):
                            te.matmul(PSB[ts][0:1, :], self.onescol,
                                      xst[ch % 2][:, ts * TS:(ts + 1) * TS],
                                      start=(ch == 0), stop=(ch == NCH - 1))
                        for ts in range(NTS):
                            inst = te.matmul(PSB[4 + ts][0:1, :], self.onescol,
                                             sqs[ch % 2][:, ts * TS:(ts + 1) * TS],
                                             start=(ch == 0), stop=(ch == NCH - 1))
                        S["sst"].inc(inst)

                @blk.vector
                def _(ve):
                    for ts in range(NTS):
                        ve.wait_ge(S["ca"].h, b["ca"] + 4 * ts + 2)
                        S["cd"].inc(ve.scalar_tensor_tensor(
                            self.RVA[0:1, :], PSB[4 + ts][0:1, :], 1.0 / D,
                            self.RSQ[0:1, :], ALU.mult, ALU.subtract))
                        ve.wait_ge(S["ca"].h, b["ca"] + 4 * ts + 4)
                        S["cd"].inc(ve.tensor_mul(self.RMB[0:1, :],
                                                  self.RMU[0:1, :],
                                                  self.RSB[0:1, :]))

                @blk.sync
                def _(sync):
                    for ts in range(NTS):
                        sync.wait_ge(S["cd"].h, b["cd"] + 2 * ts + 2)
                        S["qs"].inc(sync.dma_start(
                            self.rows_scr[0:1, ts * TS:(ts + 1) * TS],
                            self.RSB[0:1, :]), 16)
                        S["qs"].inc(sync.dma_start(
                            self.rows_scr[1:2, ts * TS:(ts + 1) * TS],
                            self.RMB[0:1, :]), 16)
                    sync.wait_ge(S["qs"].h, S["qs"].n)

    def p4_mix(self, pn, xsrc):
        nc, S = self.nc, self.S
        with ExitStack() as ps:
            xst = [ps.enter_context(nc.sbuf_tensor(self.uname(f"m_x{i}"), [P, T], BF16))
                   for i in range(2)]
            lnst = [ps.enter_context(nc.sbuf_tensor(self.uname(f"m_l{i}"), [P, T], BF16))
                    for i in range(2)]
            dst_ = [ps.enter_context(nc.sbuf_tensor(self.uname(f"m_d{i}"), [P, T], BF16))
                    for i in range(2)]
            b = self.snap()
            tmc = self.tm[pn]
            with nc.Block() as blk:
                @blk.gpsimd
                def _(g_):
                    for ch in range(NCH):
                        if ch >= 2:
                            g_.wait_ge(S["cp"].h, b["cp"] + ch - 1)
                        self.dinc("xc", ch, g_.dma_start(xst[ch % 2][:],
                                                         xsrc[ch, :, :]))
                        self.dwait(g_, "xc", ch, b)
                        if ch >= 2:
                            g_.wait_ge(S["cd"].h, b["cd"] + 3 * (ch - 1))
                        S["cp"].inc(g_.tensor_mul(lnst[ch % 2][:], xst[ch % 2][:],
                                                  self.RSTD[:]))

                @blk.vector
                def _(ve):
                    for ch in range(NCH):
                        ln = lnst[ch % 2]
                        dd = dst_[ch % 2]
                        a = self.ACT_A[:, ch * T:(ch + 1) * T]
                        ve.wait_ge(S["cp"].h, b["cp"] + ch + 1)
                        S["cd"].inc(ve.tensor_sub(ln[:], ln[:], self.MURS[:]))
                        ve.wait_ge(S["cd"].h, S["cd"].n)
                        S["cd"].inc(ve.tensor_sub(dd[:, 1:T], ln[:, 1:T],
                                                  ln[:, 0:T - 1]))
                        ve.wait_ge(S["cd"].h, S["cd"].n)
                        ve.scalar_tensor_tensor(a[:, 1:T], dd[:, 1:T],
                                                tmc[:, ch:ch + 1], ln[:, 0:T - 1],
                                                ALU.mult, ALU.add)
                        S["cd"].inc(ve.tensor_scalar_mul(a[:, 0:1], ln[:, 0:1],
                                                         tmc[:, ch:ch + 1]))

    def p2_main(self, mode, Wsrc, xres=None, xdst=None, dst_scr=None,
                silu=False, save_wt=False):
        nc, S, PSB = self.nc, self.S, self.PSB
        lora = mode in ("ffn", "ffn2")
        from_scr = (mode == "ffn2")
        with ExitStack() as ps:
            wT = [ps.enter_context(nc.sbuf_tensor(self.uname(f"w_t{i}"), [P, D], BF16))
                  for i in range(2)]
            if not from_scr:
                wfp = [ps.enter_context(nc.sbuf_tensor(self.uname(f"w_f{i}"), [P, 1024], F32))
                       for i in range(2)]
                wbf = ps.enter_context(nc.sbuf_tensor(self.uname("w_b"), [P, D], BF16))
            if lora:
                btl = [ps.enter_context(nc.sbuf_tensor(self.uname(f"w_bt{i}"), [P, 128], BF16))
                       for i in range(2)]
            if mode != "proj":
                xcb = [ps.enter_context(nc.sbuf_tensor(self.uname(f"w_xc{i}"), [P, TS], BF16))
                       for i in range(2)]
            if mode == "ffn2":
                fin = [ps.enter_context(nc.sbuf_tensor(self.uname(f"w_fi{i}"), [P, TS], F32))
                       for i in range(2)]
                yro = [ps.enter_context(nc.sbuf_tensor(self.uname(f"w_yr{i}"), [P, TS], F32))
                       for i in range(2)]
            else:
                ob = [ps.enter_context(nc.sbuf_tensor(self.uname(f"w_ob{i}"), [P, TS], BF16))
                      for i in range(2)]
            b = self.snap()

            with nc.Block() as blk:
                @blk.sync
                def _(sync):
                    if from_scr:
                        for och in range(NCH):
                            if och >= 2:
                                sync.wait_ge(S["mm"].h, b["mm"] + 4 * (och - 1))
                            self.dinc("wta", och, sync.dma_start(
                                wT[och % 2][:], self.wtf[och, :, :]))
                    else:
                        for och in range(NCH):
                            for pc in range(4):
                                i = och * 4 + pc
                                if i >= 2:
                                    sync.wait_ge(S["cv"].h, b["cv"] + i - 1)
                                self.dinc("wl", i, sync.dma_start(
                                    wfp[i % 2][:],
                                    Wsrc[och * 128:(och + 1) * 128,
                                         pc * 1024:(pc + 1) * 1024]))

                if not from_scr:
                    @blk.scalar
                    def _(act):
                        for och in range(NCH):
                            if och >= 1:
                                act.wait_ge(S["tpg"].h, b["tpg"] + 8 * och)
                            for pc in range(4):
                                i = och * 4 + pc
                                self.dwait(act, "wl", i, b)
                                S["cv"].inc(act.activation(
                                    wbf[:, pc * 1024:(pc + 1) * 1024],
                                    wfp[i % 2][:], AF.Copy))
                            if silu and och >= 1:
                                self._silu_evac(act, och - 1, b, ob)
                        if silu:
                            self._silu_evac(act, NCH - 1, b, ob)

                @blk.tensor
                def _(te):
                    for och in range(NCH):
                        if not from_scr:
                            te.wait_ge(S["cv"].h, b["cv"] + 4 * och + 4)
                            for g in range(8):
                                G = och * 8 + g
                                if G >= 2:
                                    te.wait_ge(S["wt"].h, b["wt"] + G - 1)
                                for m in range(4):
                                    jj = g * 4 + m
                                    inst = te.matmul(
                                        PSB[2 + G % 2][:, m * 128:(m + 1) * 128],
                                        wbf[:, jj * 128:(jj + 1) * 128], self.ident,
                                        start=True, stop=True)
                                S["tpg"].inc(inst)
                            te.wait_ge(S["wt"].h, b["wt"] + 8 * (och + 1))
                        else:
                            self.dwait(te, "wta", och, b)
                        if lora:
                            self.dwait(te, "bt", och, b)
                        for ts in range(NTS):
                            I = och * 4 + ts
                            if I >= 2:
                                te.wait_ge(S["ev"].h, b["ev"] + I - 1)
                            for kc in range(NCH):
                                inst = te.matmul(
                                    PSB[I % 2][:],
                                    wT[och % 2][:, kc * 128:(kc + 1) * 128],
                                    self.ACT_A[:, kc * T + ts * TS:kc * T + (ts + 1) * TS],
                                    start=(kc == 0),
                                    stop=(kc == NCH - 1 and not lora))
                            if lora:
                                te.matmul(PSB[I % 2][:], btl[och % 2][:],
                                          self.TSC[:, ts * TS:(ts + 1) * TS],
                                          start=False, stop=False)
                                brow_src = self.BRB0 if och < 16 else self.BRB1
                                cc = (och % 16) * 128
                                inst = te.matmul(PSB[I % 2][:],
                                                 brow_src[0:1, cc:cc + 128],
                                                 self.CROW[0:1, :],
                                                 start=False, stop=True)
                            S["mm"].inc(inst)
                            if mode == "ffn2" and I >= 1:
                                self._exit_tp(te, I - 1, b, fin)
                    if mode == "ffn2":
                        self._exit_tp(te, 4 * NCH - 1, b, fin)

                @blk.vector
                def _(ve):
                    for och in range(NCH):
                        if not from_scr:
                            for g in range(8):
                                G = och * 8 + g
                                ve.wait_ge(S["tpg"].h, b["tpg"] + G + 1)
                                if och >= 2 and g == 0:
                                    ve.wait_ge(S["mm"].h, b["mm"] + 4 * (och - 1))
                                    if save_wt:
                                        self.dwait(ve, "ws", och - 2, b)
                                S["wt"].inc(ve.tensor_copy(
                                    wT[och % 2][:, g * 512:(g + 1) * 512],
                                    PSB[2 + G % 2][:]))
                        if silu:
                            continue
                        for ts in range(NTS):
                            I = och * 4 + ts
                            ve.wait_ge(S["mm"].h, b["mm"] + I + 1)
                            if mode != "proj":
                                self.dwait(ve, "xc", I, b)
                            if I >= 2 and mode != "ffn2":
                                self.dwait(ve, "st", I - 2, b)
                            if mode == "ffn":
                                S["ev"].inc(ve.scalar_tensor_tensor(
                                    ob[I % 2][:], PSB[I % 2][:], 0.5, xcb[I % 2][:],
                                    ALU.mult, ALU.add))
                            elif mode == "outproj":
                                S["ev"].inc(ve.tensor_add(
                                    ob[I % 2][:], PSB[I % 2][:], xcb[I % 2][:]))
                            elif mode == "proj":
                                S["ev"].inc(ve.tensor_copy(ob[I % 2][:],
                                                           PSB[I % 2][:]))
                            else:
                                if I >= 2:
                                    ve.wait_ge(S["xt"].h, b["xt"] + I - 1)
                                S["ev"].inc(ve.scalar_tensor_tensor(
                                    fin[I % 2][:], PSB[I % 2][:], 0.5, xcb[I % 2][:],
                                    ALU.mult, ALU.add))
                        if mode == "ffn2":
                            for ts in range(NTS):
                                I = och * 4 + ts
                                ve.wait_ge(S["xt"].h, b["xt"] + I + 1)
                                if I >= 2:
                                    self.dwait(ve, "ys", I - 2, b)
                                S["yr"].inc(ve.tensor_copy(yro[I % 2][:],
                                                           PSB[4 + I % 2][:]))

                @blk.gpsimd
                def _(g_):
                    def ystore(I2):
                        och2, ts2 = divmod(I2, 4)
                        g_.wait_ge(S["yr"].h, b["yr"] + I2 + 1)
                        dsty = self.y[ts2 * TS:(ts2 + 1) * TS,
                                      och2 * 128:(och2 + 1) * 128].rearrange(
                                          "(jj p) o -> p jj o", p=128)
                        self.dinc("ys", I2, g_.dma_start(
                            dsty, yro[I2 % 2][:].rearrange(
                                "p (jj o) -> p jj o", o=128)))
                    for och in range(NCH):
                        if lora:
                            if och >= 2:
                                g_.wait_ge(S["mm"].h, b["mm"] + 4 * (och - 1))
                            self.dinc("bt", och, g_.dma_start(btl[och % 2][:],
                                                              self.btil[och, :, :]))
                        for ts in range(NTS):
                            I = och * 4 + ts
                            if mode != "proj":
                                if I >= 2:
                                    g_.wait_ge(S["ev"].h, b["ev"] + I - 1)
                                self.dinc("xc", I, g_.dma_start(
                                    xcb[I % 2][:],
                                    xres[och, :, ts * TS:(ts + 1) * TS]))
                            if mode in ("ffn", "outproj"):
                                g_.wait_ge(S["ev"].h, b["ev"] + I + 1)
                                self.dinc("st", I, g_.dma_start(
                                    xdst[och, :, ts * TS:(ts + 1) * TS],
                                    ob[I % 2][:]))
                            elif mode == "proj":
                                g_.wait_ge(S["ev"].h, b["ev"] + I + 1)
                                self.dinc("st", I, g_.dma_start(
                                    dst_scr[och, :, ts * TS:(ts + 1) * TS],
                                    ob[I % 2][:]))
                            else:
                                if och >= 1:
                                    ystore((och - 1) * 4 + ts)
                        if save_wt:
                            g_.wait_ge(S["wt"].h, b["wt"] + 8 * (och + 1))
                            self.dinc("ws", och, g_.dma_start(self.wtf[och, :, :],
                                                              wT[och % 2][:]))
                    if mode == "ffn2":
                        for ts in range(NTS):
                            ystore((NCH - 1) * 4 + ts)

    def _silu_evac(self, act, och, b, ob):
        S, PSB = self.S, self.PSB
        for ts in range(NTS):
            I = och * 4 + ts
            act.wait_ge(S["mm"].h, b["mm"] + I + 1)
            if I >= 2:
                self.dwait(act, "st", I - 2, b)
            S["ev"].inc(act.activation(ob[I % 2][:], PSB[I % 2][:], AF.Silu))

    def _exit_tp(self, te, I, b, fin):
        S, PSB = self.S, self.PSB
        te.wait_ge(S["ev"].h, b["ev"] + I + 1)
        if I >= 2:
            te.wait_ge(S["yr"].h, b["yr"] + I - 1)
        for m in range(4):
            inst = te.matmul(PSB[4 + I % 2][:, m * 128:(m + 1) * 128],
                             fin[I % 2][:, m * 128:(m + 1) * 128], self.identf,
                             start=True, stop=True)
        S["xt"].inc(inst)

    def p5a_wkv(self):
        nc, S, PSB = self.nc, self.S, self.PSB
        with ExitStack() as ps:
            kvb = {nm: [ps.enter_context(nc.sbuf_tensor(self.uname(f"k_{nm}{i}"), [P, T], BF16))
                        for i in range(2)] for nm in "kvr"}
            obb = ps.enter_context(nc.sbuf_tensor(self.uname("k_ob"), [P, T], BF16))
            PM = [ps.enter_context(nc.sbuf_tensor(self.uname(f"k_pm{i}"), [P, 10 * 128], BF16))
                  for i in range(2)]
            SCL = [ps.enter_context(nc.sbuf_tensor(self.uname(f"k_sc{i}"), [P, 11 * 128], BF16))
                   for i in range(2)]
            UM = [ps.enter_context(nc.sbuf_tensor(self.uname(f"k_um{i}"), [P, 64], BF16))
                  for i in range(2)]
            vTT = [ps.enter_context(nc.sbuf_tensor(self.uname(f"k_vt{i}"), [P, 256], BF16))
                   for i in range(2)]
            ASB = [ps.enter_context(nc.sbuf_tensor(self.uname(f"k_as{i}"), [P, 256], BF16))
                   for i in range(2)]
            TMPU = [ps.enter_context(nc.sbuf_tensor(self.uname(f"k_tu{i}"), [P, 128], BF16))
                    for i in range(2)]
            S0 = ps.enter_context(nc.sbuf_tensor(self.uname("k_s0"), [P, 64], F32))
            S0b = ps.enter_context(nc.sbuf_tensor(self.uname("k_s0b"), [P, 64], BF16))
            LNW = [ps.enter_context(nc.sbuf_tensor(self.uname(f"k_lnw{i}"), [P, 10], F32))
                   for i in range(2)]
            b = self.snap()

            def X(hc, c):
                return hc * NTC + c

            def scl_of(Xc):
                return SCL[Xc % 2]

            with nc.Block() as blk:
                @blk.sync
                def _(sync):
                    for hc in range(NHC):
                        if hc >= 2:
                            sync.wait_ge(S["uf"].h, b["uf"] + (hc - 1) * NTC)
                        for nm in "kvr":
                            self.dinc("kv", hc, sync.dma_start(
                                kvb[nm][hc % 2][:], self.kvrg[nm][hc, :, :]))
                        if hc >= 1:
                            sync.wait_ge(S["uf"].h, b["uf"] + hc * NTC)
                            S["ob"].inc(sync.dma_start(
                                self.obscr[hc - 1, :, :], obb[:]), 16)
                    sync.wait_ge(S["uf"].h, b["uf"] + NHC * NTC)
                    S["ob"].inc(sync.dma_start(self.obscr[NHC - 1, :, :], obb[:]), 16)
                    sync.wait_ge(S["ob"].h, S["ob"].n)

                @blk.scalar
                def _(act):
                    def pm_build(h):
                        L = LNW[h % 2]
                        S["ax2"].inc(act.activation(L[:, 0:1],
                                                    self.tdv[:, h:h + 1], AF.Exp))
                        act.wait_ge(S["cd"].h, b["cd"] + 3 * (h + 1))
                        if h >= 2:
                            act.wait_ge(S["pp"].h, b["pp"] + (h - 1) * NTC)
                        pm = PM[h % 2]
                        act.activation(pm[:, 0:128], self.iota[128], AF.Exp,
                                       scale=L[:, 1:2])
                        act.activation(pm[:, 128:256], self.iota[64], AF.Exp,
                                       scale=L[:, 1:2])
                        act.activation(pm[:, 256:384], self.iota[32], AF.Exp,
                                       scale=L[:, 1:2])
                        act.activation(pm[:, 384:512], self.iota[16], AF.Exp,
                                       scale=L[:, 1:2])
                        act.activation(pm[:, 512:640], self.iota[16], AF.Exp,
                                       scale=L[:, 2:3])
                        act.activation(pm[:, 640:768], self.iota[128], AF.Exp,
                                       scale=L[:, 0:1], bias=L[:, 4:5])
                        act.activation(pm[:, 768:896], self.iota[64], AF.Exp,
                                       scale=L[:, 0:1], bias=L[:, 5:6])
                        act.activation(pm[:, 896:1024], self.iota[32], AF.Exp,
                                       scale=L[:, 0:1], bias=L[:, 6:7])
                        act.activation(pm[:, 1024:1152], self.iota[16], AF.Exp,
                                       scale=L[:, 0:1], bias=L[:, 7:8])
                        act.activation(pm[:, 1152:1280], self.iota[16], AF.Exp,
                                       scale=L[:, 3:4], bias=L[:, 3:4])
                        S["pmb"].inc(act.activation(L[:, 8:9], L[:, 1:2],
                                                    AF.Exp, scale=128.0))
                    pm_build(0)
                    for hc in range(NHC):
                        act.wait_ge(S["sbm"].h, b["sbm"] + hc * (NTC + 1))
                        S["sbm"].inc(act.memzero(S0b[:]))
                        for c in range(NTC):
                            Xc = X(hc, c)
                            act.wait_ge(S["tt"].h, b["tt"] + Xc + 1)
                            act.activation(vTT[c % 2][:, 0:128],
                                           PSB[c % 2][:, 0:128], AF.Copy)
                            S["ae"].inc(act.activation(vTT[c % 2][:, 128:256],
                                                       PSB[c % 2][:, 128:256],
                                                       AF.Copy))
                            if c == 0 and hc + 1 < NHC:
                                pm_build(hc + 1)
                            act.wait_ge(S["uf"].h, b["uf"] + Xc + 1)
                            S["sbm"].inc(act.activation(S0b[:], S0[:], AF.Copy))

                @blk.gpsimd
                def _(g_):
                    for hc in range(NHC):
                        g_.wait_ge(S["pmb"].h, b["pmb"] + hc + 1)
                        nmkv = "kv" if hc % 2 == 0 else "kv2"
                        g_.wait_ge(S[nmkv].h, b[nmkv] + 48 * (hc // 2 + 1))
                        pm = PM[hc % 2]
                        g_.tensor_scalar_mul(UM[hc % 2][:], self.ones64,
                                             self.faaaa[:, hc:hc + 1])
                        for c in range(NTC):
                            Xc = X(hc, c)
                            sc = SCL[Xc % 2]
                            if Xc >= 2:
                                g_.wait_ge(S["om"].h, b["om"] + Xc - 1)
                            ksl = kvb["k"][hc % 2][:, c * C:(c + 1) * C]
                            g_.tensor_tensor(sc[:, 640:768], ksl, pm[:, 768:896], ALU.mult)
                            g_.tensor_tensor(sc[:, 768:896], ksl, pm[:, 896:1024], ALU.mult)
                            g_.tensor_tensor(sc[:, 896:1024], ksl, pm[:, 1024:1152], ALU.mult)
                            S["pp"].inc(g_.tensor_tensor(sc[:, 1152:1280], ksl,
                                                         pm[:, 640:768], ALU.mult))

                def emit_G1(te, hc, c):
                    Xc = X(hc, c)
                    sc = scl_of(Xc)
                    te.wait_ge(S["pp"].h, b["pp"] + Xc + 1)
                    te.wait_ge(S["dp"].h, b["dp"] + Xc + 1)
                    if Xc >= 2:
                        te.wait_ge(S["ae"].h, b["ae"] + Xc - 1)
                        te.wait_ge(S["cg"].h, b["cg"] + Xc - 1)
                        te.wait_ge(S["am"].h, b["am"] + Xc - 1)
                    vsl = kvb["v"][hc % 2][:, c * C:(c + 1) * C]
                    rt = sc[:, 0:128]
                    r64, r32, r16 = sc[:, 128:256], sc[:, 256:384], sc[:, 384:512]
                    rcl, kcl = sc[:, 512:640], sc[:, 1024:1152]
                    k64, k32, k16 = sc[:, 640:768], sc[:, 768:896], sc[:, 896:1024]
                    kg = sc[:, 1152:1280]
                    rk = sc[:, 1280:1408]
                    te.matmul(PSB[c % 2][:, 0:128], vsl, self.ident,
                              start=True, stop=True)
                    te.matmul(PSB[c % 2][:, 128:256], kg, self.ident,
                              start=True, stop=True)
                    pA = PSB[2 + c % 2]
                    for h in range(2):
                        po = 64 * h
                        co = 128 * h
                        te.matmul(pA[:, co:co + 128], kcl[po:po + 64, :],
                                  rcl[po:po + 64, :], start=True, stop=True,
                                  tile_position=(po, 0))
                        te.matmul(pA[0:64, co + 64:co + 128], k64[po:po + 64, 0:64],
                                  r64[po:po + 64, 64:128],
                                  start=True, stop=True, tile_position=(po, 0))
                        te.matmul(pA[0:32, co + 32:co + 64], k32[po:po + 64, 0:32],
                                  r32[po:po + 64, 32:64],
                                  start=True, stop=True, tile_position=(po, 0))
                        te.matmul(pA[64:96, co + 96:co + 128], k32[po:po + 64, 64:96],
                                  r32[po:po + 64, 96:128],
                                  start=True, stop=True, tile_position=(po, 64))
                        for m in range(4):
                            j0 = 32 * m
                            te.matmul(pA[j0:j0 + 16, co + j0 + 16:co + j0 + 32],
                                      k16[po:po + 64, j0:j0 + 16],
                                      r16[po:po + 64, j0 + 16:j0 + 32],
                                      start=True, stop=True,
                                      tile_position=(po, j0))
                        inst = te.matmul(PSB[4 + c % 2][po:po + 64, 0:128],
                                         UM[hc % 2][po:po + 64, :],
                                         rk[po:po + 64, :],
                                         start=True, stop=True,
                                         tile_position=(po, po))
                    S["tt"].inc(inst)

                def emit_OM(te, hc, c):
                    Xc = X(hc, c)
                    sc = scl_of(Xc)
                    rt = sc[:, 0:128]
                    te.wait_ge(S["am"].h, b["am"] + Xc + 1)
                    te.wait_ge(S["ae"].h, b["ae"] + Xc + 1)
                    te.wait_ge(S["sbm"].h, b["sbm"] + hc * (NTC + 1) + 1 + c)
                    for h in range(2):
                        po = 64 * h
                        te.matmul(PSB[6][po:po + 64, 0:128],
                                  vTT[c % 2][:, po:po + 64],
                                  ASB[c % 2][:, h * 128:(h + 1) * 128],
                                  start=True, stop=False, tile_position=(0, po))
                        te.matmul(PSB[6][po:po + 64, 0:128],
                                  S0b[po:po + 64, :], rt[po:po + 64, :],
                                  start=False, stop=True, tile_position=(po, po))
                        inst = te.matmul(PSB[7][po:po + 64, 0:64],
                                         vTT[c % 2][:, 128 + po:128 + po + 64],
                                         vTT[c % 2][:, po:po + 64],
                                         start=True, stop=True,
                                         tile_position=(0, po))
                    S["om"].inc(inst)

                @blk.tensor
                def _(te):
                    prev = None
                    for hc in range(NHC):
                        for c in range(NTC):
                            emit_G1(te, hc, c)
                            if prev is not None:
                                emit_OM(te, *prev)
                            prev = (hc, c)
                    emit_OM(te, *prev)

                @blk.vector
                def _(ve):
                    def lnw_chain(h):
                        L = LNW[h % 2]
                        ve.wait_ge(S["ax2"].h, b["ax2"] + h + 1)
                        S["cd"].inc(ve.tensor_scalar_mul(L[:, 1:2], L[:, 0:1], -1.0))
                        ve.wait_ge(S["cd"].h, S["cd"].n)
                        S["cd"].inc(ve.tensor_scalar_max(L[:, 2:3], L[:, 1:2],
                                                         LNWMIN))
                        ve.wait_ge(S["cd"].h, S["cd"].n)
                        ve.tensor_scalar_mul(L[:, 3:4], L[:, 2:3], -1.0)
                        ve.tensor_scalar_mul(L[:, 4:5], L[:, 1:2], 127.0)
                        ve.tensor_scalar_mul(L[:, 5:6], L[:, 1:2], 63.0)
                        ve.tensor_scalar_mul(L[:, 6:7], L[:, 1:2], 31.0)
                        S["cd"].inc(ve.tensor_scalar_mul(L[:, 7:8], L[:, 1:2], 15.0))

                    def preps(Xc):
                        hc, c = divmod(Xc, NTC)
                        sc = SCL[Xc % 2]
                        pm = PM[hc % 2]
                        if c == 0:
                            ve.wait_ge(S["pmb"].h, b["pmb"] + hc + 1)
                            nmkv = "kv" if hc % 2 == 0 else "kv2"
                            ve.wait_ge(S[nmkv].h, b[nmkv] + 48 * (hc // 2 + 1))
                        if Xc >= 2:
                            ve.wait_ge(S["om"].h, b["om"] + Xc - 1)
                        rsl = kvb["r"][hc % 2][:, c * C:(c + 1) * C]
                        ksl = kvb["k"][hc % 2][:, c * C:(c + 1) * C]
                        ve.tensor_tensor(sc[:, 0:128], rsl, pm[:, 0:128], ALU.mult)
                        ve.tensor_tensor(sc[:, 128:256], rsl, pm[:, 128:256], ALU.mult)
                        ve.tensor_tensor(sc[:, 256:384], rsl, pm[:, 256:384], ALU.mult)
                        ve.tensor_tensor(sc[:, 384:512], rsl, pm[:, 384:512], ALU.mult)
                        ve.tensor_tensor(sc[:, 512:640], rsl, pm[:, 512:640], ALU.mult)
                        ve.tensor_tensor(sc[:, 1024:1152], ksl, pm[:, 1152:1280],
                                         ALU.mult)
                        S["dp"].inc(ve.tensor_tensor(sc[:, 1280:1408], rsl, ksl,
                                                     ALU.mult))

                    def tail(Xc):
                        hc, c = divmod(Xc, NTC)
                        if c == 0:
                            lnw_chain(hc) if False else None
                            ve.wait_ge(S["sbm"].h,
                                       b["sbm"] + hc * (NTC + 1) + 1)
                            if hc >= 1:
                                ve.wait_ge(S["ob"].h, b["ob"] + 16 * hc)
                            ve.memset(S0[:], 0.0)
                        wc = LNW[hc % 2][:, 8:9]
                        ve.wait_ge(S["tt"].h, b["tt"] + Xc + 1)
                        pA = PSB[2 + c % 2]
                        ve.tensor_tensor(ASB[c % 2][:, 0:128], pA[:, 0:128],
                                         self.masku, ALU.mult)
                        S["am"].inc(ve.tensor_tensor(ASB[c % 2][:, 128:256],
                                                     pA[:, 128:256],
                                                     self.masku, ALU.mult))
                        ve.wait_ge(S["om"].h, b["om"] + Xc + 1)
                        vsl = kvb["v"][hc % 2][:, c * C:(c + 1) * C]
                        S["cg"].inc(ve.tensor_tensor(TMPU[c % 2][:], vsl,
                                                     PSB[4 + c % 2][:, 0:128],
                                                     ALU.mult))
                        ve.wait_ge(S["cg"].h, S["cg"].n)
                        ve.tensor_tensor(obb[:, c * C:(c + 1) * C],
                                         PSB[6][:, 0:128], TMPU[c % 2][:],
                                         ALU.add)
                        S["uf"].inc(ve.scalar_tensor_tensor(
                            S0[:], S0[:], wc, PSB[7][:, 0:64],
                            ALU.mult, ALU.add))

                    lnw_chain(0)
                    for Xc in range(NHC * NTC):
                        hc, c = divmod(Xc, NTC)
                        if c == 1 and hc + 1 < NHC:
                            lnw_chain(hc + 1)
                        preps(Xc)
                        if Xc >= 1:
                            tail(Xc - 1)
                    tail(NHC * NTC - 1)

    def p5b_gn(self):
        nc, S, PSB = self.nc, self.S, self.PSB
        with ExitStack() as ps:
            obb = [ps.enter_context(nc.sbuf_tensor(self.uname(f"g_ob{i}"), [P, T], BF16))
                   for i in range(2)]
            sqb = [ps.enter_context(nc.sbuf_tensor(self.uname(f"g_sq{i}"), [P, T], BF16))
                   for i in range(2)]
            gb = [ps.enter_context(nc.sbuf_tensor(self.uname(f"g_g{i}"), [P, T], BF16))
                  for i in range(2)]
            b = self.snap()
            with nc.Block() as blk:
                @blk.gpsimd
                def _(g_):
                    def t3(hc2):
                        g_.wait_ge(S["cg"].h, b["cg"] + 8 * hc2 + 8)
                        S["ax2"].inc(g_.tensor_scalar(
                            sqb[hc2 % 2][:], sqb[hc2 % 2][:],
                            self.gnw[:, hc2:hc2 + 1], self.gnb[:, hc2:hc2 + 1],
                            ALU.mult, ALU.add))
                    for hc in range(NHC):
                        if hc >= 2:
                            g_.wait_ge(S["gn"].h, b["gn"] + hc - 1)
                        self.dinc("xc", hc, g_.dma_start(obb[hc % 2][:],
                                                         self.obscr[hc, :, :]))
                        self.dinc("gv", hc, g_.dma_start(gb[hc % 2][:],
                                                         self.kvrg["g"][hc, :, :]))
                        if hc >= 1:
                            t3(hc - 1)
                    t3(NHC - 1)

                @blk.scalar
                def _(act):
                    for hc in range(NHC):
                        self.dwait(act, "xc", hc, b)
                        if hc >= 2:
                            act.wait_ge(S["sst"].h, b["sst"] + hc - 1)
                        S["aux"].inc(act.activation(sqb[hc % 2][:], obb[hc % 2][:],
                                                    AF.Square))
                        for ts in range(NTS):
                            slot = 4 * hc + ts
                            act.wait_ge(S["sst"].h, b["sst"] + hc + 1)
                            if slot >= 1:
                                act.wait_ge(S["cd"].h, b["cd"] + 2 * slot)
                                act.wait_ge(S["ln"].h, b["ln"] + slot)
                            S["ca"].inc(act.activation(self.RMU[0:2, :],
                                                       PSB[ts][0:2, :], AF.Copy,
                                                       scale=1.0 / 64))
                            act.wait_ge(S["ca"].h, S["ca"].n)
                            S["ca"].inc(act.activation(self.RSQ[0:2, :],
                                                       self.RMU[0:2, :], AF.Square))
                            act.wait_ge(S["cd"].h, b["cd"] + 2 * slot + 1)
                            S["ca"].inc(act.activation(self.RVA[0:2, :],
                                                       self.RVA[0:2, :], AF.Ln,
                                                       bias=self.CF[0:2, 640:641]))
                            act.wait_ge(S["ca"].h, S["ca"].n)
                            S["ca"].inc(act.activation(self.RSB[0:2, :],
                                                       self.RVA[0:2, :], AF.Exp,
                                                       scale=-0.5))

                @blk.tensor
                def _(te):
                    for hc in range(NHC):
                        self.dwait(te, "xc", hc, b)
                        te.wait_ge(S["aux"].h, b["aux"] + hc + 1)
                        if hc >= 1:
                            te.wait_ge(S["cg"].h, b["cg"] + 8 * hc)
                        for ts in range(NTS):
                            te.matmul(PSB[ts][0:2, :], self.hm2,
                                      obb[hc % 2][:, ts * TS:(ts + 1) * TS],
                                      start=True, stop=True)
                        for ts in range(NTS):
                            inst = te.matmul(PSB[4 + ts][0:2, :], self.hm2,
                                             sqb[hc % 2][:, ts * TS:(ts + 1) * TS],
                                             start=True, stop=True)
                        S["sst"].inc(inst)
                        for ts in range(NTS):
                            slot = 4 * hc + ts
                            te.wait_ge(S["cd"].h, b["cd"] + 2 * slot + 2)
                            te.matmul(PSB[ts][:, 0:512], self.hm2T, self.RSB[0:2, :],
                                      start=True, stop=True)
                            inst = te.matmul(PSB[4 + ts][:, 0:512], self.hm2T,
                                             self.RMB[0:2, :],
                                             start=True, stop=True)
                            S["ln"].inc(inst)

                @blk.vector
                def _(ve):
                    for hc in range(NHC):
                        for ts in range(NTS):
                            slot = 4 * hc + ts
                            ve.wait_ge(S["ca"].h, b["ca"] + 16 * hc + 4 * ts + 2)
                            S["cd"].inc(ve.scalar_tensor_tensor(
                                self.RVA[0:2, :], PSB[4 + ts][0:2, :], 1.0 / 64,
                                self.RSQ[0:2, :], ALU.mult, ALU.subtract))
                            ve.wait_ge(S["ca"].h, b["ca"] + 16 * hc + 4 * ts + 4)
                            S["cd"].inc(ve.tensor_mul(self.RMB[0:2, :],
                                                      self.RMU[0:2, :],
                                                      self.RSB[0:2, :]))
                        for ts in range(NTS):
                            slot = 4 * hc + ts
                            ve.wait_ge(S["ln"].h, b["ln"] + slot + 1)
                            sl = slice(ts * TS, (ts + 1) * TS)
                            S["cg"].inc(ve.tensor_tensor(
                                sqb[hc % 2][:, sl], obb[hc % 2][:, sl],
                                PSB[ts][:, 0:512], ALU.mult))
                            ve.wait_ge(S["cg"].h, S["cg"].n)
                            S["cg"].inc(ve.tensor_tensor(
                                sqb[hc % 2][:, sl], sqb[hc % 2][:, sl],
                                PSB[4 + ts][:, 0:512], ALU.subtract))
                        ve.wait_ge(S["ax2"].h, b["ax2"] + hc + 1)
                        self.dwait(ve, "gv", hc, b)
                        S["gn"].inc(ve.tensor_mul(
                            self.ACT_A[:, hc * T:(hc + 1) * T],
                            sqb[hc % 2][:], gb[hc % 2][:]))

    def p_final(self):
        nc, S = self.nc, self.S
        with nc.Block() as blk:
            @blk.sync
            def _(sync):
                for nm in ["ys", "ys2", "st", "st2", "ws", "ws2", "ob", "qs", "bc",
                           "rbs", "xc", "xc2", "bt", "bt2", "gv", "gv2"]:
                    if S[nm].n:
                        sync.wait_ge(S[nm].h, S[nm].n)


def _prep_inputs(inputs):
    cbf, cf32, crow = _host_consts()
    vecs = np.zeros((128, 8 * 32), dtype=np.float32)
    order = [inputs["L0_attn_time_mix_key"], inputs["L0_attn_time_mix_value"],
             inputs["L0_attn_time_mix_receptance"], inputs["L0_attn_time_mix_gate"],
             inputs["L0_attn_gn_weight"], inputs["L0_attn_gn_bias"],
             np.asarray(inputs["L0_attn_time_decay"]).reshape(-1),
             np.asarray(inputs["L0_attn_time_faaaa"]).reshape(-1)]
    for i, v in enumerate(order):
        vecs[:, i * 32:(i + 1) * 32] = np.asarray(v, np.float32).reshape(32, 128).T
    A = np.ascontiguousarray(np.asarray(inputs["L0_ffn_A"], np.float32).reshape(128, D))
    Bm = np.ascontiguousarray(
        np.asarray(inputs["L0_ffn_B"], np.float32).transpose(1, 0, 2).reshape(D, 128))
    bvec = np.asarray(inputs["L0_ffn_b"], np.float32)
    base = {
        "wk": np.asarray(inputs["L0_attn_w_key"], np.float32),
        "wv": np.asarray(inputs["L0_attn_w_value"], np.float32),
        "wr": np.asarray(inputs["L0_attn_w_receptance"], np.float32),
        "wg": np.asarray(inputs["L0_attn_w_gate"], np.float32),
        "wo": np.asarray(inputs["L0_attn_w_output"], np.float32),
        "wf": np.asarray(inputs["L0_ffn_W"], np.float32),
        "amat": A, "bmat": Bm,
        "rwm": np.asarray(inputs["L0_ffn_router_W"], np.float32),
        "rbr": np.asarray(inputs["L0_ffn_router_b"], np.float32).reshape(1, 8),
        "brow0": np.ascontiguousarray(bvec[:T]).reshape(1, T),
        "brow1": np.ascontiguousarray(bvec[T:]).reshape(1, T),
        "vecs": vecs, "cbf": cbf, "cf32": cf32, "crow": crow,
    }
    xall = np.asarray(inputs["L0_attn_hidden"], np.float32)
    maps = []
    for core in range(8):
        m = dict(base)
        m["x0"] = np.ascontiguousarray(xall[core])
        maps.append(m)
    return maps


def kernel(**inputs):
    maps = _prep_inputs(inputs)
    nc = KB().build()
    res = run_bass_kernel_spmd(nc, maps, core_ids=list(range(8)))
    out = np.stack([np.asarray(r["y"], np.float32) for r in res.results], axis=0)
    return out


# version 06
# speedup vs baseline: 1.0512x; 1.0512x over previous
import numpy as np
import ml_dtypes
from contextlib import ExitStack

import concourse.bass as bass
import concourse.mybir as mybir
from concourse.bass_utils import run_bass_kernel_spmd

F32 = mybir.dt.float32
BF16 = mybir.dt.bfloat16
AF = mybir.ActivationFunctionType
ALU = mybir.AluOpType
AX = mybir.AxisListType

P = 128
T = 2048
TS = 512
NTS = 4
NCH = 32
D = 4096
C = 128
NTC = 16
NHC = 32
LNWMIN = -5.0
EPS = 1e-5
QL, QR = 611, 2047
NQ = QR - QL

BF = ml_dtypes.bfloat16


def _host_consts():
    j = np.arange(128)
    cbf = np.zeros((128, 579), dtype=np.float32)
    cbf[:, 0:128] = np.eye(128)
    cbf[:, 128:256] = np.triu(np.ones((128, 128)), 1)
    cbf[:, 256:320] = 1.0
    cbf[:64, 320] = 1.0
    cbf[64:, 321] = 1.0
    cbf[:, 322] = 1.0
    for e in range(8):
        cbf[e, 323 + e * 16:323 + (e + 1) * 16] = 1.0
    cbf[0, 451:515] = 1.0
    cbf[1, 515:579] = 1.0
    cf32 = np.zeros((128, 642), dtype=np.float32)
    cf32[:, 0:128] = np.eye(128)
    cf32[:, 128:256] = j[None, :]
    cf32[:, 256:384] = (j % 64)[None, :]
    cf32[:, 384:512] = (j % 32)[None, :]
    cf32[:, 512:640] = (j % 16)[None, :]
    cf32[:, 640] = 1e-5
    return cbf.astype(BF), cf32, np.ones((1, 512), dtype=BF)


class Sem:
    def __init__(self, es, nc, name):
        self.h = es.enter_context(nc.semaphore(name))
        self.n = 0

    def inc(self, inst, k=1):
        inst.then_inc(self.h, k)
        self.n += k
        return self.n


class KB:
    def __init__(self):
        nc = bass.Bass("TRN2", target_bir_lowering=False)
        self.nc = nc
        self.x0 = nc.dram_tensor("x0", [T, D], F32, kind="ExternalInput").ap()
        self.w = {nm: nc.dram_tensor(nm, [D, D], F32, kind="ExternalInput").ap()
                  for nm in ["wk", "wv", "wr", "wg", "wo", "wf"]}
        self.amat = nc.dram_tensor("amat", [128, D], F32, kind="ExternalInput").ap()
        self.bmat = nc.dram_tensor("bmat", [D, 128], F32, kind="ExternalInput").ap()
        self.rwm = nc.dram_tensor("rwm", [8, D], F32, kind="ExternalInput").ap()
        self.rbr = nc.dram_tensor("rbr", [1, 8], F32, kind="ExternalInput").ap()
        self.brow0 = nc.dram_tensor("brow0", [1, T], F32, kind="ExternalInput").ap()
        self.brow1 = nc.dram_tensor("brow1", [1, T], F32, kind="ExternalInput").ap()
        self.vecs = nc.dram_tensor("vecs", [128, 8 * 32], F32, kind="ExternalInput").ap()
        self.cbf = nc.dram_tensor("cbf", [128, 579], BF16, kind="ExternalInput").ap()
        self.cf32 = nc.dram_tensor("cf32", [128, 642], F32, kind="ExternalInput").ap()
        self.crow = nc.dram_tensor("crow", [1, 512], BF16, kind="ExternalInput").ap()
        self.y = nc.dram_tensor("y", [T, D], F32, kind="ExternalOutput").ap()

        self.xfm = [nc.dram_tensor(f"xfm{i}", [NCH, P, T], BF16, kind="Internal").ap()
                    for i in range(3)]
        self.kvrg = {nm: nc.dram_tensor(nm + "scr", [NCH, P, T], BF16, kind="Internal").ap()
                     for nm in "kvrg"}
        self.obscr = nc.dram_tensor("obscr", [NHC, P, T], BF16, kind="Internal").ap()
        self.wtf = nc.dram_tensor("wtf_scr", [NCH, P, D], BF16, kind="Internal").ap()
        self.atil = nc.dram_tensor("atil_scr", [NCH, P, 128], BF16, kind="Internal").ap()
        self.btil = nc.dram_tensor("btil_scr", [NCH, P, 128], BF16, kind="Internal").ap()
        self.rows_scr = nc.dram_tensor("rows_scr", [4, T], BF16, kind="Internal").ap()

    def snap(self):
        return {k: s.n for k, s in self.S.items()}

    HW_DMA_SEMS = ["qs", "wl", "wl2", "bc", "ob", "kv", "kv2", "wta", "wta2"]
    SW_DMA_SEMS = ["st", "st2", "xc", "xc2", "bt", "bt2", "ws", "ws2", "rbs",
                   "gv", "gv2", "ys", "ys2"]

    def _drain_dmas(self):
        S = self.S
        with self.nc.Block() as blk:
            @blk.sync
            def _(sync):
                for nm in self.HW_DMA_SEMS:
                    if S[nm].n:
                        sync.wait_ge(S[nm].h, S[nm].n)

            @blk.gpsimd
            def _(g_):
                for nm in self.SW_DMA_SEMS:
                    if S[nm].n:
                        g_.wait_ge(S[nm].h, S[nm].n)

    def dinc(self, name, idx, inst):
        self.S[name if idx % 2 == 0 else name + "2"].inc(inst, 16)

    def dwait(self, eng, name, idx, b):
        nm = name if idx % 2 == 0 else name + "2"
        eng.wait_ge(self.S[nm].h, b[nm] + 16 * (idx // 2 + 1))

    def uname(self, base):
        self._uc = getattr(self, "_uc", 0) + 1
        return f"{base}_{self._uc}"

    def build(self, upto=None, debug=()):
        self._upto = upto
        self._debug = tuple(debug)
        nc = self.nc
        if debug:
            self.dbg_out = {}
            scr = {"xfm0": self.xfm[0], "xfm1": self.xfm[1], "xfm2": self.xfm[2],
                   "kscr": self.kvrg["k"], "vscr": self.kvrg["v"],
                   "rscr": self.kvrg["r"], "gscr": self.kvrg["g"],
                   "obscr": self.obscr, "rows": self.rows_scr,
                   "atil": self.atil, "btil": self.btil, "wtf": self.wtf}
            for nm in debug:
                t = scr[nm]
                self.dbg_out[nm] = nc.dram_tensor(
                    "dbg_" + nm, list(t.shape), BF16, kind="ExternalOutput").ap()
        es = ExitStack()
        with es:
            sb = lambda name, shape, dt: es.enter_context(nc.sbuf_tensor(name, shape, dt))
            self.ACT_A = sb("ACT_A", [P, NCH * T], BF16)
            self.RSTD = sb("RSTD", [P, T], BF16)
            self.MURS = sb("MURS", [P, T], BF16)
            self.TSC = sb("TSC", [P, T], BF16)
            self.VECS = sb("VECS", [P, 8 * 32], F32)
            self.CBF = sb("CBF", [P, 579], BF16)
            self.CF = sb("CF", [P, 642], F32)
            self.CROW = sb("CROW", [1, 512], BF16)
            self.RWT = sb("RWT", [P, 256], BF16)
            self.BRB0 = sb("BRB0", [1, T], BF16)
            self.BRB1 = sb("BRB1", [1, T], BF16)
            self.RBR8 = sb("RBR8", [1, 8], F32)
            self.RTBC = sb("RTBC", [P, 1], F32)
            self.QSUM = sb("QSUM", [P, NCH], F32)
            self.QBF = sb("QBF", [P, NCH], BF16)
            self.SROW = sb("SROW", [1, 32], F32)
            self.PBF = sb("PBF", [1, 8], BF16)
            self.PCOL = sb("PCOL", [8, 1], BF16)
            self.RMUs = [sb(f"RMU{i}", [2, TS], F32) for i in range(2)]
            self.RSQs = [sb(f"RSQ{i}", [2, TS], F32) for i in range(2)]
            self.RVAs = [sb(f"RVA{i}", [2, TS], F32) for i in range(2)]
            self.RSBs = [sb(f"RSB{i}", [2, TS], BF16) for i in range(2)]
            self.RMBs = [sb(f"RMB{i}", [2, TS], BF16) for i in range(2)]
            self.RMU, self.RSQ, self.RVA = self.RMUs[0], self.RSQs[0], self.RVAs[0]
            self.RSB, self.RMB = self.RSBs[0], self.RMBs[0]

            self.ident = self.CBF[:, 0:128]
            self.masku = self.CBF[:, 128:256]
            self.ones64 = self.CBF[:, 256:320]
            self.hm2 = self.CBF[:, 320:322]
            self.onescol = self.CBF[:, 322:323]
            self.exp8 = self.CBF[0:8, 323:451]
            self.hm2T = self.CBF[0:2, 451:579]
            self.identf = self.CF[:, 0:128]
            self.iota = {128: self.CF[:, 128:256], 64: self.CF[:, 256:384],
                         32: self.CF[:, 384:512], 16: self.CF[:, 512:640]}
            self.tm = {pn: self.VECS[:, i * 32:(i + 1) * 32]
                       for i, pn in enumerate("kvrg")}
            self.gnw = self.VECS[:, 4 * 32:5 * 32]
            self.gnb = self.VECS[:, 5 * 32:6 * 32]
            self.tdv = self.VECS[:, 6 * 32:7 * 32]
            self.faaaa = self.VECS[:, 7 * 32:8 * 32]

            self.PSB = [es.enter_context(nc.psum_tensor(f"PS{i}", [P, 512], F32))
                        for i in range(8)]
            self.S = {nm: Sem(es, nc, "s_" + nm) for nm in
                      ["qs", "wl", "cv", "tpg", "sst", "wt", "mm", "ev", "xc",
                       "st", "bt", "ws", "xt", "yr", "ys", "pp", "tt", "ae",
                       "am", "om", "uf", "sbm", "kv", "pmb", "cd", "ca", "cp",
                       "cg", "ln", "gn", "aux", "ax2", "bc", "ob",
                       "wl2", "xc2", "st2", "bt2", "ys2", "ws2", "kv2",
                       "wta", "wta2", "rbs", "gv", "gv2", "dp"]}

            seq = [
                ("consts", self.p0_consts),
                ("entry", self.p0_entry),
                ("small", self.p0_small),
                ("lnq1", lambda: self.p1_lnq(self.xfm[0])),
                ("router1", self.p1_router),
                ("tsc1", self.p1_tsc),
                ("ffn1", lambda: self.p2_main("ffn", self.w["wf"],
                                              xres=self.xfm[0], xdst=self.xfm[1],
                                              save_wt=True)),
                ("stats1", lambda: self.p_stats(self.xfm[1])),
                ("bcast1", self._bcast_rows),
            ]
            for pn in "kvrg":
                seq.append(("mix_" + pn,
                            (lambda p: lambda: self.p4_mix(p, self.xfm[1]))(pn)))
                seq.append(("proj_" + pn,
                            (lambda p: lambda: self.p2_main(
                                "proj", self.w["w" + p], dst_scr=self.kvrg[p],
                                silu=(p == "g")))(pn)))
            seq += [
                ("wkv", self.p5a_wkv),
                ("gn", self.p5b_gn),
                ("outproj", lambda: self.p2_main("outproj", self.w["wo"],
                                                 xres=self.xfm[1],
                                                 xdst=self.xfm[2])),
                ("stats2", lambda: self.p_stats(self.xfm[2])),
                ("lnq2", lambda: self.p1_lnq(self.xfm[2])),
                ("router2", self.p1_router),
                ("tsc2", self.p1_tsc),
                ("ffn2", lambda: self.p2_main("ffn2", None, xres=self.xfm[2])),
            ]
            for name, fn in seq:
                fn()
                self._drain_dmas()
                if upto is not None and name == upto:
                    break
            if self._debug:
                self._emit_debug()
            self.p_final()
        return nc

    def _emit_debug(self):
        nc, S = self.nc, self.S
        scr = {"xfm0": self.xfm[0], "xfm1": self.xfm[1], "xfm2": self.xfm[2],
               "kscr": self.kvrg["k"], "vscr": self.kvrg["v"],
               "rscr": self.kvrg["r"], "gscr": self.kvrg["g"],
               "obscr": self.obscr, "rows": self.rows_scr,
               "atil": self.atil, "btil": self.btil, "wtf": self.wtf}
        with nc.Block() as blk:
            @blk.sync
            def _(sync):
                for snm in ["qs", "wl", "wl2", "st", "st2", "xc", "xc2", "bt",
                            "bt2", "ws", "ws2", "ys2", "kv", "kv2", "gv", "gv2",
                            "bc", "ob", "rbs", "wta", "wta2"]:
                    if S[snm].n:
                        sync.wait_ge(S[snm].h, S[snm].n)
                for nm in self._debug:
                    srcap = scr[nm]
                    dstap = self.dbg_out[nm]
                    if nm == "rows":
                        srcap = srcap[0:2, :]
                        dstap = dstap[0:2, :]
                    S["qs"].inc(sync.dma_start(dstap, srcap), 16)
                sync.wait_ge(S["qs"].h, S["qs"].n)

    def p0_consts(self):
        nc, S = self.nc, self.S
        with nc.Block() as blk:
            @blk.sync
            def _(sync):
                for dst, src in [(self.CBF[:], self.cbf), (self.CF[:], self.cf32),
                                 (self.CROW[:], self.crow), (self.VECS[:], self.vecs),
                                 (self.RBR8[:], self.rbr)]:
                    S["qs"].inc(sync.dma_start(dst, src), 16)
                sync.wait_ge(S["qs"].h, S["qs"].n)

    def p0_entry(self):
        nc, S, PSB = self.nc, self.S, self.PSB
        NT = 16
        with ExitStack() as ps:
            xtf = [ps.enter_context(nc.sbuf_tensor(self.uname(f"e_xtf{i}"), [P, 2048], F32))
                   for i in range(2)]
            xtb = ps.enter_context(nc.sbuf_tensor(self.uname("e_xtb"), [P, D], BF16))
            stg = [ps.enter_context(nc.sbuf_tensor(self.uname(f"e_stg{i}"), [P, 512], BF16))
                   for i in range(2)]
            sqg = [ps.enter_context(nc.sbuf_tensor(self.uname(f"e_sqg{i}"), [P, 512], BF16))
                   for i in range(2)]
            b = self.snap()
            with nc.Block() as blk:
                @blk.sync
                def _(sync):
                    for i in range(2 * NT):
                        if i >= 2:
                            sync.wait_ge(S["cv"].h, b["cv"] + i - 1)
                        tt, pc = divmod(i, 2)
                        self.dinc("wl", i, sync.dma_start(
                            xtf[i % 2][:], self.x0[tt * 128:(tt + 1) * 128,
                                                   pc * 2048:(pc + 1) * 2048]))

                @blk.scalar
                def _(act):
                    for tt in range(NT):
                        if tt >= 1:
                            act.wait_ge(S["tpg"].h, b["tpg"] + 8 * tt)
                        for pc in range(2):
                            i = tt * 2 + pc
                            self.dwait(act, "wl", i, b)
                            S["cv"].inc(act.activation(
                                xtb[:, pc * 2048:(pc + 1) * 2048],
                                xtf[i % 2][:], AF.Copy))
                        for g in range(8):
                            G = tt * 8 + g
                            act.wait_ge(S["wt"].h, b["wt"] + G + 1)
                            if G >= 2:
                                act.wait_ge(S["sst"].h, b["sst"] + G - 1)
                            S["aux"].inc(act.activation(
                                sqg[G % 2][:], stg[G % 2][:], AF.Square))
                        if tt % 4 == 3:
                            ts = tt // 4
                            act.wait_ge(S["sst"].h, b["sst"] + 8 * (tt + 1))
                            if ts >= 1:
                                act.wait_ge(S["cg"].h, b["cg"] + ts)
                                act.wait_ge(S["rbs"].h, b["rbs"] + 16 * 2 * ts)
                            S["ca"].inc(act.activation(self.RMU[0:1, :],
                                                       PSB[2][0:1, :], AF.Copy,
                                                       scale=1.0 / D))
                            act.wait_ge(S["ca"].h, S["ca"].n)
                            S["ca"].inc(act.activation(self.RSQ[0:1, :],
                                                       self.RMU[0:1, :], AF.Square))
                            act.wait_ge(S["cd"].h, b["cd"] + 1 * (ts + 1))
                            S["ca"].inc(act.activation(self.RVA[0:1, :],
                                                       self.RVA[0:1, :], AF.Ln,
                                                       bias=self.CF[0:1, 640:641]))
                            act.wait_ge(S["ca"].h, S["ca"].n)
                            S["ca"].inc(act.activation(self.RSB[0:1, :],
                                                       self.RVA[0:1, :], AF.Exp,
                                                       scale=-0.5))

                @blk.tensor
                def _(te):
                    for tt in range(NT):
                        te.wait_ge(S["cv"].h, b["cv"] + 2 * tt + 2)
                        if tt % 4 == 0 and tt > 0:
                            te.wait_ge(S["ca"].h, b["ca"] + 4 * (tt // 4))
                            te.wait_ge(S["cd"].h, b["cd"] + (tt // 4))
                        for g in range(8):
                            G = tt * 8 + g
                            if G >= 2:
                                te.wait_ge(S["wt"].h, b["wt"] + G - 1)
                            for m in range(4):
                                jj = g * 4 + m
                                inst = te.matmul(
                                    PSB[G % 2][:, m * 128:(m + 1) * 128],
                                    xtb[:, jj * 128:(jj + 1) * 128], self.ident,
                                    start=True, stop=True)
                            S["tpg"].inc(inst)
                            if g >= 1:
                                self._entry_stats(te, tt, g - 1, b, stg, sqg)
                        self._entry_stats(te, tt, 7, b, stg, sqg)

                @blk.vector
                def _(ve):
                    for tt in range(NT):
                        for g in range(8):
                            G = tt * 8 + g
                            ve.wait_ge(S["tpg"].h, b["tpg"] + G + 1)
                            if G >= 2:
                                ve.wait_ge(S["aux"].h, b["aux"] + G - 1)
                                self.dwait(ve, "st", G - 2, b)
                            S["wt"].inc(ve.tensor_copy(stg[G % 2][:], PSB[G % 2][:]))
                        if tt % 4 == 3:
                            ts = tt // 4
                            ve.wait_ge(S["ca"].h, b["ca"] + 4 * ts + 2)
                            S["cd"].inc(ve.scalar_tensor_tensor(
                                self.RVA[0:1, :], PSB[3][0:1, :], 1.0 / D,
                                self.RSQ[0:1, :], ALU.mult, ALU.subtract))
                            ve.wait_ge(S["ca"].h, b["ca"] + 4 * ts + 4)
                            S["cg"].inc(ve.tensor_mul(self.RMB[0:1, :],
                                                      self.RMU[0:1, :],
                                                      self.RSB[0:1, :]))

                @blk.gpsimd
                def _(g_):
                    for G in range(8 * NT):
                        g_.wait_ge(S["wt"].h, b["wt"] + G + 1)
                        tt, g = divmod(G, 8)
                        dst = self.xfm[0][4 * g:4 * g + 4, :,
                                          tt * 128:(tt + 1) * 128].transpose([1, 0, 2])
                        self.dinc("st", G, g_.dma_start(dst, stg[G % 2][:]))
                        if tt % 4 == 3 and g == 7:
                            ts = tt // 4
                            g_.wait_ge(S["cg"].h, b["cg"] + ts + 1)
                            S["rbs"].inc(g_.dma_start(
                                self.rows_scr[0:1, ts * TS:(ts + 1) * TS],
                                self.RSB[0:1, :]), 16)
                            S["rbs"].inc(g_.dma_start(
                                self.rows_scr[1:2, ts * TS:(ts + 1) * TS],
                                self.RMB[0:1, :]), 16)
                    g_.wait_ge(S["st"].h, S["st"].n)
                    g_.wait_ge(S["st2"].h, S["st2"].n)

    def _entry_stats(self, te, tt, g, b, stg, sqg):
        S, PSB = self.S, self.PSB
        G = tt * 8 + g
        te.wait_ge(S["wt"].h, b["wt"] + G + 1)
        te.wait_ge(S["aux"].h, b["aux"] + G + 1)
        col = (tt % 4) * 128
        first = (g == 0)
        last = (g == 7)
        for m in range(4):
            te.matmul(PSB[2][0:1, col:col + 128], self.onescol,
                      stg[G % 2][:, m * 128:(m + 1) * 128],
                      start=(first and m == 0 and tt % 4 == 0),
                      stop=(last and m == 3 and tt % 4 == 3))
        for m in range(4):
            inst = te.matmul(PSB[3][0:1, col:col + 128], self.onescol,
                             sqg[G % 2][:, m * 128:(m + 1) * 128],
                             start=(first and m == 0 and tt % 4 == 0),
                             stop=(last and m == 3 and tt % 4 == 3))
        S["sst"].inc(inst)

    def p0_small(self):
        nc, S, PSB = self.nc, self.S, self.PSB
        with ExitStack() as ps:
            stf = ps.enter_context(nc.sbuf_tensor(self.uname("s_stf"), [P, 1024], F32))
            stb = ps.enter_context(nc.sbuf_tensor(self.uname("s_stb"), [P, 1024], BF16))
            otb = ps.enter_context(nc.sbuf_tensor(self.uname("s_otb"), [P, 512], BF16))
            brf = ps.enter_context(nc.sbuf_tensor(self.uname("s_brf"), [1, T], F32))
            for strip in range(4):
                self._small_strip(self.amat[:, strip * 1024:(strip + 1) * 1024],
                                  stf, stb, None,
                                  self.atil[strip * 8:(strip + 1) * 8, :, :], 8)
            for strip in range(8):
                src = self.bmat[strip * 512:(strip + 1) * 512, :].rearrange(
                    "(a p) e -> p a e", p=128)
                self._small_strip(src, stf, stb, otb,
                                  self.btil[strip * 4:(strip + 1) * 4, :, :], 4)
            for s4 in range(4):
                with nc.Block() as blk:
                    @blk.sync
                    def _(sync):
                        S["qs"].inc(sync.dma_start(
                            stf[0:8, :], self.rwm[:, s4 * 1024:(s4 + 1) * 1024]), 16)
                        sync.wait_ge(S["qs"].h, S["qs"].n)
                with nc.Block() as blk:
                    @blk.scalar
                    def _(act):
                        act.activation(stb[0:8, :], stf[0:8, :], AF.Copy)
                with nc.Block() as blk:
                    @blk.tensor
                    def _(te):
                        for m in range(8):
                            te.matmul(PSB[0][:, m * 64:m * 64 + 8],
                                      stb[0:8, m * 128:(m + 1) * 128],
                                      self.ident[0:8, 0:8], start=True, stop=True)
                with nc.Block() as blk:
                    @blk.vector
                    def _(ve):
                        for m in range(8):
                            kc = s4 * 8 + m
                            ve.tensor_copy(self.RWT[:, kc * 8:(kc + 1) * 8],
                                           PSB[0][:, m * 64:m * 64 + 8])
            with nc.Block() as blk:
                @blk.sync
                def _(sync):
                    S["qs"].inc(sync.dma_start(brf[:], self.brow0), 16)
                    sync.wait_ge(S["qs"].h, S["qs"].n)
            with nc.Block() as blk:
                @blk.scalar
                def _(act):
                    act.activation(self.BRB0[:], brf[:], AF.Copy)
            with nc.Block() as blk:
                @blk.sync
                def _(sync):
                    S["qs"].inc(sync.dma_start(brf[:], self.brow1), 16)
                    sync.wait_ge(S["qs"].h, S["qs"].n)
            with nc.Block() as blk:
                @blk.scalar
                def _(act):
                    act.activation(self.BRB1[:], brf[:], AF.Copy)

    def _small_strip(self, src, stf, stb, otb, dst_scr, ntile):
        nc, S, PSB = self.nc, self.S, self.PSB
        w = ntile * 128
        out = stb if otb is None else otb
        with nc.Block() as blk:
            @blk.sync
            def _(sync):
                S["qs"].inc(sync.dma_start(stf[:, 0:w], src), 16)
                sync.wait_ge(S["qs"].h, S["qs"].n)
        with nc.Block() as blk:
            @blk.scalar
            def _(act):
                act.activation(stb[:, 0:w], stf[:, 0:w], AF.Copy)
        with nc.Block() as blk:
            @blk.tensor
            def _(te):
                for m in range(ntile):
                    te.matmul(PSB[m // 4 % 2][:, (m % 4) * 128:(m % 4 + 1) * 128],
                              stb[:, m * 128:(m + 1) * 128], self.ident,
                              start=True, stop=True)
        with nc.Block() as blk:
            @blk.vector
            def _(ve):
                for q in range((ntile + 3) // 4):
                    n = min(4, ntile - q * 4) * 128
                    ve.tensor_copy(out[:, q * 512:q * 512 + n], PSB[q % 2][:, 0:n])
        with nc.Block() as blk:
            @blk.sync
            def _(sync):
                S["qs"].inc(sync.dma_start(
                    dst_scr.transpose([1, 0, 2]),
                    out[:, 0:w].rearrange("p (a c) -> p a c", c=128)), 16)
                sync.wait_ge(S["qs"].h, S["qs"].n)

    def _bcast_rows(self):
        nc, S = self.nc, self.S
        with nc.Block() as blk:
            @blk.sync
            def _(sync):
                if S["rbs"].n:
                    sync.wait_ge(S["rbs"].h, S["rbs"].n)
                if S["qs"].n:
                    sync.wait_ge(S["qs"].h, S["qs"].n)
                S["bc"].inc(sync.dma_start(
                    self.RSTD[:], self.rows_scr[0:1, :].broadcast_to([P, T])), 16)
                S["bc"].inc(sync.dma_start(
                    self.MURS[:], self.rows_scr[1:2, :].broadcast_to([P, T])), 16)
                sync.wait_ge(S["bc"].h, S["bc"].n)

    def p1_lnq(self, xsrc):
        nc, S = self.nc, self.S
        self._bcast_rows()
        with ExitStack() as ps:
            xst = [ps.enter_context(nc.sbuf_tensor(self.uname(f"l_x{i}"), [P, T], BF16))
                   for i in range(2)]
            b = self.snap()
            with nc.Block() as blk:
                @blk.gpsimd
                def _(g_):
                    for ch in range(NCH):
                        if ch >= 2:
                            g_.wait_ge(S["cd"].h, b["cd"] + 3 * (ch - 1))
                        self.dinc("xc", ch, g_.dma_start(xst[ch % 2][:],
                                                         xsrc[ch, :, :]))

                @blk.vector
                def _(ve):
                    for ch in range(NCH):
                        a = self.ACT_A[:, ch * T:(ch + 1) * T]
                        self.dwait(ve, "xc", ch, b)
                        S["cd"].inc(ve.tensor_mul(a, xst[ch % 2][:], self.RSTD[:]))
                        ve.wait_ge(S["cd"].h, S["cd"].n)
                        S["cd"].inc(ve.tensor_sub(a, a, self.MURS[:]))
                        ve.wait_ge(S["cd"].h, S["cd"].n)
                        S["cd"].inc(ve.reduce_sum(
                            self.QSUM[:, ch:ch + 1],
                            self.ACT_A[:, ch * T + QL:ch * T + QR], axis=AX.X))

    def p1_router(self):
        nc, S, PSB = self.nc, self.S, self.PSB
        with nc.Block() as blk:
            @blk.scalar
            def _(act):
                act.activation(self.QBF[:], self.QSUM[:], AF.Copy)
        with nc.Block() as blk:
            @blk.tensor
            def _(te):
                for kc in range(NCH):
                    te.matmul(PSB[0][0:1, 0:8], self.QBF[:, kc:kc + 1],
                              self.RWT[:, kc * 8:(kc + 1) * 8],
                              start=(kc == 0), stop=(kc == NCH - 1))
        with nc.Block() as blk:
            @blk.vector
            def _(ve):
                S["cd"].inc(ve.scalar_tensor_tensor(
                    self.SROW[0:1, 0:8], PSB[0][0:1, 0:8], 1.0 / NQ,
                    self.RBR8[:], ALU.mult, ALU.add))
                ve.wait_ge(S["cd"].h, S["cd"].n)
                S["cd"].inc(ve.reduce_max(self.SROW[0:1, 8:9],
                                          self.SROW[0:1, 0:8], axis=AX.X))
                ve.wait_ge(S["cd"].h, S["cd"].n)
                S["cd"].inc(ve.tensor_scalar_mul(self.SROW[0:1, 9:10],
                                                 self.SROW[0:1, 8:9], -1.0))
        with nc.Block() as blk:
            @blk.scalar
            def _(act):
                act.activation(self.SROW[0:1, 16:24], self.SROW[0:1, 0:8],
                               AF.Exp, bias=self.SROW[0:1, 9:10])
        with nc.Block() as blk:
            @blk.vector
            def _(ve):
                S["cd"].inc(ve.reduce_sum(self.SROW[0:1, 10:11],
                                          self.SROW[0:1, 16:24], axis=AX.X))
                ve.wait_ge(S["cd"].h, S["cd"].n)
                S["cd"].inc(ve.reciprocal(self.SROW[0:1, 11:12],
                                          self.SROW[0:1, 10:11]))
                ve.wait_ge(S["cd"].h, S["cd"].n)
                S["cd"].inc(ve.tensor_scalar(self.SROW[0:1, 24:32],
                                             self.SROW[0:1, 16:24],
                                             self.SROW[0:1, 11:12], 2.0,
                                             ALU.mult, ALU.mult))
        with nc.Block() as blk:
            @blk.scalar
            def _(act):
                act.activation(self.PBF[:], self.SROW[0:1, 24:32], AF.Copy)
        with nc.Block() as blk:
            @blk.tensor
            def _(te):
                te.matmul(PSB[1][0:8, 0:1], self.PBF[:], self.CROW[0:1, 0:1],
                          start=True, stop=True)
        with nc.Block() as blk:
            @blk.vector
            def _(ve):
                ve.tensor_copy(self.PCOL[:], PSB[1][0:8, 0:1])
        with nc.Block() as blk:
            @blk.tensor
            def _(te):
                te.matmul(PSB[2][:, 0:1], self.exp8, self.PCOL[:],
                          start=True, stop=True)
        with nc.Block() as blk:
            @blk.vector
            def _(ve):
                ve.tensor_copy(self.RTBC[:], PSB[2][:, 0:1])

    def p1_tsc(self):
        nc, S, PSB = self.nc, self.S, self.PSB
        with ExitStack() as ps:
            atl = ps.enter_context(nc.sbuf_tensor(self.uname("t_atl"), [P, NCH * 128], BF16))
            b = self.snap()
            with nc.Block() as blk:
                @blk.sync
                def _(sync):
                    S["qs"].inc(sync.dma_start(
                        atl[:], self.atil[:, :, :].transpose([1, 0, 2])), 16)
                    sync.wait_ge(S["qs"].h, S["qs"].n)
            with nc.Block() as blk:
                @blk.tensor
                def _(te):
                    for ts in range(NTS):
                        if ts >= 2:
                            te.wait_ge(S["ev"].h, b["ev"] + ts - 1)
                        for kc in range(NCH):
                            inst = te.matmul(
                                PSB[3 + ts % 2][:],
                                atl[:, kc * 128:(kc + 1) * 128],
                                self.ACT_A[:, kc * T + ts * TS:kc * T + (ts + 1) * TS],
                                start=(kc == 0), stop=(kc == NCH - 1))
                        S["mm"].inc(inst)

                @blk.vector
                def _(ve):
                    for ts in range(NTS):
                        ve.wait_ge(S["mm"].h, b["mm"] + ts + 1)
                        S["ev"].inc(ve.tensor_scalar_mul(
                            self.TSC[:, ts * TS:(ts + 1) * TS],
                            PSB[3 + ts % 2][:], self.RTBC[:, 0:1]))

    def p_stats(self, xsrc):
        nc, S, PSB = self.nc, self.S, self.PSB
        with ExitStack() as ps:
            xst = [ps.enter_context(nc.sbuf_tensor(self.uname(f"st_x{i}"), [P, T], BF16))
                   for i in range(2)]
            sqs = [ps.enter_context(nc.sbuf_tensor(self.uname(f"st_q{i}"), [P, T], BF16))
                   for i in range(2)]
            b = self.snap()
            with nc.Block() as blk:
                @blk.gpsimd
                def _(g_):
                    for ch in range(NCH):
                        if ch >= 2:
                            g_.wait_ge(S["sst"].h, b["sst"] + ch - 1)
                        self.dinc("xc", ch, g_.dma_start(xst[ch % 2][:], xsrc[ch, :, :]))

                @blk.scalar
                def _(act):
                    for ch in range(NCH):
                        self.dwait(act, "xc", ch, b)
                        if ch >= 2:
                            act.wait_ge(S["sst"].h, b["sst"] + ch - 1)
                        S["aux"].inc(act.activation(sqs[ch % 2][:], xst[ch % 2][:],
                                                    AF.Square))
                    for ts in range(NTS):
                        act.wait_ge(S["sst"].h, b["sst"] + NCH)
                        if ts >= 1:
                            act.wait_ge(S["cd"].h, b["cd"] + 2 * ts)
                            act.wait_ge(S["qs"].h, b["qs"] + 16 * 2 * ts)
                        S["ca"].inc(act.activation(self.RMU[0:1, :],
                                                   PSB[ts][0:1, :], AF.Copy,
                                                   scale=1.0 / D))
                        act.wait_ge(S["ca"].h, S["ca"].n)
                        S["ca"].inc(act.activation(self.RSQ[0:1, :],
                                                   self.RMU[0:1, :], AF.Square))
                        act.wait_ge(S["cd"].h, b["cd"] + 2 * ts + 1)
                        S["ca"].inc(act.activation(self.RVA[0:1, :],
                                                   self.RVA[0:1, :], AF.Ln,
                                                   bias=self.CF[0:1, 640:641]))
                        act.wait_ge(S["ca"].h, S["ca"].n)
                        S["ca"].inc(act.activation(self.RSB[0:1, :],
                                                   self.RVA[0:1, :], AF.Exp,
                                                   scale=-0.5))

                @blk.tensor
                def _(te):
                    for ch in range(NCH):
                        self.dwait(te, "xc", ch, b)
                        te.wait_ge(S["aux"].h, b["aux"] + ch + 1)
                        for ts in range(NTS):
                            te.matmul(PSB[ts][0:1, :], self.onescol,
                                      xst[ch % 2][:, ts * TS:(ts + 1) * TS],
                                      start=(ch == 0), stop=(ch == NCH - 1))
                        for ts in range(NTS):
                            inst = te.matmul(PSB[4 + ts][0:1, :], self.onescol,
                                             sqs[ch % 2][:, ts * TS:(ts + 1) * TS],
                                             start=(ch == 0), stop=(ch == NCH - 1))
                        S["sst"].inc(inst)

                @blk.vector
                def _(ve):
                    for ts in range(NTS):
                        ve.wait_ge(S["ca"].h, b["ca"] + 4 * ts + 2)
                        S["cd"].inc(ve.scalar_tensor_tensor(
                            self.RVA[0:1, :], PSB[4 + ts][0:1, :], 1.0 / D,
                            self.RSQ[0:1, :], ALU.mult, ALU.subtract))
                        ve.wait_ge(S["ca"].h, b["ca"] + 4 * ts + 4)
                        S["cd"].inc(ve.tensor_mul(self.RMB[0:1, :],
                                                  self.RMU[0:1, :],
                                                  self.RSB[0:1, :]))

                @blk.sync
                def _(sync):
                    for ts in range(NTS):
                        sync.wait_ge(S["cd"].h, b["cd"] + 2 * ts + 2)
                        S["qs"].inc(sync.dma_start(
                            self.rows_scr[0:1, ts * TS:(ts + 1) * TS],
                            self.RSB[0:1, :]), 16)
                        S["qs"].inc(sync.dma_start(
                            self.rows_scr[1:2, ts * TS:(ts + 1) * TS],
                            self.RMB[0:1, :]), 16)
                    sync.wait_ge(S["qs"].h, S["qs"].n)

    def p4_mix(self, pn, xsrc):
        nc, S = self.nc, self.S
        with ExitStack() as ps:
            xst = [ps.enter_context(nc.sbuf_tensor(self.uname(f"m_x{i}"), [P, T], BF16))
                   for i in range(2)]
            lnst = [ps.enter_context(nc.sbuf_tensor(self.uname(f"m_l{i}"), [P, T], BF16))
                    for i in range(2)]
            dst_ = [ps.enter_context(nc.sbuf_tensor(self.uname(f"m_d{i}"), [P, T], BF16))
                    for i in range(2)]
            b = self.snap()
            tmc = self.tm[pn]
            with nc.Block() as blk:
                @blk.gpsimd
                def _(g_):
                    for ch in range(NCH):
                        if ch >= 2:
                            g_.wait_ge(S["cp"].h, b["cp"] + ch - 1)
                        self.dinc("xc", ch, g_.dma_start(xst[ch % 2][:],
                                                         xsrc[ch, :, :]))
                        self.dwait(g_, "xc", ch, b)
                        if ch >= 2:
                            g_.wait_ge(S["cd"].h, b["cd"] + 3 * (ch - 1))
                        S["cp"].inc(g_.tensor_mul(lnst[ch % 2][:], xst[ch % 2][:],
                                                  self.RSTD[:]))

                @blk.vector
                def _(ve):
                    for ch in range(NCH):
                        ln = lnst[ch % 2]
                        dd = dst_[ch % 2]
                        a = self.ACT_A[:, ch * T:(ch + 1) * T]
                        ve.wait_ge(S["cp"].h, b["cp"] + ch + 1)
                        S["cd"].inc(ve.tensor_sub(ln[:], ln[:], self.MURS[:]))
                        ve.wait_ge(S["cd"].h, S["cd"].n)
                        S["cd"].inc(ve.tensor_sub(dd[:, 1:T], ln[:, 1:T],
                                                  ln[:, 0:T - 1]))
                        ve.wait_ge(S["cd"].h, S["cd"].n)
                        ve.scalar_tensor_tensor(a[:, 1:T], dd[:, 1:T],
                                                tmc[:, ch:ch + 1], ln[:, 0:T - 1],
                                                ALU.mult, ALU.add)
                        S["cd"].inc(ve.tensor_scalar_mul(a[:, 0:1], ln[:, 0:1],
                                                         tmc[:, ch:ch + 1]))

    def p2_main(self, mode, Wsrc, xres=None, xdst=None, dst_scr=None,
                silu=False, save_wt=False):
        nc, S, PSB = self.nc, self.S, self.PSB
        lora = mode in ("ffn", "ffn2")
        from_scr = (mode == "ffn2")
        with ExitStack() as ps:
            wT = [ps.enter_context(nc.sbuf_tensor(self.uname(f"w_t{i}"), [P, D], BF16))
                  for i in range(2)]
            if not from_scr:
                wfp = [ps.enter_context(nc.sbuf_tensor(self.uname(f"w_f{i}"), [P, 1024], F32))
                       for i in range(2)]
                wbf = ps.enter_context(nc.sbuf_tensor(self.uname("w_b"), [P, D], BF16))
            if lora:
                btl = [ps.enter_context(nc.sbuf_tensor(self.uname(f"w_bt{i}"), [P, 128], BF16))
                       for i in range(2)]
            if mode != "proj":
                xcb = [ps.enter_context(nc.sbuf_tensor(self.uname(f"w_xc{i}"), [P, TS], BF16))
                       for i in range(2)]
            if mode == "ffn2":
                fin = [ps.enter_context(nc.sbuf_tensor(self.uname(f"w_fi{i}"), [P, TS], F32))
                       for i in range(2)]
                yro = [ps.enter_context(nc.sbuf_tensor(self.uname(f"w_yr{i}"), [P, TS], F32))
                       for i in range(2)]
            else:
                ob = [ps.enter_context(nc.sbuf_tensor(self.uname(f"w_ob{i}"), [P, TS], BF16))
                      for i in range(2)]
            b = self.snap()

            with nc.Block() as blk:
                @blk.sync
                def _(sync):
                    if from_scr:
                        for och in range(NCH):
                            if och >= 2:
                                sync.wait_ge(S["mm"].h, b["mm"] + 4 * (och - 1))
                            self.dinc("wta", och, sync.dma_start(
                                wT[och % 2][:], self.wtf[och, :, :]))
                    else:
                        for och in range(NCH):
                            for pc in range(4):
                                i = och * 4 + pc
                                if i >= 2:
                                    sync.wait_ge(S["cv"].h, b["cv"] + i - 1)
                                self.dinc("wl", i, sync.dma_start(
                                    wfp[i % 2][:],
                                    Wsrc[och * 128:(och + 1) * 128,
                                         pc * 1024:(pc + 1) * 1024]))

                if not from_scr:
                    @blk.scalar
                    def _(act):
                        for och in range(NCH):
                            if och >= 1:
                                act.wait_ge(S["tpg"].h, b["tpg"] + 8 * och)
                            for pc in range(4):
                                i = och * 4 + pc
                                self.dwait(act, "wl", i, b)
                                S["cv"].inc(act.activation(
                                    wbf[:, pc * 1024:(pc + 1) * 1024],
                                    wfp[i % 2][:], AF.Copy))
                            if silu and och >= 1:
                                self._silu_evac(act, och - 1, b, ob)
                        if silu:
                            self._silu_evac(act, NCH - 1, b, ob)

                @blk.tensor
                def _(te):
                    for och in range(NCH):
                        if not from_scr:
                            te.wait_ge(S["cv"].h, b["cv"] + 4 * och + 4)
                            for g in range(8):
                                G = och * 8 + g
                                if G >= 2:
                                    te.wait_ge(S["wt"].h, b["wt"] + G - 1)
                                for m in range(4):
                                    jj = g * 4 + m
                                    inst = te.matmul(
                                        PSB[2 + G % 2][:, m * 128:(m + 1) * 128],
                                        wbf[:, jj * 128:(jj + 1) * 128], self.ident,
                                        start=True, stop=True)
                                S["tpg"].inc(inst)
                            te.wait_ge(S["wt"].h, b["wt"] + 8 * (och + 1))
                        else:
                            self.dwait(te, "wta", och, b)
                        if lora:
                            self.dwait(te, "bt", och, b)
                        for ts in range(NTS):
                            I = och * 4 + ts
                            if I >= 2:
                                te.wait_ge(S["ev"].h, b["ev"] + I - 1)
                            for kc in range(NCH):
                                inst = te.matmul(
                                    PSB[I % 2][:],
                                    wT[och % 2][:, kc * 128:(kc + 1) * 128],
                                    self.ACT_A[:, kc * T + ts * TS:kc * T + (ts + 1) * TS],
                                    start=(kc == 0),
                                    stop=(kc == NCH - 1 and not lora))
                            if lora:
                                te.matmul(PSB[I % 2][:], btl[och % 2][:],
                                          self.TSC[:, ts * TS:(ts + 1) * TS],
                                          start=False, stop=False)
                                brow_src = self.BRB0 if och < 16 else self.BRB1
                                cc = (och % 16) * 128
                                inst = te.matmul(PSB[I % 2][:],
                                                 brow_src[0:1, cc:cc + 128],
                                                 self.CROW[0:1, :],
                                                 start=False, stop=True)
                            S["mm"].inc(inst)
                            if mode == "ffn2" and I >= 1:
                                self._exit_tp(te, I - 1, b, fin)
                    if mode == "ffn2":
                        self._exit_tp(te, 4 * NCH - 1, b, fin)

                @blk.vector
                def _(ve):
                    for och in range(NCH):
                        if not from_scr:
                            for g in range(8):
                                G = och * 8 + g
                                ve.wait_ge(S["tpg"].h, b["tpg"] + G + 1)
                                if och >= 2 and g == 0:
                                    ve.wait_ge(S["mm"].h, b["mm"] + 4 * (och - 1))
                                    if save_wt:
                                        self.dwait(ve, "ws", och - 2, b)
                                S["wt"].inc(ve.tensor_copy(
                                    wT[och % 2][:, g * 512:(g + 1) * 512],
                                    PSB[2 + G % 2][:]))
                        if silu:
                            continue
                        for ts in range(NTS):
                            I = och * 4 + ts
                            ve.wait_ge(S["mm"].h, b["mm"] + I + 1)
                            if mode != "proj":
                                self.dwait(ve, "xc", I, b)
                            if I >= 2 and mode != "ffn2":
                                self.dwait(ve, "st", I - 2, b)
                            if mode == "ffn":
                                S["ev"].inc(ve.scalar_tensor_tensor(
                                    ob[I % 2][:], PSB[I % 2][:], 0.5, xcb[I % 2][:],
                                    ALU.mult, ALU.add))
                            elif mode == "outproj":
                                S["ev"].inc(ve.tensor_add(
                                    ob[I % 2][:], PSB[I % 2][:], xcb[I % 2][:]))
                            elif mode == "proj":
                                S["ev"].inc(ve.tensor_copy(ob[I % 2][:],
                                                           PSB[I % 2][:]))
                            else:
                                if I >= 2:
                                    ve.wait_ge(S["xt"].h, b["xt"] + I - 1)
                                S["ev"].inc(ve.scalar_tensor_tensor(
                                    fin[I % 2][:], PSB[I % 2][:], 0.5, xcb[I % 2][:],
                                    ALU.mult, ALU.add))
                        if mode == "ffn2":
                            for ts in range(NTS):
                                I = och * 4 + ts
                                ve.wait_ge(S["xt"].h, b["xt"] + I + 1)
                                if I >= 2:
                                    self.dwait(ve, "ys", I - 2, b)
                                S["yr"].inc(ve.tensor_copy(yro[I % 2][:],
                                                           PSB[4 + I % 2][:]))

                @blk.gpsimd
                def _(g_):
                    def ystore(I2):
                        och2, ts2 = divmod(I2, 4)
                        g_.wait_ge(S["yr"].h, b["yr"] + I2 + 1)
                        dsty = self.y[ts2 * TS:(ts2 + 1) * TS,
                                      och2 * 128:(och2 + 1) * 128].rearrange(
                                          "(jj p) o -> p jj o", p=128)
                        self.dinc("ys", I2, g_.dma_start(
                            dsty, yro[I2 % 2][:].rearrange(
                                "p (jj o) -> p jj o", o=128)))
                    for och in range(NCH):
                        if lora:
                            if och >= 2:
                                g_.wait_ge(S["mm"].h, b["mm"] + 4 * (och - 1))
                            self.dinc("bt", och, g_.dma_start(btl[och % 2][:],
                                                              self.btil[och, :, :]))
                        for ts in range(NTS):
                            I = och * 4 + ts
                            if mode != "proj":
                                if I >= 2:
                                    g_.wait_ge(S["ev"].h, b["ev"] + I - 1)
                                self.dinc("xc", I, g_.dma_start(
                                    xcb[I % 2][:],
                                    xres[och, :, ts * TS:(ts + 1) * TS]))
                            if mode in ("ffn", "outproj"):
                                g_.wait_ge(S["ev"].h, b["ev"] + I + 1)
                                self.dinc("st", I, g_.dma_start(
                                    xdst[och, :, ts * TS:(ts + 1) * TS],
                                    ob[I % 2][:]))
                            elif mode == "proj":
                                g_.wait_ge(S["ev"].h, b["ev"] + I + 1)
                                self.dinc("st", I, g_.dma_start(
                                    dst_scr[och, :, ts * TS:(ts + 1) * TS],
                                    ob[I % 2][:]))
                            else:
                                if och >= 1:
                                    ystore((och - 1) * 4 + ts)
                        if save_wt:
                            g_.wait_ge(S["wt"].h, b["wt"] + 8 * (och + 1))
                            self.dinc("ws", och, g_.dma_start(self.wtf[och, :, :],
                                                              wT[och % 2][:]))
                    if mode == "ffn2":
                        for ts in range(NTS):
                            ystore((NCH - 1) * 4 + ts)

    def _silu_evac(self, act, och, b, ob):
        S, PSB = self.S, self.PSB
        for ts in range(NTS):
            I = och * 4 + ts
            act.wait_ge(S["mm"].h, b["mm"] + I + 1)
            if I >= 2:
                self.dwait(act, "st", I - 2, b)
            S["ev"].inc(act.activation(ob[I % 2][:], PSB[I % 2][:], AF.Silu))

    def _exit_tp(self, te, I, b, fin):
        S, PSB = self.S, self.PSB
        te.wait_ge(S["ev"].h, b["ev"] + I + 1)
        if I >= 2:
            te.wait_ge(S["yr"].h, b["yr"] + I - 1)
        for m in range(4):
            inst = te.matmul(PSB[4 + I % 2][:, m * 128:(m + 1) * 128],
                             fin[I % 2][:, m * 128:(m + 1) * 128], self.identf,
                             start=True, stop=True)
        S["xt"].inc(inst)

    def p5a_wkv(self):
        nc, S, PSB = self.nc, self.S, self.PSB
        with ExitStack() as ps:
            kvb = {nm: [self.ACT_A[:, (2 * j + i) * T:(2 * j + i + 1) * T]
                        for i in range(2)]
                   for j, nm in enumerate("kvr")}
            obb = ps.enter_context(nc.sbuf_tensor(self.uname("k_ob"), [P, T], BF16))
            PM = [ps.enter_context(nc.sbuf_tensor(self.uname(f"k_pm{i}"), [P, 10 * 128], BF16))
                  for i in range(2)]
            SCL = [ps.enter_context(nc.sbuf_tensor(self.uname(f"k_sc{i}"), [P, 11 * 128], BF16))
                   for i in range(4)]
            UM = [ps.enter_context(nc.sbuf_tensor(self.uname(f"k_um{i}"), [P, 64], BF16))
                  for i in range(2)]
            vTT = [ps.enter_context(nc.sbuf_tensor(self.uname(f"k_vt{i}"), [P, 256], BF16))
                   for i in range(2)]
            ASB = [ps.enter_context(nc.sbuf_tensor(self.uname(f"k_as{i}"), [P, 256], BF16))
                   for i in range(2)]
            TMPU = [ps.enter_context(nc.sbuf_tensor(self.uname(f"k_tu{i}"), [P, 128], BF16))
                    for i in range(2)]
            S0 = ps.enter_context(nc.sbuf_tensor(self.uname("k_s0"), [P, 64], F32))
            S0b = ps.enter_context(nc.sbuf_tensor(self.uname("k_s0b"), [P, 64], BF16))
            LNW = [ps.enter_context(nc.sbuf_tensor(self.uname(f"k_lnw{i}"), [P, 10], F32))
                   for i in range(2)]
            b = self.snap()

            def X(hc, c):
                return hc * NTC + c

            def scl_of(Xc):
                return SCL[Xc % 4]

            with nc.Block() as blk:
                @blk.sync
                def _(sync):
                    for hc in range(NHC):
                        if hc >= 2:
                            sync.wait_ge(S["uf"].h, b["uf"] + (hc - 1) * NTC)
                        for nm in "kvr":
                            self.dinc("kv", hc, sync.dma_start(
                                kvb[nm][hc % 2][:], self.kvrg[nm][hc, :, :]))
                        if hc >= 1:
                            sync.wait_ge(S["uf"].h, b["uf"] + hc * NTC)
                            S["ob"].inc(sync.dma_start(
                                self.obscr[hc - 1, :, :], obb[:]), 16)
                    sync.wait_ge(S["uf"].h, b["uf"] + NHC * NTC)
                    S["ob"].inc(sync.dma_start(self.obscr[NHC - 1, :, :], obb[:]), 16)
                    sync.wait_ge(S["ob"].h, S["ob"].n)

                @blk.scalar
                def _(act):
                    def pm_build(h):
                        L = LNW[h % 2]
                        S["ax2"].inc(act.activation(L[:, 0:1],
                                                    self.tdv[:, h:h + 1], AF.Exp))
                        act.wait_ge(S["cd"].h, b["cd"] + 3 * (h + 1))
                        if h >= 2:
                            act.wait_ge(S["pp"].h, b["pp"] + (h - 1) * NTC)
                        pm = PM[h % 2]
                        act.activation(pm[:, 0:128], self.iota[128], AF.Exp,
                                       scale=L[:, 1:2])
                        act.activation(pm[:, 128:256], self.iota[64], AF.Exp,
                                       scale=L[:, 1:2])
                        act.activation(pm[:, 256:384], self.iota[32], AF.Exp,
                                       scale=L[:, 1:2])
                        act.activation(pm[:, 384:512], self.iota[16], AF.Exp,
                                       scale=L[:, 1:2])
                        act.activation(pm[:, 512:640], self.iota[16], AF.Exp,
                                       scale=L[:, 2:3])
                        act.activation(pm[:, 640:768], self.iota[128], AF.Exp,
                                       scale=L[:, 0:1], bias=L[:, 4:5])
                        act.activation(pm[:, 768:896], self.iota[64], AF.Exp,
                                       scale=L[:, 0:1], bias=L[:, 5:6])
                        act.activation(pm[:, 896:1024], self.iota[32], AF.Exp,
                                       scale=L[:, 0:1], bias=L[:, 6:7])
                        act.activation(pm[:, 1024:1152], self.iota[16], AF.Exp,
                                       scale=L[:, 0:1], bias=L[:, 7:8])
                        act.activation(pm[:, 1152:1280], self.iota[16], AF.Exp,
                                       scale=L[:, 3:4], bias=L[:, 3:4])
                        S["pmb"].inc(act.activation(L[:, 8:9], L[:, 1:2],
                                                    AF.Exp, scale=128.0))
                    pm_build(0)
                    for hc in range(NHC):
                        act.wait_ge(S["sbm"].h, b["sbm"] + hc * (NTC + 1))
                        S["sbm"].inc(act.memzero(S0b[:]))
                        for c in range(NTC):
                            Xc = X(hc, c)
                            act.wait_ge(S["tt"].h, b["tt"] + Xc + 1)
                            act.activation(vTT[c % 2][:, 0:128],
                                           PSB[c % 2][:, 0:128], AF.Copy)
                            S["ae"].inc(act.activation(vTT[c % 2][:, 128:256],
                                                       PSB[c % 2][:, 128:256],
                                                       AF.Copy))
                            if c == 0 and hc + 1 < NHC:
                                pm_build(hc + 1)
                            act.wait_ge(S["uf"].h, b["uf"] + Xc + 1)
                            S["sbm"].inc(act.activation(S0b[:], S0[:], AF.Copy))

                @blk.gpsimd
                def _(g_):
                    for hc in range(NHC):
                        g_.wait_ge(S["pmb"].h, b["pmb"] + hc + 1)
                        nmkv = "kv" if hc % 2 == 0 else "kv2"
                        g_.wait_ge(S[nmkv].h, b[nmkv] + 48 * (hc // 2 + 1))
                        pm = PM[hc % 2]
                        g_.tensor_scalar_mul(UM[hc % 2][:], self.ones64,
                                             self.faaaa[:, hc:hc + 1])
                        for c in range(NTC):
                            Xc = X(hc, c)
                            sc = SCL[Xc % 4]
                            if Xc >= 4:
                                g_.wait_ge(S["om"].h, b["om"] + Xc - 3)
                            ksl = kvb["k"][hc % 2][:, c * C:(c + 1) * C]
                            g_.tensor_tensor(sc[:, 640:768], ksl, pm[:, 768:896], ALU.mult)
                            g_.tensor_tensor(sc[:, 768:896], ksl, pm[:, 896:1024], ALU.mult)
                            g_.tensor_tensor(sc[:, 896:1024], ksl, pm[:, 1024:1152], ALU.mult)
                            S["pp"].inc(g_.tensor_tensor(sc[:, 1152:1280], ksl,
                                                         pm[:, 640:768], ALU.mult))

                def emit_G1(te, hc, c):
                    Xc = X(hc, c)
                    sc = scl_of(Xc)
                    te.wait_ge(S["pp"].h, b["pp"] + Xc + 1)
                    te.wait_ge(S["dp"].h, b["dp"] + Xc + 1)
                    if Xc >= 2:
                        te.wait_ge(S["ae"].h, b["ae"] + Xc - 1)
                        te.wait_ge(S["cg"].h, b["cg"] + Xc - 1)
                        te.wait_ge(S["am"].h, b["am"] + Xc - 1)
                    vsl = kvb["v"][hc % 2][:, c * C:(c + 1) * C]
                    rt = sc[:, 0:128]
                    r64, r32, r16 = sc[:, 128:256], sc[:, 256:384], sc[:, 384:512]
                    rcl, kcl = sc[:, 512:640], sc[:, 1024:1152]
                    k64, k32, k16 = sc[:, 640:768], sc[:, 768:896], sc[:, 896:1024]
                    kg = sc[:, 1152:1280]
                    rk = sc[:, 1280:1408]
                    te.matmul(PSB[c % 2][:, 0:128], vsl, self.ident,
                              start=True, stop=True)
                    te.matmul(PSB[c % 2][:, 128:256], kg, self.ident,
                              start=True, stop=True)
                    pA = PSB[2 + c % 2]
                    for h in range(2):
                        po = 64 * h
                        co = 128 * h
                        te.matmul(pA[:, co:co + 128], kcl[po:po + 64, :],
                                  rcl[po:po + 64, :], start=True, stop=True,
                                  tile_position=(po, 0))
                        te.matmul(pA[0:64, co + 64:co + 128], k64[po:po + 64, 0:64],
                                  r64[po:po + 64, 64:128],
                                  start=True, stop=True, tile_position=(po, 0))
                        te.matmul(pA[0:32, co + 32:co + 64], k32[po:po + 64, 0:32],
                                  r32[po:po + 64, 32:64],
                                  start=True, stop=True, tile_position=(po, 0))
                        te.matmul(pA[64:96, co + 96:co + 128], k32[po:po + 64, 64:96],
                                  r32[po:po + 64, 96:128],
                                  start=True, stop=True, tile_position=(po, 64))
                        for m in range(4):
                            j0 = 32 * m
                            te.matmul(pA[j0:j0 + 16, co + j0 + 16:co + j0 + 32],
                                      k16[po:po + 64, j0:j0 + 16],
                                      r16[po:po + 64, j0 + 16:j0 + 32],
                                      start=True, stop=True,
                                      tile_position=(po, j0))
                        inst = te.matmul(PSB[4 + c % 2][po:po + 64, 0:128],
                                         UM[hc % 2][po:po + 64, :],
                                         rk[po:po + 64, :],
                                         start=True, stop=True,
                                         tile_position=(po, po))
                    S["tt"].inc(inst)

                def emit_OM(te, hc, c):
                    Xc = X(hc, c)
                    sc = scl_of(Xc)
                    rt = sc[:, 0:128]
                    te.wait_ge(S["am"].h, b["am"] + Xc + 1)
                    te.wait_ge(S["ae"].h, b["ae"] + Xc + 1)
                    te.wait_ge(S["sbm"].h, b["sbm"] + hc * (NTC + 1) + 1 + c)
                    for h in range(2):
                        po = 64 * h
                        te.matmul(PSB[6][po:po + 64, 0:128],
                                  vTT[c % 2][:, po:po + 64],
                                  ASB[c % 2][:, h * 128:(h + 1) * 128],
                                  start=True, stop=False, tile_position=(0, po))
                        te.matmul(PSB[6][po:po + 64, 0:128],
                                  S0b[po:po + 64, :], rt[po:po + 64, :],
                                  start=False, stop=True, tile_position=(po, po))
                        inst = te.matmul(PSB[7][po:po + 64, 0:64],
                                         vTT[c % 2][:, 128 + po:128 + po + 64],
                                         vTT[c % 2][:, po:po + 64],
                                         start=True, stop=True,
                                         tile_position=(0, po))
                    S["om"].inc(inst)

                @blk.tensor
                def _(te):
                    prev = None
                    for hc in range(NHC):
                        for c in range(NTC):
                            emit_G1(te, hc, c)
                            if prev is not None:
                                emit_OM(te, *prev)
                            prev = (hc, c)
                    emit_OM(te, *prev)

                @blk.vector
                def _(ve):
                    def lnw_chain(h):
                        L = LNW[h % 2]
                        ve.wait_ge(S["ax2"].h, b["ax2"] + h + 1)
                        S["cd"].inc(ve.tensor_scalar_mul(L[:, 1:2], L[:, 0:1], -1.0))
                        ve.wait_ge(S["cd"].h, S["cd"].n)
                        S["cd"].inc(ve.tensor_scalar_max(L[:, 2:3], L[:, 1:2],
                                                         LNWMIN))
                        ve.wait_ge(S["cd"].h, S["cd"].n)
                        ve.tensor_scalar_mul(L[:, 3:4], L[:, 2:3], -1.0)
                        ve.tensor_scalar_mul(L[:, 4:5], L[:, 1:2], 127.0)
                        ve.tensor_scalar_mul(L[:, 5:6], L[:, 1:2], 63.0)
                        ve.tensor_scalar_mul(L[:, 6:7], L[:, 1:2], 31.0)
                        S["cd"].inc(ve.tensor_scalar_mul(L[:, 7:8], L[:, 1:2], 15.0))

                    def preps(Xc):
                        hc, c = divmod(Xc, NTC)
                        sc = SCL[Xc % 4]
                        pm = PM[hc % 2]
                        if c == 0:
                            ve.wait_ge(S["pmb"].h, b["pmb"] + hc + 1)
                            nmkv = "kv" if hc % 2 == 0 else "kv2"
                            ve.wait_ge(S[nmkv].h, b[nmkv] + 48 * (hc // 2 + 1))
                        if Xc >= 4:
                            ve.wait_ge(S["om"].h, b["om"] + Xc - 3)
                        rsl = kvb["r"][hc % 2][:, c * C:(c + 1) * C]
                        ksl = kvb["k"][hc % 2][:, c * C:(c + 1) * C]
                        ve.tensor_tensor(sc[:, 0:128], rsl, pm[:, 0:128], ALU.mult)
                        ve.tensor_tensor(sc[:, 128:256], rsl, pm[:, 128:256], ALU.mult)
                        ve.tensor_tensor(sc[:, 256:384], rsl, pm[:, 256:384], ALU.mult)
                        ve.tensor_tensor(sc[:, 384:512], rsl, pm[:, 384:512], ALU.mult)
                        ve.tensor_tensor(sc[:, 512:640], rsl, pm[:, 512:640], ALU.mult)
                        ve.tensor_tensor(sc[:, 1024:1152], ksl, pm[:, 1152:1280],
                                         ALU.mult)
                        S["dp"].inc(ve.tensor_tensor(sc[:, 1280:1408], rsl, ksl,
                                                     ALU.mult))

                    def tail(Xc):
                        hc, c = divmod(Xc, NTC)
                        if c == 0:
                            lnw_chain(hc) if False else None
                            ve.wait_ge(S["sbm"].h,
                                       b["sbm"] + hc * (NTC + 1) + 1)
                            if hc >= 1:
                                ve.wait_ge(S["ob"].h, b["ob"] + 16 * hc)
                            ve.memset(S0[:], 0.0)
                        wc = LNW[hc % 2][:, 8:9]
                        ve.wait_ge(S["tt"].h, b["tt"] + Xc + 1)
                        pA = PSB[2 + c % 2]
                        ve.tensor_tensor(ASB[c % 2][:, 0:128], pA[:, 0:128],
                                         self.masku, ALU.mult)
                        S["am"].inc(ve.tensor_tensor(ASB[c % 2][:, 128:256],
                                                     pA[:, 128:256],
                                                     self.masku, ALU.mult))
                        ve.wait_ge(S["om"].h, b["om"] + Xc + 1)
                        vsl = kvb["v"][hc % 2][:, c * C:(c + 1) * C]
                        S["cg"].inc(ve.tensor_tensor(TMPU[c % 2][:], vsl,
                                                     PSB[4 + c % 2][:, 0:128],
                                                     ALU.mult))
                        ve.wait_ge(S["cg"].h, S["cg"].n)
                        ve.tensor_tensor(obb[:, c * C:(c + 1) * C],
                                         PSB[6][:, 0:128], TMPU[c % 2][:],
                                         ALU.add)
                        S["uf"].inc(ve.scalar_tensor_tensor(
                            S0[:], S0[:], wc, PSB[7][:, 0:64],
                            ALU.mult, ALU.add))

                    NX = NHC * NTC
                    lnw_chain(0)
                    preps(0)
                    preps(1)
                    for Xc in range(NX):
                        hc, c = divmod(Xc, NTC)
                        if c == 1 and hc + 1 < NHC:
                            lnw_chain(hc + 1)
                        if Xc >= 1:
                            tail(Xc - 1)
                        if Xc + 2 < NX:
                            preps(Xc + 2)
                    tail(NX - 1)

    def p5b_gn(self):
        nc, S, PSB = self.nc, self.S, self.PSB
        with ExitStack() as ps:
            obb = [ps.enter_context(nc.sbuf_tensor(self.uname(f"g_ob{i}"), [P, T], BF16))
                   for i in range(2)]
            sqb = [ps.enter_context(nc.sbuf_tensor(self.uname(f"g_sq{i}"), [P, T], BF16))
                   for i in range(2)]
            gb = [ps.enter_context(nc.sbuf_tensor(self.uname(f"g_g{i}"), [P, T], BF16))
                  for i in range(2)]
            b = self.snap()
            with nc.Block() as blk:
                @blk.gpsimd
                def _(g_):
                    def t3(hc2):
                        g_.wait_ge(S["cg"].h, b["cg"] + 8 * hc2 + 8)
                        S["ax2"].inc(g_.tensor_scalar(
                            sqb[hc2 % 2][:], sqb[hc2 % 2][:],
                            self.gnw[:, hc2:hc2 + 1], self.gnb[:, hc2:hc2 + 1],
                            ALU.mult, ALU.add))
                    for hc in range(NHC):
                        if hc >= 2:
                            g_.wait_ge(S["gn"].h, b["gn"] + hc - 1)
                        self.dinc("xc", hc, g_.dma_start(obb[hc % 2][:],
                                                         self.obscr[hc, :, :]))
                        self.dinc("gv", hc, g_.dma_start(gb[hc % 2][:],
                                                         self.kvrg["g"][hc, :, :]))
                        if hc >= 1:
                            t3(hc - 1)
                    t3(NHC - 1)

                @blk.scalar
                def _(act):
                    for hc in range(NHC):
                        self.dwait(act, "xc", hc, b)
                        if hc >= 2:
                            act.wait_ge(S["sst"].h, b["sst"] + hc - 1)
                        S["aux"].inc(act.activation(sqb[hc % 2][:], obb[hc % 2][:],
                                                    AF.Square))
                        for ts in range(NTS):
                            slot = 4 * hc + ts
                            pr = slot % 2
                            act.wait_ge(S["sst"].h, b["sst"] + hc + 1)
                            if slot >= 2:
                                act.wait_ge(S["cd"].h, b["cd"] + 2 * (slot - 1))
                                act.wait_ge(S["ln"].h, b["ln"] + slot - 1)
                            S["ca"].inc(act.activation(self.RMUs[pr][0:2, :],
                                                       PSB[ts][0:2, :], AF.Copy,
                                                       scale=1.0 / 64))
                            act.wait_ge(S["ca"].h, S["ca"].n)
                            S["ca"].inc(act.activation(self.RSQs[pr][0:2, :],
                                                       self.RMUs[pr][0:2, :],
                                                       AF.Square))
                            act.wait_ge(S["cd"].h, b["cd"] + 2 * slot + 1)
                            S["ca"].inc(act.activation(self.RVAs[pr][0:2, :],
                                                       self.RVAs[pr][0:2, :], AF.Ln,
                                                       bias=self.CF[0:2, 640:641]))
                            act.wait_ge(S["ca"].h, S["ca"].n)
                            S["ca"].inc(act.activation(self.RSBs[pr][0:2, :],
                                                       self.RVAs[pr][0:2, :], AF.Exp,
                                                       scale=-0.5))

                @blk.tensor
                def _(te):
                    for hc in range(NHC):
                        self.dwait(te, "xc", hc, b)
                        te.wait_ge(S["aux"].h, b["aux"] + hc + 1)
                        if hc >= 1:
                            te.wait_ge(S["cg"].h, b["cg"] + 8 * hc)
                        for ts in range(NTS):
                            te.matmul(PSB[ts][0:2, :], self.hm2,
                                      obb[hc % 2][:, ts * TS:(ts + 1) * TS],
                                      start=True, stop=True)
                        for ts in range(NTS):
                            inst = te.matmul(PSB[4 + ts][0:2, :], self.hm2,
                                             sqb[hc % 2][:, ts * TS:(ts + 1) * TS],
                                             start=True, stop=True)
                        S["sst"].inc(inst)
                        for ts in range(NTS):
                            slot = 4 * hc + ts
                            pr = slot % 2
                            te.wait_ge(S["cd"].h, b["cd"] + 2 * slot + 2)
                            te.matmul(PSB[ts][:, 0:512], self.hm2T,
                                      self.RSBs[pr][0:2, :],
                                      start=True, stop=True)
                            inst = te.matmul(PSB[4 + ts][:, 0:512], self.hm2T,
                                             self.RMBs[pr][0:2, :],
                                             start=True, stop=True)
                            S["ln"].inc(inst)

                @blk.vector
                def _(ve):
                    for hc in range(NHC):
                        for ts in range(NTS):
                            slot = 4 * hc + ts
                            pr = slot % 2
                            ve.wait_ge(S["ca"].h, b["ca"] + 16 * hc + 4 * ts + 2)
                            S["cd"].inc(ve.scalar_tensor_tensor(
                                self.RVAs[pr][0:2, :], PSB[4 + ts][0:2, :], 1.0 / 64,
                                self.RSQs[pr][0:2, :], ALU.mult, ALU.subtract))
                            ve.wait_ge(S["ca"].h, b["ca"] + 16 * hc + 4 * ts + 4)
                            S["cd"].inc(ve.tensor_mul(self.RMBs[pr][0:2, :],
                                                      self.RMUs[pr][0:2, :],
                                                      self.RSBs[pr][0:2, :]))
                        for ts in range(NTS):
                            slot = 4 * hc + ts
                            ve.wait_ge(S["ln"].h, b["ln"] + slot + 1)
                            sl = slice(ts * TS, (ts + 1) * TS)
                            S["cg"].inc(ve.tensor_tensor(
                                sqb[hc % 2][:, sl], obb[hc % 2][:, sl],
                                PSB[ts][:, 0:512], ALU.mult))
                            ve.wait_ge(S["cg"].h, S["cg"].n)
                            S["cg"].inc(ve.tensor_tensor(
                                sqb[hc % 2][:, sl], sqb[hc % 2][:, sl],
                                PSB[4 + ts][:, 0:512], ALU.subtract))
                        ve.wait_ge(S["ax2"].h, b["ax2"] + hc + 1)
                        self.dwait(ve, "gv", hc, b)
                        S["gn"].inc(ve.tensor_mul(
                            self.ACT_A[:, hc * T:(hc + 1) * T],
                            sqb[hc % 2][:], gb[hc % 2][:]))

    def p_final(self):
        nc, S = self.nc, self.S
        with nc.Block() as blk:
            @blk.sync
            def _(sync):
                for nm in ["ys", "ys2", "st", "st2", "ws", "ws2", "ob", "qs", "bc",
                           "rbs", "xc", "xc2", "bt", "bt2", "gv", "gv2"]:
                    if S[nm].n:
                        sync.wait_ge(S[nm].h, S[nm].n)


def _prep_inputs(inputs):
    cbf, cf32, crow = _host_consts()
    vecs = np.zeros((128, 8 * 32), dtype=np.float32)
    order = [inputs["L0_attn_time_mix_key"], inputs["L0_attn_time_mix_value"],
             inputs["L0_attn_time_mix_receptance"], inputs["L0_attn_time_mix_gate"],
             inputs["L0_attn_gn_weight"], inputs["L0_attn_gn_bias"],
             np.asarray(inputs["L0_attn_time_decay"]).reshape(-1),
             np.asarray(inputs["L0_attn_time_faaaa"]).reshape(-1)]
    for i, v in enumerate(order):
        vecs[:, i * 32:(i + 1) * 32] = np.asarray(v, np.float32).reshape(32, 128).T
    A = np.ascontiguousarray(np.asarray(inputs["L0_ffn_A"], np.float32).reshape(128, D))
    Bm = np.ascontiguousarray(
        np.asarray(inputs["L0_ffn_B"], np.float32).transpose(1, 0, 2).reshape(D, 128))
    bvec = np.asarray(inputs["L0_ffn_b"], np.float32)
    base = {
        "wk": np.asarray(inputs["L0_attn_w_key"], np.float32),
        "wv": np.asarray(inputs["L0_attn_w_value"], np.float32),
        "wr": np.asarray(inputs["L0_attn_w_receptance"], np.float32),
        "wg": np.asarray(inputs["L0_attn_w_gate"], np.float32),
        "wo": np.asarray(inputs["L0_attn_w_output"], np.float32),
        "wf": np.asarray(inputs["L0_ffn_W"], np.float32),
        "amat": A, "bmat": Bm,
        "rwm": np.asarray(inputs["L0_ffn_router_W"], np.float32),
        "rbr": np.asarray(inputs["L0_ffn_router_b"], np.float32).reshape(1, 8),
        "brow0": np.ascontiguousarray(bvec[:T]).reshape(1, T),
        "brow1": np.ascontiguousarray(bvec[T:]).reshape(1, T),
        "vecs": vecs, "cbf": cbf, "cf32": cf32, "crow": crow,
    }
    xall = np.asarray(inputs["L0_attn_hidden"], np.float32)
    maps = []
    for core in range(8):
        m = dict(base)
        m["x0"] = np.ascontiguousarray(xall[core])
        maps.append(m)
    return maps


def kernel(**inputs):
    maps = _prep_inputs(inputs)
    nc = KB().build()
    res = run_bass_kernel_spmd(nc, maps, core_ids=list(range(8)))
    out = np.stack([np.asarray(r["y"], np.float32) for r in res.results], axis=0)
    return out


# version 07
# speedup vs baseline: 1.0835x; 1.0307x over previous
import numpy as np
import ml_dtypes
from contextlib import ExitStack

import concourse.bass as bass
import concourse.mybir as mybir
from concourse.bass_utils import run_bass_kernel_spmd

F32 = mybir.dt.float32
BF16 = mybir.dt.bfloat16
AF = mybir.ActivationFunctionType
ALU = mybir.AluOpType
AX = mybir.AxisListType

P = 128
T = 2048
TS = 512
NTS = 4
NCH = 32
D = 4096
C = 128
NTC = 16
NHC = 32
LNWMIN = -5.0
EPS = 1e-5
QL, QR = 611, 2047
NQ = QR - QL

BF = ml_dtypes.bfloat16


def _host_consts():
    j = np.arange(128)
    cbf = np.zeros((128, 579), dtype=np.float32)
    cbf[:, 0:128] = np.eye(128)
    cbf[:, 128:256] = np.triu(np.ones((128, 128)), 1)
    cbf[:, 256:320] = 1.0
    cbf[:64, 320] = 1.0
    cbf[64:, 321] = 1.0
    cbf[:, 322] = 1.0
    for e in range(8):
        cbf[e, 323 + e * 16:323 + (e + 1) * 16] = 1.0
    cbf[0, 451:515] = 1.0
    cbf[1, 515:579] = 1.0
    cf32 = np.zeros((128, 642), dtype=np.float32)
    cf32[:, 0:128] = np.eye(128)
    cf32[:, 128:256] = j[None, :]
    cf32[:, 256:384] = (j % 64)[None, :]
    cf32[:, 384:512] = (j % 32)[None, :]
    cf32[:, 512:640] = (j % 16)[None, :]
    cf32[:, 640] = 1e-5
    return cbf.astype(BF), cf32, np.ones((1, 512), dtype=BF)


class Sem:
    def __init__(self, es, nc, name):
        self.h = es.enter_context(nc.semaphore(name))
        self.n = 0

    def inc(self, inst, k=1):
        inst.then_inc(self.h, k)
        self.n += k
        return self.n


class KB:
    def __init__(self):
        nc = bass.Bass("TRN2", target_bir_lowering=False)
        self.nc = nc
        self.x0 = nc.dram_tensor("x0", [T, D], F32, kind="ExternalInput").ap()
        self.w = {nm: nc.dram_tensor(nm, [D, D], F32, kind="ExternalInput").ap()
                  for nm in ["wk", "wv", "wr", "wg", "wo", "wf"]}
        self.amat = nc.dram_tensor("amat", [128, D], F32, kind="ExternalInput").ap()
        self.bmat = nc.dram_tensor("bmat", [D, 128], F32, kind="ExternalInput").ap()
        self.rwm = nc.dram_tensor("rwm", [8, D], F32, kind="ExternalInput").ap()
        self.rbr = nc.dram_tensor("rbr", [1, 8], F32, kind="ExternalInput").ap()
        self.brow0 = nc.dram_tensor("brow0", [1, T], F32, kind="ExternalInput").ap()
        self.brow1 = nc.dram_tensor("brow1", [1, T], F32, kind="ExternalInput").ap()
        self.vecs = nc.dram_tensor("vecs", [128, 8 * 32], F32, kind="ExternalInput").ap()
        self.cbf = nc.dram_tensor("cbf", [128, 579], BF16, kind="ExternalInput").ap()
        self.cf32 = nc.dram_tensor("cf32", [128, 642], F32, kind="ExternalInput").ap()
        self.crow = nc.dram_tensor("crow", [1, 512], BF16, kind="ExternalInput").ap()
        self.y = nc.dram_tensor("y", [T, D], F32, kind="ExternalOutput").ap()

        self.xfm = [nc.dram_tensor(f"xfm{i}", [NCH, P, T], BF16, kind="Internal").ap()
                    for i in range(3)]
        self.kvrg = {nm: nc.dram_tensor(nm + "scr", [NCH, P, T], BF16, kind="Internal").ap()
                     for nm in "kvrg"}
        self.obscr = nc.dram_tensor("obscr", [NHC, P, T], BF16, kind="Internal").ap()
        self.wtf = nc.dram_tensor("wtf_scr", [NCH, P, D], BF16, kind="Internal").ap()
        self.atil = nc.dram_tensor("atil_scr", [NCH, P, 128], BF16, kind="Internal").ap()
        self.btil = nc.dram_tensor("btil_scr", [NCH, P, 128], BF16, kind="Internal").ap()
        self.rows_scr = nc.dram_tensor("rows_scr", [4, T], BF16, kind="Internal").ap()

    def snap(self):
        return {k: s.n for k, s in self.S.items()}

    HW_DMA_SEMS = ["qs", "wl", "wl2", "bc", "ob", "kv", "kv2", "wta", "wta2"]
    SW_DMA_SEMS = ["st", "st2", "xc", "xc2", "bt", "bt2", "ws", "ws2", "rbs",
                   "gv", "gv2", "ys", "ys2"]

    def _drain_dmas(self):
        S = self.S
        with self.nc.Block() as blk:
            @blk.sync
            def _(sync):
                for nm in self.HW_DMA_SEMS:
                    if S[nm].n:
                        sync.wait_ge(S[nm].h, S[nm].n)

            @blk.gpsimd
            def _(g_):
                for nm in self.SW_DMA_SEMS:
                    if S[nm].n:
                        g_.wait_ge(S[nm].h, S[nm].n)

    def dinc(self, name, idx, inst):
        self.S[name if idx % 2 == 0 else name + "2"].inc(inst, 16)

    def dwait(self, eng, name, idx, b):
        nm = name if idx % 2 == 0 else name + "2"
        eng.wait_ge(self.S[nm].h, b[nm] + 16 * (idx // 2 + 1))

    def uname(self, base):
        self._uc = getattr(self, "_uc", 0) + 1
        return f"{base}_{self._uc}"

    def build(self, upto=None, debug=()):
        self._upto = upto
        self._debug = tuple(debug)
        nc = self.nc
        if debug:
            self.dbg_out = {}
            scr = {"xfm0": self.xfm[0], "xfm1": self.xfm[1], "xfm2": self.xfm[2],
                   "kscr": self.kvrg["k"], "vscr": self.kvrg["v"],
                   "rscr": self.kvrg["r"], "gscr": self.kvrg["g"],
                   "obscr": self.obscr, "rows": self.rows_scr,
                   "atil": self.atil, "btil": self.btil, "wtf": self.wtf}
            for nm in debug:
                t = scr[nm]
                self.dbg_out[nm] = nc.dram_tensor(
                    "dbg_" + nm, list(t.shape), BF16, kind="ExternalOutput").ap()
        es = ExitStack()
        with es:
            sb = lambda name, shape, dt: es.enter_context(nc.sbuf_tensor(name, shape, dt))
            self.ACT_A = sb("ACT_A", [P, NCH * T], BF16)
            self.RSTD = sb("RSTD", [P, T], BF16)
            self.MURS = sb("MURS", [P, T], BF16)
            self.TSC = sb("TSC", [P, T], BF16)
            self.VECS = sb("VECS", [P, 8 * 32], F32)
            self.CBF = sb("CBF", [P, 579], BF16)
            self.CF = sb("CF", [P, 642], F32)
            self.CROW = sb("CROW", [1, 512], BF16)
            self.RWT = sb("RWT", [P, 256], BF16)
            self.BRB0 = sb("BRB0", [1, T], BF16)
            self.BRB1 = sb("BRB1", [1, T], BF16)
            self.RBR8 = sb("RBR8", [1, 8], F32)
            self.RTBC = sb("RTBC", [P, 1], F32)
            self.QSUM = sb("QSUM", [P, NCH], F32)
            self.QBF = sb("QBF", [P, NCH], BF16)
            self.SROW = sb("SROW", [1, 32], F32)
            self.PBF = sb("PBF", [1, 8], BF16)
            self.PCOL = sb("PCOL", [8, 1], BF16)
            self.RMUs = [sb(f"RMU{i}", [2, TS], F32) for i in range(2)]
            self.RSQs = [sb(f"RSQ{i}", [2, TS], F32) for i in range(2)]
            self.RVAs = [sb(f"RVA{i}", [2, TS], F32) for i in range(2)]
            self.RSBs = [sb(f"RSB{i}", [2, TS], BF16) for i in range(2)]
            self.RMBs = [sb(f"RMB{i}", [2, TS], BF16) for i in range(2)]
            self.RMU, self.RSQ, self.RVA = self.RMUs[0], self.RSQs[0], self.RVAs[0]
            self.RSB, self.RMB = self.RSBs[0], self.RMBs[0]

            self.ident = self.CBF[:, 0:128]
            self.masku = self.CBF[:, 128:256]
            self.ones64 = self.CBF[:, 256:320]
            self.hm2 = self.CBF[:, 320:322]
            self.onescol = self.CBF[:, 322:323]
            self.exp8 = self.CBF[0:8, 323:451]
            self.hm2T = self.CBF[0:2, 451:579]
            self.identf = self.CF[:, 0:128]
            self.iota = {128: self.CF[:, 128:256], 64: self.CF[:, 256:384],
                         32: self.CF[:, 384:512], 16: self.CF[:, 512:640]}
            self.tm = {pn: self.VECS[:, i * 32:(i + 1) * 32]
                       for i, pn in enumerate("kvrg")}
            self.gnw = self.VECS[:, 4 * 32:5 * 32]
            self.gnb = self.VECS[:, 5 * 32:6 * 32]
            self.tdv = self.VECS[:, 6 * 32:7 * 32]
            self.faaaa = self.VECS[:, 7 * 32:8 * 32]

            self.PSB = [es.enter_context(nc.psum_tensor(f"PS{i}", [P, 512], F32))
                        for i in range(8)]
            self.S = {nm: Sem(es, nc, "s_" + nm) for nm in
                      ["qs", "wl", "cv", "tpg", "sst", "wt", "mm", "ev", "xc",
                       "st", "bt", "ws", "xt", "yr", "ys", "pp", "tt", "ae",
                       "am", "om", "uf", "sbm", "kv", "pmb", "cd", "ca", "cp",
                       "cg", "ln", "gn", "aux", "ax2", "bc", "ob",
                       "wl2", "xc2", "st2", "bt2", "ys2", "ws2", "kv2",
                       "wta", "wta2", "rbs", "gv", "gv2", "dp"]}

            seq = [
                ("consts", self.p0_consts),
                ("entry", self.p0_entry),
                ("small", self.p0_small),
                ("lnq1", lambda: self.p1_lnq(self.xfm[0])),
                ("router1", self.p1_router),
                ("tsc1", self.p1_tsc),
                ("ffn1", lambda: self.p2_main("ffn", self.w["wf"],
                                              xres=self.xfm[0], xdst=self.xfm[1],
                                              save_wt=True)),
                ("stats1", lambda: self.p_stats(self.xfm[1])),
                ("bcast1", self._bcast_rows),
            ]
            for pn in "kvrg":
                seq.append(("mix_" + pn,
                            (lambda p: lambda: self.p4_mix(p, self.xfm[1]))(pn)))
                seq.append(("proj_" + pn,
                            (lambda p: lambda: self.p2_main(
                                "proj", self.w["w" + p], dst_scr=self.kvrg[p],
                                silu=(p == "g")))(pn)))
            seq += [
                ("wkv", self.p5a_wkv),
                ("gn", self.p5b_gn),
                ("outproj", lambda: self.p2_main("outproj", self.w["wo"],
                                                 xres=self.xfm[1],
                                                 xdst=self.xfm[2])),
                ("stats2", lambda: self.p_stats(self.xfm[2])),
                ("lnq2", lambda: self.p1_lnq(self.xfm[2])),
                ("router2", self.p1_router),
                ("tsc2", self.p1_tsc),
                ("ffn2", lambda: self.p2_main("ffn2", None, xres=self.xfm[2])),
            ]
            for name, fn in seq:
                fn()
                self._drain_dmas()
                if upto is not None and name == upto:
                    break
            if self._debug:
                self._emit_debug()
            self.p_final()
        return nc

    def _emit_debug(self):
        nc, S = self.nc, self.S
        scr = {"xfm0": self.xfm[0], "xfm1": self.xfm[1], "xfm2": self.xfm[2],
               "kscr": self.kvrg["k"], "vscr": self.kvrg["v"],
               "rscr": self.kvrg["r"], "gscr": self.kvrg["g"],
               "obscr": self.obscr, "rows": self.rows_scr,
               "atil": self.atil, "btil": self.btil, "wtf": self.wtf}
        with nc.Block() as blk:
            @blk.sync
            def _(sync):
                for snm in ["qs", "wl", "wl2", "st", "st2", "xc", "xc2", "bt",
                            "bt2", "ws", "ws2", "ys2", "kv", "kv2", "gv", "gv2",
                            "bc", "ob", "rbs", "wta", "wta2"]:
                    if S[snm].n:
                        sync.wait_ge(S[snm].h, S[snm].n)
                for nm in self._debug:
                    srcap = scr[nm]
                    dstap = self.dbg_out[nm]
                    if nm == "rows":
                        srcap = srcap[0:2, :]
                        dstap = dstap[0:2, :]
                    S["qs"].inc(sync.dma_start(dstap, srcap), 16)
                sync.wait_ge(S["qs"].h, S["qs"].n)

    def p0_consts(self):
        nc, S = self.nc, self.S
        with nc.Block() as blk:
            @blk.sync
            def _(sync):
                for dst, src in [(self.CBF[:], self.cbf), (self.CF[:], self.cf32),
                                 (self.CROW[:], self.crow), (self.VECS[:], self.vecs),
                                 (self.RBR8[:], self.rbr)]:
                    S["qs"].inc(sync.dma_start(dst, src), 16)
                sync.wait_ge(S["qs"].h, S["qs"].n)

    def p0_entry(self):
        nc, S, PSB = self.nc, self.S, self.PSB
        NT = 16
        with ExitStack() as ps:
            xtf = [ps.enter_context(nc.sbuf_tensor(self.uname(f"e_xtf{i}"), [P, 2048], F32))
                   for i in range(2)]
            xtb = ps.enter_context(nc.sbuf_tensor(self.uname("e_xtb"), [P, D], BF16))
            stg = [ps.enter_context(nc.sbuf_tensor(self.uname(f"e_stg{i}"), [P, 512], BF16))
                   for i in range(2)]
            sqg = [ps.enter_context(nc.sbuf_tensor(self.uname(f"e_sqg{i}"), [P, 512], BF16))
                   for i in range(2)]
            b = self.snap()
            with nc.Block() as blk:
                @blk.sync
                def _(sync):
                    for i in range(2 * NT):
                        if i >= 2:
                            sync.wait_ge(S["cv"].h, b["cv"] + i - 1)
                        tt, pc = divmod(i, 2)
                        self.dinc("wl", i, sync.dma_start(
                            xtf[i % 2][:], self.x0[tt * 128:(tt + 1) * 128,
                                                   pc * 2048:(pc + 1) * 2048]))

                @blk.scalar
                def _(act):
                    for tt in range(NT):
                        if tt >= 1:
                            act.wait_ge(S["tpg"].h, b["tpg"] + 8 * tt)
                        for pc in range(2):
                            i = tt * 2 + pc
                            self.dwait(act, "wl", i, b)
                            S["cv"].inc(act.activation(
                                xtb[:, pc * 2048:(pc + 1) * 2048],
                                xtf[i % 2][:], AF.Copy))
                        for g in range(8):
                            G = tt * 8 + g
                            act.wait_ge(S["wt"].h, b["wt"] + G + 1)
                            if G >= 2:
                                act.wait_ge(S["sst"].h, b["sst"] + G - 1)
                            S["aux"].inc(act.activation(
                                sqg[G % 2][:], stg[G % 2][:], AF.Square))
                        if tt % 4 == 3:
                            ts = tt // 4
                            act.wait_ge(S["sst"].h, b["sst"] + 8 * (tt + 1))
                            if ts >= 1:
                                act.wait_ge(S["cg"].h, b["cg"] + ts)
                                act.wait_ge(S["rbs"].h, b["rbs"] + 16 * 2 * ts)
                            S["ca"].inc(act.activation(self.RMU[0:1, :],
                                                       PSB[2][0:1, :], AF.Copy,
                                                       scale=1.0 / D))
                            act.wait_ge(S["ca"].h, S["ca"].n)
                            S["ca"].inc(act.activation(self.RSQ[0:1, :],
                                                       self.RMU[0:1, :], AF.Square))
                            act.wait_ge(S["cd"].h, b["cd"] + 1 * (ts + 1))
                            S["ca"].inc(act.activation(self.RVA[0:1, :],
                                                       self.RVA[0:1, :], AF.Ln,
                                                       bias=self.CF[0:1, 640:641]))
                            act.wait_ge(S["ca"].h, S["ca"].n)
                            S["ca"].inc(act.activation(self.RSB[0:1, :],
                                                       self.RVA[0:1, :], AF.Exp,
                                                       scale=-0.5))

                @blk.tensor
                def _(te):
                    for tt in range(NT):
                        te.wait_ge(S["cv"].h, b["cv"] + 2 * tt + 2)
                        if tt % 4 == 0 and tt > 0:
                            te.wait_ge(S["ca"].h, b["ca"] + 4 * (tt // 4))
                            te.wait_ge(S["cd"].h, b["cd"] + (tt // 4))
                        for g in range(8):
                            G = tt * 8 + g
                            if G >= 2:
                                te.wait_ge(S["wt"].h, b["wt"] + G - 1)
                            for m in range(4):
                                jj = g * 4 + m
                                inst = te.matmul(
                                    PSB[G % 2][:, m * 128:(m + 1) * 128],
                                    xtb[:, jj * 128:(jj + 1) * 128], self.ident,
                                    start=True, stop=True)
                            S["tpg"].inc(inst)
                            if g >= 1:
                                self._entry_stats(te, tt, g - 1, b, stg, sqg)
                        self._entry_stats(te, tt, 7, b, stg, sqg)

                @blk.vector
                def _(ve):
                    for tt in range(NT):
                        for g in range(8):
                            G = tt * 8 + g
                            ve.wait_ge(S["tpg"].h, b["tpg"] + G + 1)
                            if G >= 2:
                                ve.wait_ge(S["aux"].h, b["aux"] + G - 1)
                                self.dwait(ve, "st", G - 2, b)
                            S["wt"].inc(ve.tensor_copy(stg[G % 2][:], PSB[G % 2][:]))
                        if tt % 4 == 3:
                            ts = tt // 4
                            ve.wait_ge(S["ca"].h, b["ca"] + 4 * ts + 2)
                            S["cd"].inc(ve.scalar_tensor_tensor(
                                self.RVA[0:1, :], PSB[3][0:1, :], 1.0 / D,
                                self.RSQ[0:1, :], ALU.mult, ALU.subtract))
                            ve.wait_ge(S["ca"].h, b["ca"] + 4 * ts + 4)
                            S["cg"].inc(ve.tensor_mul(self.RMB[0:1, :],
                                                      self.RMU[0:1, :],
                                                      self.RSB[0:1, :]))

                @blk.gpsimd
                def _(g_):
                    for G in range(8 * NT):
                        g_.wait_ge(S["wt"].h, b["wt"] + G + 1)
                        tt, g = divmod(G, 8)
                        dst = self.xfm[0][4 * g:4 * g + 4, :,
                                          tt * 128:(tt + 1) * 128].transpose([1, 0, 2])
                        self.dinc("st", G, g_.dma_start(dst, stg[G % 2][:]))
                        if tt % 4 == 3 and g == 7:
                            ts = tt // 4
                            g_.wait_ge(S["cg"].h, b["cg"] + ts + 1)
                            S["rbs"].inc(g_.dma_start(
                                self.rows_scr[0:1, ts * TS:(ts + 1) * TS],
                                self.RSB[0:1, :]), 16)
                            S["rbs"].inc(g_.dma_start(
                                self.rows_scr[1:2, ts * TS:(ts + 1) * TS],
                                self.RMB[0:1, :]), 16)
                    g_.wait_ge(S["st"].h, S["st"].n)
                    g_.wait_ge(S["st2"].h, S["st2"].n)

    def _entry_stats(self, te, tt, g, b, stg, sqg):
        S, PSB = self.S, self.PSB
        G = tt * 8 + g
        te.wait_ge(S["wt"].h, b["wt"] + G + 1)
        te.wait_ge(S["aux"].h, b["aux"] + G + 1)
        col = (tt % 4) * 128
        first = (g == 0)
        last = (g == 7)
        for m in range(4):
            te.matmul(PSB[2][0:1, col:col + 128], self.onescol,
                      stg[G % 2][:, m * 128:(m + 1) * 128],
                      start=(first and m == 0 and tt % 4 == 0),
                      stop=(last and m == 3 and tt % 4 == 3))
        for m in range(4):
            inst = te.matmul(PSB[3][0:1, col:col + 128], self.onescol,
                             sqg[G % 2][:, m * 128:(m + 1) * 128],
                             start=(first and m == 0 and tt % 4 == 0),
                             stop=(last and m == 3 and tt % 4 == 3))
        S["sst"].inc(inst)

    def p0_small(self):
        nc, S, PSB = self.nc, self.S, self.PSB
        with ExitStack() as ps:
            stf = ps.enter_context(nc.sbuf_tensor(self.uname("s_stf"), [P, 1024], F32))
            stb = ps.enter_context(nc.sbuf_tensor(self.uname("s_stb"), [P, 1024], BF16))
            otb = ps.enter_context(nc.sbuf_tensor(self.uname("s_otb"), [P, 512], BF16))
            brf = ps.enter_context(nc.sbuf_tensor(self.uname("s_brf"), [1, T], F32))
            for strip in range(4):
                self._small_strip(self.amat[:, strip * 1024:(strip + 1) * 1024],
                                  stf, stb, None,
                                  self.atil[strip * 8:(strip + 1) * 8, :, :], 8)
            for strip in range(8):
                src = self.bmat[strip * 512:(strip + 1) * 512, :].rearrange(
                    "(a p) e -> p a e", p=128)
                self._small_strip(src, stf, stb, otb,
                                  self.btil[strip * 4:(strip + 1) * 4, :, :], 4)
            for s4 in range(4):
                with nc.Block() as blk:
                    @blk.sync
                    def _(sync):
                        S["qs"].inc(sync.dma_start(
                            stf[0:8, :], self.rwm[:, s4 * 1024:(s4 + 1) * 1024]), 16)
                        sync.wait_ge(S["qs"].h, S["qs"].n)
                with nc.Block() as blk:
                    @blk.scalar
                    def _(act):
                        act.activation(stb[0:8, :], stf[0:8, :], AF.Copy)
                with nc.Block() as blk:
                    @blk.tensor
                    def _(te):
                        for m in range(8):
                            te.matmul(PSB[0][:, m * 64:m * 64 + 8],
                                      stb[0:8, m * 128:(m + 1) * 128],
                                      self.ident[0:8, 0:8], start=True, stop=True)
                with nc.Block() as blk:
                    @blk.vector
                    def _(ve):
                        for m in range(8):
                            kc = s4 * 8 + m
                            ve.tensor_copy(self.RWT[:, kc * 8:(kc + 1) * 8],
                                           PSB[0][:, m * 64:m * 64 + 8])
            with nc.Block() as blk:
                @blk.sync
                def _(sync):
                    S["qs"].inc(sync.dma_start(brf[:], self.brow0), 16)
                    sync.wait_ge(S["qs"].h, S["qs"].n)
            with nc.Block() as blk:
                @blk.scalar
                def _(act):
                    act.activation(self.BRB0[:], brf[:], AF.Copy)
            with nc.Block() as blk:
                @blk.sync
                def _(sync):
                    S["qs"].inc(sync.dma_start(brf[:], self.brow1), 16)
                    sync.wait_ge(S["qs"].h, S["qs"].n)
            with nc.Block() as blk:
                @blk.scalar
                def _(act):
                    act.activation(self.BRB1[:], brf[:], AF.Copy)

    def _small_strip(self, src, stf, stb, otb, dst_scr, ntile):
        nc, S, PSB = self.nc, self.S, self.PSB
        w = ntile * 128
        out = stb if otb is None else otb
        with nc.Block() as blk:
            @blk.sync
            def _(sync):
                S["qs"].inc(sync.dma_start(stf[:, 0:w], src), 16)
                sync.wait_ge(S["qs"].h, S["qs"].n)
        with nc.Block() as blk:
            @blk.scalar
            def _(act):
                act.activation(stb[:, 0:w], stf[:, 0:w], AF.Copy)
        with nc.Block() as blk:
            @blk.tensor
            def _(te):
                for m in range(ntile):
                    te.matmul(PSB[m // 4 % 2][:, (m % 4) * 128:(m % 4 + 1) * 128],
                              stb[:, m * 128:(m + 1) * 128], self.ident,
                              start=True, stop=True)
        with nc.Block() as blk:
            @blk.vector
            def _(ve):
                for q in range((ntile + 3) // 4):
                    n = min(4, ntile - q * 4) * 128
                    ve.tensor_copy(out[:, q * 512:q * 512 + n], PSB[q % 2][:, 0:n])
        with nc.Block() as blk:
            @blk.sync
            def _(sync):
                S["qs"].inc(sync.dma_start(
                    dst_scr.transpose([1, 0, 2]),
                    out[:, 0:w].rearrange("p (a c) -> p a c", c=128)), 16)
                sync.wait_ge(S["qs"].h, S["qs"].n)

    def _bcast_rows(self):
        nc, S = self.nc, self.S
        with nc.Block() as blk:
            @blk.sync
            def _(sync):
                if S["rbs"].n:
                    sync.wait_ge(S["rbs"].h, S["rbs"].n)
                if S["qs"].n:
                    sync.wait_ge(S["qs"].h, S["qs"].n)
                S["bc"].inc(sync.dma_start(
                    self.RSTD[:], self.rows_scr[0:1, :].broadcast_to([P, T])), 16)
                S["bc"].inc(sync.dma_start(
                    self.MURS[:], self.rows_scr[1:2, :].broadcast_to([P, T])), 16)
                sync.wait_ge(S["bc"].h, S["bc"].n)

    def p1_lnq(self, xsrc):
        nc, S = self.nc, self.S
        self._bcast_rows()
        with ExitStack() as ps:
            xst = [ps.enter_context(nc.sbuf_tensor(self.uname(f"l_x{i}"), [P, T], BF16))
                   for i in range(2)]
            b = self.snap()
            with nc.Block() as blk:
                @blk.gpsimd
                def _(g_):
                    for ch in range(NCH):
                        if ch >= 2:
                            g_.wait_ge(S["cd"].h, b["cd"] + 3 * (ch - 1))
                        self.dinc("xc", ch, g_.dma_start(xst[ch % 2][:],
                                                         xsrc[ch, :, :]))

                @blk.vector
                def _(ve):
                    for ch in range(NCH):
                        a = self.ACT_A[:, ch * T:(ch + 1) * T]
                        self.dwait(ve, "xc", ch, b)
                        S["cd"].inc(ve.tensor_mul(a, xst[ch % 2][:], self.RSTD[:]))
                        ve.wait_ge(S["cd"].h, S["cd"].n)
                        S["cd"].inc(ve.tensor_sub(a, a, self.MURS[:]))
                        ve.wait_ge(S["cd"].h, S["cd"].n)
                        S["cd"].inc(ve.reduce_sum(
                            self.QSUM[:, ch:ch + 1],
                            self.ACT_A[:, ch * T + QL:ch * T + QR], axis=AX.X))

    def p1_router(self):
        nc, S, PSB = self.nc, self.S, self.PSB
        with nc.Block() as blk:
            @blk.scalar
            def _(act):
                act.activation(self.QBF[:], self.QSUM[:], AF.Copy)
        with nc.Block() as blk:
            @blk.tensor
            def _(te):
                for kc in range(NCH):
                    te.matmul(PSB[0][0:1, 0:8], self.QBF[:, kc:kc + 1],
                              self.RWT[:, kc * 8:(kc + 1) * 8],
                              start=(kc == 0), stop=(kc == NCH - 1))
        with nc.Block() as blk:
            @blk.vector
            def _(ve):
                S["cd"].inc(ve.scalar_tensor_tensor(
                    self.SROW[0:1, 0:8], PSB[0][0:1, 0:8], 1.0 / NQ,
                    self.RBR8[:], ALU.mult, ALU.add))
                ve.wait_ge(S["cd"].h, S["cd"].n)
                S["cd"].inc(ve.reduce_max(self.SROW[0:1, 8:9],
                                          self.SROW[0:1, 0:8], axis=AX.X))
                ve.wait_ge(S["cd"].h, S["cd"].n)
                S["cd"].inc(ve.tensor_scalar_mul(self.SROW[0:1, 9:10],
                                                 self.SROW[0:1, 8:9], -1.0))
        with nc.Block() as blk:
            @blk.scalar
            def _(act):
                act.activation(self.SROW[0:1, 16:24], self.SROW[0:1, 0:8],
                               AF.Exp, bias=self.SROW[0:1, 9:10])
        with nc.Block() as blk:
            @blk.vector
            def _(ve):
                S["cd"].inc(ve.reduce_sum(self.SROW[0:1, 10:11],
                                          self.SROW[0:1, 16:24], axis=AX.X))
                ve.wait_ge(S["cd"].h, S["cd"].n)
                S["cd"].inc(ve.reciprocal(self.SROW[0:1, 11:12],
                                          self.SROW[0:1, 10:11]))
                ve.wait_ge(S["cd"].h, S["cd"].n)
                S["cd"].inc(ve.tensor_scalar(self.SROW[0:1, 24:32],
                                             self.SROW[0:1, 16:24],
                                             self.SROW[0:1, 11:12], 2.0,
                                             ALU.mult, ALU.mult))
        with nc.Block() as blk:
            @blk.scalar
            def _(act):
                act.activation(self.PBF[:], self.SROW[0:1, 24:32], AF.Copy)
        with nc.Block() as blk:
            @blk.tensor
            def _(te):
                te.matmul(PSB[1][0:8, 0:1], self.PBF[:], self.CROW[0:1, 0:1],
                          start=True, stop=True)
        with nc.Block() as blk:
            @blk.vector
            def _(ve):
                ve.tensor_copy(self.PCOL[:], PSB[1][0:8, 0:1])
        with nc.Block() as blk:
            @blk.tensor
            def _(te):
                te.matmul(PSB[2][:, 0:1], self.exp8, self.PCOL[:],
                          start=True, stop=True)
        with nc.Block() as blk:
            @blk.vector
            def _(ve):
                ve.tensor_copy(self.RTBC[:], PSB[2][:, 0:1])

    def p1_tsc(self):
        nc, S, PSB = self.nc, self.S, self.PSB
        with ExitStack() as ps:
            atl = ps.enter_context(nc.sbuf_tensor(self.uname("t_atl"), [P, NCH * 128], BF16))
            b = self.snap()
            with nc.Block() as blk:
                @blk.sync
                def _(sync):
                    S["qs"].inc(sync.dma_start(
                        atl[:], self.atil[:, :, :].transpose([1, 0, 2])), 16)
                    sync.wait_ge(S["qs"].h, S["qs"].n)
            with nc.Block() as blk:
                @blk.tensor
                def _(te):
                    for ts in range(NTS):
                        if ts >= 2:
                            te.wait_ge(S["ev"].h, b["ev"] + ts - 1)
                        for kc in range(NCH):
                            inst = te.matmul(
                                PSB[3 + ts % 2][:],
                                atl[:, kc * 128:(kc + 1) * 128],
                                self.ACT_A[:, kc * T + ts * TS:kc * T + (ts + 1) * TS],
                                start=(kc == 0), stop=(kc == NCH - 1))
                        S["mm"].inc(inst)

                @blk.vector
                def _(ve):
                    for ts in range(NTS):
                        ve.wait_ge(S["mm"].h, b["mm"] + ts + 1)
                        S["ev"].inc(ve.tensor_scalar_mul(
                            self.TSC[:, ts * TS:(ts + 1) * TS],
                            PSB[3 + ts % 2][:], self.RTBC[:, 0:1]))

    def p_stats(self, xsrc):
        nc, S, PSB = self.nc, self.S, self.PSB
        with ExitStack() as ps:
            xst = [ps.enter_context(nc.sbuf_tensor(self.uname(f"st_x{i}"), [P, T], BF16))
                   for i in range(2)]
            sqs = [ps.enter_context(nc.sbuf_tensor(self.uname(f"st_q{i}"), [P, T], BF16))
                   for i in range(2)]
            b = self.snap()
            with nc.Block() as blk:
                @blk.gpsimd
                def _(g_):
                    for ch in range(NCH):
                        if ch >= 2:
                            g_.wait_ge(S["sst"].h, b["sst"] + ch - 1)
                        self.dinc("xc", ch, g_.dma_start(xst[ch % 2][:], xsrc[ch, :, :]))

                @blk.scalar
                def _(act):
                    for ch in range(NCH):
                        self.dwait(act, "xc", ch, b)
                        if ch >= 2:
                            act.wait_ge(S["sst"].h, b["sst"] + ch - 1)
                        S["aux"].inc(act.activation(sqs[ch % 2][:], xst[ch % 2][:],
                                                    AF.Square))
                    for ts in range(NTS):
                        act.wait_ge(S["sst"].h, b["sst"] + NCH)
                        if ts >= 1:
                            act.wait_ge(S["cd"].h, b["cd"] + 2 * ts)
                            act.wait_ge(S["qs"].h, b["qs"] + 16 * 2 * ts)
                        S["ca"].inc(act.activation(self.RMU[0:1, :],
                                                   PSB[ts][0:1, :], AF.Copy,
                                                   scale=1.0 / D))
                        act.wait_ge(S["ca"].h, S["ca"].n)
                        S["ca"].inc(act.activation(self.RSQ[0:1, :],
                                                   self.RMU[0:1, :], AF.Square))
                        act.wait_ge(S["cd"].h, b["cd"] + 2 * ts + 1)
                        S["ca"].inc(act.activation(self.RVA[0:1, :],
                                                   self.RVA[0:1, :], AF.Ln,
                                                   bias=self.CF[0:1, 640:641]))
                        act.wait_ge(S["ca"].h, S["ca"].n)
                        S["ca"].inc(act.activation(self.RSB[0:1, :],
                                                   self.RVA[0:1, :], AF.Exp,
                                                   scale=-0.5))

                @blk.tensor
                def _(te):
                    for ch in range(NCH):
                        self.dwait(te, "xc", ch, b)
                        te.wait_ge(S["aux"].h, b["aux"] + ch + 1)
                        for ts in range(NTS):
                            te.matmul(PSB[ts][0:1, :], self.onescol,
                                      xst[ch % 2][:, ts * TS:(ts + 1) * TS],
                                      start=(ch == 0), stop=(ch == NCH - 1))
                        for ts in range(NTS):
                            inst = te.matmul(PSB[4 + ts][0:1, :], self.onescol,
                                             sqs[ch % 2][:, ts * TS:(ts + 1) * TS],
                                             start=(ch == 0), stop=(ch == NCH - 1))
                        S["sst"].inc(inst)

                @blk.vector
                def _(ve):
                    for ts in range(NTS):
                        ve.wait_ge(S["ca"].h, b["ca"] + 4 * ts + 2)
                        S["cd"].inc(ve.scalar_tensor_tensor(
                            self.RVA[0:1, :], PSB[4 + ts][0:1, :], 1.0 / D,
                            self.RSQ[0:1, :], ALU.mult, ALU.subtract))
                        ve.wait_ge(S["ca"].h, b["ca"] + 4 * ts + 4)
                        S["cd"].inc(ve.tensor_mul(self.RMB[0:1, :],
                                                  self.RMU[0:1, :],
                                                  self.RSB[0:1, :]))

                @blk.sync
                def _(sync):
                    for ts in range(NTS):
                        sync.wait_ge(S["cd"].h, b["cd"] + 2 * ts + 2)
                        S["qs"].inc(sync.dma_start(
                            self.rows_scr[0:1, ts * TS:(ts + 1) * TS],
                            self.RSB[0:1, :]), 16)
                        S["qs"].inc(sync.dma_start(
                            self.rows_scr[1:2, ts * TS:(ts + 1) * TS],
                            self.RMB[0:1, :]), 16)
                    sync.wait_ge(S["qs"].h, S["qs"].n)

    def p4_mix(self, pn, xsrc):
        nc, S = self.nc, self.S
        with ExitStack() as ps:
            xst = [ps.enter_context(nc.sbuf_tensor(self.uname(f"m_x{i}"), [P, T], BF16))
                   for i in range(2)]
            lnst = [ps.enter_context(nc.sbuf_tensor(self.uname(f"m_l{i}"), [P, T], BF16))
                    for i in range(2)]
            mab = [ps.enter_context(nc.sbuf_tensor(self.uname(f"m_a{i}"), [P, T], BF16))
                   for i in range(2)]
            mbb = [ps.enter_context(nc.sbuf_tensor(self.uname(f"m_b{i}"), [P, T], BF16))
                   for i in range(2)]
            tmm = ps.enter_context(nc.sbuf_tensor(self.uname("m_tm"), [P, 32], F32))
            b = self.snap()
            tmc = self.tm[pn]
            with nc.Block() as blk:
                @blk.gpsimd
                def _(g_):
                    for ch in range(NCH):
                        if ch >= 2:
                            g_.wait_ge(S["cd"].h, b["cd"] + 3 * (ch - 2) + 2)
                        self.dinc("xc", ch, g_.dma_start(xst[ch % 2][:],
                                                         xsrc[ch, :, :]))

                @blk.vector
                def _(ve):
                    S["cd"].inc(ve.tensor_scalar(tmm[:], tmc[:], -1.0, 1.0,
                                                 ALU.mult, ALU.add))

                    def combine(ch2):
                        a = self.ACT_A[:, ch2 * T:(ch2 + 1) * T]
                        ve.wait_ge(S["ca"].h, b["ca"] + ch2 + 1)
                        ma, mb = mab[ch2 % 2], mbb[ch2 % 2]
                        ve.tensor_add(a[:, 1:T], ma[:, 1:T], mb[:, 0:T - 1])
                        S["cd"].inc(ve.tensor_copy(a[:, 0:1], ma[:, 0:1]))

                    for ch in range(NCH):
                        ln = lnst[ch % 2]
                        self.dwait(ve, "xc", ch, b)
                        if ch >= 2:
                            ve.wait_ge(S["ca"].h, b["ca"] + ch - 1)
                        S["cd"].inc(ve.tensor_mul(ln[:], xst[ch % 2][:],
                                                  self.RSTD[:]))
                        ve.wait_ge(S["cd"].h, S["cd"].n)
                        S["cd"].inc(ve.tensor_sub(ln[:], ln[:], self.MURS[:]))
                        if ch >= 1:
                            combine(ch - 1)
                    combine(NCH - 1)

                @blk.scalar
                def _(act):
                    for ch in range(NCH):
                        act.wait_ge(S["cd"].h, b["cd"] + 3 * ch + 3)
                        if ch >= 2:
                            act.wait_ge(S["cd"].h, b["cd"] + 3 * (ch - 1) + 1)
                        act.activation(mab[ch % 2][:], lnst[ch % 2][:], AF.Copy,
                                       scale=tmc[:, ch:ch + 1])
                        S["ca"].inc(act.activation(mbb[ch % 2][:], lnst[ch % 2][:],
                                                   AF.Copy, scale=tmm[:, ch:ch + 1]))

    def p2_main(self, mode, Wsrc, xres=None, xdst=None, dst_scr=None,
                silu=False, save_wt=False):
        nc, S, PSB = self.nc, self.S, self.PSB
        lora = mode in ("ffn", "ffn2")
        from_scr = (mode == "ffn2")
        with ExitStack() as ps:
            wT = [ps.enter_context(nc.sbuf_tensor(self.uname(f"w_t{i}"), [P, D], BF16))
                  for i in range(2)]
            if not from_scr:
                wfp = [ps.enter_context(nc.sbuf_tensor(self.uname(f"w_f{i}"), [P, 1024], F32))
                       for i in range(2)]
                wbf = ps.enter_context(nc.sbuf_tensor(self.uname("w_b"), [P, D], BF16))
            if lora:
                btl = [ps.enter_context(nc.sbuf_tensor(self.uname(f"w_bt{i}"), [P, 128], BF16))
                       for i in range(2)]
            if mode != "proj":
                xcb = [ps.enter_context(nc.sbuf_tensor(self.uname(f"w_xc{i}"), [P, TS], BF16))
                       for i in range(2)]
            if mode == "ffn2":
                fin = [ps.enter_context(nc.sbuf_tensor(self.uname(f"w_fi{i}"), [P, TS], F32))
                       for i in range(2)]
                yro = [ps.enter_context(nc.sbuf_tensor(self.uname(f"w_yr{i}"), [P, TS], F32))
                       for i in range(2)]
            else:
                ob = [ps.enter_context(nc.sbuf_tensor(self.uname(f"w_ob{i}"), [P, TS], BF16))
                      for i in range(2)]
            b = self.snap()

            with nc.Block() as blk:
                @blk.sync
                def _(sync):
                    if from_scr:
                        for och in range(NCH):
                            if och >= 2:
                                sync.wait_ge(S["mm"].h, b["mm"] + 4 * (och - 1))
                            self.dinc("wta", och, sync.dma_start(
                                wT[och % 2][:], self.wtf[och, :, :]))
                    else:
                        for och in range(NCH):
                            for pc in range(4):
                                i = och * 4 + pc
                                if i >= 2:
                                    sync.wait_ge(S["cv"].h, b["cv"] + i - 1)
                                self.dinc("wl", i, sync.dma_start(
                                    wfp[i % 2][:],
                                    Wsrc[och * 128:(och + 1) * 128,
                                         pc * 1024:(pc + 1) * 1024]))

                if not from_scr:
                    @blk.scalar
                    def _(act):
                        for och in range(NCH):
                            if och >= 1:
                                act.wait_ge(S["tpg"].h, b["tpg"] + 8 * och)
                            for pc in range(4):
                                i = och * 4 + pc
                                self.dwait(act, "wl", i, b)
                                S["cv"].inc(act.activation(
                                    wbf[:, pc * 1024:(pc + 1) * 1024],
                                    wfp[i % 2][:], AF.Copy))
                            if silu and och >= 1:
                                self._silu_evac(act, och - 1, b, ob)
                        if silu:
                            self._silu_evac(act, NCH - 1, b, ob)

                @blk.tensor
                def _(te):
                    def emit_tp(och):
                        te.wait_ge(S["cv"].h, b["cv"] + 4 * och + 4)
                        for g in range(8):
                            G = och * 8 + g
                            if G >= 2:
                                te.wait_ge(S["wt"].h, b["wt"] + G - 1)
                            for m in range(4):
                                jj = g * 4 + m
                                inst = te.matmul(
                                    PSB[2 + G % 2][:, m * 128:(m + 1) * 128],
                                    wbf[:, jj * 128:(jj + 1) * 128], self.ident,
                                    start=True, stop=True)
                            S["tpg"].inc(inst)

                    def emit_mm(och):
                        if not from_scr:
                            te.wait_ge(S["wt"].h, b["wt"] + 8 * (och + 1))
                        else:
                            self.dwait(te, "wta", och, b)
                        if lora:
                            self.dwait(te, "bt", och, b)
                        for ts in range(NTS):
                            I = och * 4 + ts
                            if I >= 2:
                                te.wait_ge(S["ev"].h, b["ev"] + I - 1)
                            for kc in range(NCH):
                                inst = te.matmul(
                                    PSB[I % 2][:],
                                    wT[och % 2][:, kc * 128:(kc + 1) * 128],
                                    self.ACT_A[:, kc * T + ts * TS:kc * T + (ts + 1) * TS],
                                    start=(kc == 0),
                                    stop=(kc == NCH - 1 and not lora))
                            if lora:
                                te.matmul(PSB[I % 2][:], btl[och % 2][:],
                                          self.TSC[:, ts * TS:(ts + 1) * TS],
                                          start=False, stop=False)
                                brow_src = self.BRB0 if och < 16 else self.BRB1
                                cc = (och % 16) * 128
                                inst = te.matmul(PSB[I % 2][:],
                                                 brow_src[0:1, cc:cc + 128],
                                                 self.CROW[0:1, :],
                                                 start=False, stop=True)
                            S["mm"].inc(inst)
                            if mode == "ffn2" and I >= 1:
                                self._exit_tp(te, I - 1, b, fin)

                    if from_scr:
                        for och in range(NCH):
                            emit_mm(och)
                    else:
                        emit_tp(0)
                        for och in range(NCH):
                            if och + 1 < NCH:
                                emit_tp(och + 1)
                            emit_mm(och)
                    if mode == "ffn2":
                        self._exit_tp(te, 4 * NCH - 1, b, fin)

                @blk.vector
                def _(ve):
                    def wt_evacs(och):
                        for g in range(8):
                            G = och * 8 + g
                            ve.wait_ge(S["tpg"].h, b["tpg"] + G + 1)
                            if och >= 2 and g == 0:
                                ve.wait_ge(S["mm"].h, b["mm"] + 4 * (och - 1))
                                if save_wt:
                                    self.dwait(ve, "ws", och - 2, b)
                            S["wt"].inc(ve.tensor_copy(
                                wT[och % 2][:, g * 512:(g + 1) * 512],
                                PSB[2 + G % 2][:]))
                    if not from_scr:
                        wt_evacs(0)
                    for och in range(NCH):
                        if not from_scr and och + 1 < NCH:
                            wt_evacs(och + 1)
                        if silu:
                            continue
                        for ts in range(NTS):
                            I = och * 4 + ts
                            ve.wait_ge(S["mm"].h, b["mm"] + I + 1)
                            if mode != "proj":
                                self.dwait(ve, "xc", I, b)
                            if I >= 2 and mode != "ffn2":
                                self.dwait(ve, "st", I - 2, b)
                            if mode == "ffn":
                                S["ev"].inc(ve.scalar_tensor_tensor(
                                    ob[I % 2][:], PSB[I % 2][:], 0.5, xcb[I % 2][:],
                                    ALU.mult, ALU.add))
                            elif mode == "outproj":
                                S["ev"].inc(ve.tensor_add(
                                    ob[I % 2][:], PSB[I % 2][:], xcb[I % 2][:]))
                            elif mode == "proj":
                                S["ev"].inc(ve.tensor_copy(ob[I % 2][:],
                                                           PSB[I % 2][:]))
                            else:
                                if I >= 2:
                                    ve.wait_ge(S["xt"].h, b["xt"] + I - 1)
                                S["ev"].inc(ve.scalar_tensor_tensor(
                                    fin[I % 2][:], PSB[I % 2][:], 0.5, xcb[I % 2][:],
                                    ALU.mult, ALU.add))
                        if mode == "ffn2":
                            for ts in range(NTS):
                                I = och * 4 + ts
                                ve.wait_ge(S["xt"].h, b["xt"] + I + 1)
                                if I >= 2:
                                    self.dwait(ve, "ys", I - 2, b)
                                S["yr"].inc(ve.tensor_copy(yro[I % 2][:],
                                                           PSB[4 + I % 2][:]))

                @blk.gpsimd
                def _(g_):
                    def ystore(I2):
                        och2, ts2 = divmod(I2, 4)
                        g_.wait_ge(S["yr"].h, b["yr"] + I2 + 1)
                        dsty = self.y[ts2 * TS:(ts2 + 1) * TS,
                                      och2 * 128:(och2 + 1) * 128].rearrange(
                                          "(jj p) o -> p jj o", p=128)
                        self.dinc("ys", I2, g_.dma_start(
                            dsty, yro[I2 % 2][:].rearrange(
                                "p (jj o) -> p jj o", o=128)))
                    for och in range(NCH):
                        if lora:
                            if och >= 2:
                                g_.wait_ge(S["mm"].h, b["mm"] + 4 * (och - 1))
                            self.dinc("bt", och, g_.dma_start(btl[och % 2][:],
                                                              self.btil[och, :, :]))
                        for ts in range(NTS):
                            I = och * 4 + ts
                            if mode != "proj":
                                if I >= 2:
                                    g_.wait_ge(S["ev"].h, b["ev"] + I - 1)
                                self.dinc("xc", I, g_.dma_start(
                                    xcb[I % 2][:],
                                    xres[och, :, ts * TS:(ts + 1) * TS]))
                            if mode in ("ffn", "outproj"):
                                g_.wait_ge(S["ev"].h, b["ev"] + I + 1)
                                self.dinc("st", I, g_.dma_start(
                                    xdst[och, :, ts * TS:(ts + 1) * TS],
                                    ob[I % 2][:]))
                            elif mode == "proj":
                                g_.wait_ge(S["ev"].h, b["ev"] + I + 1)
                                self.dinc("st", I, g_.dma_start(
                                    dst_scr[och, :, ts * TS:(ts + 1) * TS],
                                    ob[I % 2][:]))
                            else:
                                if och >= 1:
                                    ystore((och - 1) * 4 + ts)
                        if save_wt:
                            g_.wait_ge(S["wt"].h, b["wt"] + 8 * (och + 1))
                            self.dinc("ws", och, g_.dma_start(self.wtf[och, :, :],
                                                              wT[och % 2][:]))
                    if mode == "ffn2":
                        for ts in range(NTS):
                            ystore((NCH - 1) * 4 + ts)

    def _silu_evac(self, act, och, b, ob):
        S, PSB = self.S, self.PSB
        for ts in range(NTS):
            I = och * 4 + ts
            act.wait_ge(S["mm"].h, b["mm"] + I + 1)
            if I >= 2:
                self.dwait(act, "st", I - 2, b)
            S["ev"].inc(act.activation(ob[I % 2][:], PSB[I % 2][:], AF.Silu))

    def _exit_tp(self, te, I, b, fin):
        S, PSB = self.S, self.PSB
        te.wait_ge(S["ev"].h, b["ev"] + I + 1)
        if I >= 2:
            te.wait_ge(S["yr"].h, b["yr"] + I - 1)
        for m in range(4):
            inst = te.matmul(PSB[4 + I % 2][:, m * 128:(m + 1) * 128],
                             fin[I % 2][:, m * 128:(m + 1) * 128], self.identf,
                             start=True, stop=True)
        S["xt"].inc(inst)

    def p5a_wkv(self):
        nc, S, PSB = self.nc, self.S, self.PSB
        with ExitStack() as ps:
            kvb = {nm: [self.ACT_A[:, (2 * j + i) * T:(2 * j + i + 1) * T]
                        for i in range(2)]
                   for j, nm in enumerate("kvr")}
            obb = ps.enter_context(nc.sbuf_tensor(self.uname("k_ob"), [P, T], BF16))
            PM = [ps.enter_context(nc.sbuf_tensor(self.uname(f"k_pm{i}"), [P, 10 * 128], BF16))
                  for i in range(2)]
            SCL = [ps.enter_context(nc.sbuf_tensor(self.uname(f"k_sc{i}"), [P, 11 * 128], BF16))
                   for i in range(4)]
            UM = [ps.enter_context(nc.sbuf_tensor(self.uname(f"k_um{i}"), [P, 64], BF16))
                  for i in range(2)]
            vTT = [ps.enter_context(nc.sbuf_tensor(self.uname(f"k_vt{i}"), [P, 256], BF16))
                   for i in range(2)]
            ASB = [ps.enter_context(nc.sbuf_tensor(self.uname(f"k_as{i}"), [P, 256], BF16))
                   for i in range(2)]
            TMPU = [ps.enter_context(nc.sbuf_tensor(self.uname(f"k_tu{i}"), [P, 128], BF16))
                    for i in range(2)]
            S0 = ps.enter_context(nc.sbuf_tensor(self.uname("k_s0"), [P, 64], F32))
            S0b = ps.enter_context(nc.sbuf_tensor(self.uname("k_s0b"), [P, 64], BF16))
            LNW = [ps.enter_context(nc.sbuf_tensor(self.uname(f"k_lnw{i}"), [P, 10], F32))
                   for i in range(2)]
            b = self.snap()

            def X(hc, c):
                return hc * NTC + c

            def scl_of(Xc):
                return SCL[Xc % 4]

            with nc.Block() as blk:
                @blk.sync
                def _(sync):
                    for hc in range(NHC):
                        if hc >= 2:
                            sync.wait_ge(S["uf"].h, b["uf"] + (hc - 1) * NTC)
                        for nm in "kvr":
                            self.dinc("kv", hc, sync.dma_start(
                                kvb[nm][hc % 2][:], self.kvrg[nm][hc, :, :]))
                        if hc >= 1:
                            sync.wait_ge(S["uf"].h, b["uf"] + hc * NTC)
                            S["ob"].inc(sync.dma_start(
                                self.obscr[hc - 1, :, :], obb[:]), 16)
                    sync.wait_ge(S["uf"].h, b["uf"] + NHC * NTC)
                    S["ob"].inc(sync.dma_start(self.obscr[NHC - 1, :, :], obb[:]), 16)
                    sync.wait_ge(S["ob"].h, S["ob"].n)

                @blk.scalar
                def _(act):
                    def pm_build(h):
                        L = LNW[h % 2]
                        S["ax2"].inc(act.activation(L[:, 0:1],
                                                    self.tdv[:, h:h + 1], AF.Exp))
                        act.wait_ge(S["cd"].h, b["cd"] + 3 * (h + 1))
                        if h >= 2:
                            act.wait_ge(S["pp"].h, b["pp"] + (h - 1) * NTC)
                        pm = PM[h % 2]
                        act.activation(pm[:, 0:128], self.iota[128], AF.Exp,
                                       scale=L[:, 1:2])
                        act.activation(pm[:, 128:256], self.iota[64], AF.Exp,
                                       scale=L[:, 1:2])
                        act.activation(pm[:, 256:384], self.iota[32], AF.Exp,
                                       scale=L[:, 1:2])
                        act.activation(pm[:, 384:512], self.iota[16], AF.Exp,
                                       scale=L[:, 1:2])
                        act.activation(pm[:, 512:640], self.iota[16], AF.Exp,
                                       scale=L[:, 2:3])
                        act.activation(pm[:, 640:768], self.iota[128], AF.Exp,
                                       scale=L[:, 0:1], bias=L[:, 4:5])
                        act.activation(pm[:, 768:896], self.iota[64], AF.Exp,
                                       scale=L[:, 0:1], bias=L[:, 5:6])
                        act.activation(pm[:, 896:1024], self.iota[32], AF.Exp,
                                       scale=L[:, 0:1], bias=L[:, 6:7])
                        act.activation(pm[:, 1024:1152], self.iota[16], AF.Exp,
                                       scale=L[:, 0:1], bias=L[:, 7:8])
                        act.activation(pm[:, 1152:1280], self.iota[16], AF.Exp,
                                       scale=L[:, 3:4], bias=L[:, 3:4])
                        S["pmb"].inc(act.activation(L[:, 8:9], L[:, 1:2],
                                                    AF.Exp, scale=128.0))
                    pm_build(0)
                    for hc in range(NHC):
                        act.wait_ge(S["sbm"].h, b["sbm"] + hc * (NTC + 1))
                        S["sbm"].inc(act.memzero(S0b[:]))
                        for c in range(NTC):
                            Xc = X(hc, c)
                            act.wait_ge(S["tt"].h, b["tt"] + Xc + 1)
                            act.activation(vTT[c % 2][:, 0:128],
                                           PSB[c % 2][:, 0:128], AF.Copy)
                            S["ae"].inc(act.activation(vTT[c % 2][:, 128:256],
                                                       PSB[c % 2][:, 128:256],
                                                       AF.Copy))
                            if c == 0 and hc + 1 < NHC:
                                pm_build(hc + 1)
                            act.wait_ge(S["uf"].h, b["uf"] + Xc + 1)
                            S["sbm"].inc(act.activation(S0b[:], S0[:], AF.Copy))

                @blk.gpsimd
                def _(g_):
                    for hc in range(NHC):
                        g_.wait_ge(S["pmb"].h, b["pmb"] + hc + 1)
                        nmkv = "kv" if hc % 2 == 0 else "kv2"
                        g_.wait_ge(S[nmkv].h, b[nmkv] + 48 * (hc // 2 + 1))
                        pm = PM[hc % 2]
                        g_.tensor_scalar_mul(UM[hc % 2][:], self.ones64,
                                             self.faaaa[:, hc:hc + 1])
                        for c in range(NTC):
                            Xc = X(hc, c)
                            sc = SCL[Xc % 4]
                            if Xc >= 4:
                                g_.wait_ge(S["om"].h, b["om"] + Xc - 3)
                            ksl = kvb["k"][hc % 2][:, c * C:(c + 1) * C]
                            g_.tensor_tensor(sc[:, 640:768], ksl, pm[:, 768:896], ALU.mult)
                            g_.tensor_tensor(sc[:, 768:896], ksl, pm[:, 896:1024], ALU.mult)
                            g_.tensor_tensor(sc[:, 896:1024], ksl, pm[:, 1024:1152], ALU.mult)
                            S["pp"].inc(g_.tensor_tensor(sc[:, 1152:1280], ksl,
                                                         pm[:, 640:768], ALU.mult))

                def emit_G1(te, hc, c):
                    Xc = X(hc, c)
                    sc = scl_of(Xc)
                    te.wait_ge(S["pp"].h, b["pp"] + Xc + 1)
                    te.wait_ge(S["dp"].h, b["dp"] + Xc + 1)
                    if Xc >= 2:
                        te.wait_ge(S["ae"].h, b["ae"] + Xc - 1)
                        te.wait_ge(S["cg"].h, b["cg"] + Xc - 1)
                        te.wait_ge(S["am"].h, b["am"] + Xc - 1)
                    vsl = kvb["v"][hc % 2][:, c * C:(c + 1) * C]
                    rt = sc[:, 0:128]
                    r64, r32, r16 = sc[:, 128:256], sc[:, 256:384], sc[:, 384:512]
                    rcl, kcl = sc[:, 512:640], sc[:, 1024:1152]
                    k64, k32, k16 = sc[:, 640:768], sc[:, 768:896], sc[:, 896:1024]
                    kg = sc[:, 1152:1280]
                    rk = sc[:, 1280:1408]
                    te.matmul(PSB[c % 2][:, 0:128], vsl, self.ident,
                              start=True, stop=True)
                    te.matmul(PSB[c % 2][:, 128:256], kg, self.ident,
                              start=True, stop=True)
                    pA = PSB[2 + c % 2]
                    for h in range(2):
                        po = 64 * h
                        co = 128 * h
                        te.matmul(pA[:, co:co + 128], kcl[po:po + 64, :],
                                  rcl[po:po + 64, :], start=True, stop=True,
                                  tile_position=(po, 0))
                        te.matmul(pA[0:64, co + 64:co + 128], k64[po:po + 64, 0:64],
                                  r64[po:po + 64, 64:128],
                                  start=True, stop=True, tile_position=(po, 0))
                        te.matmul(pA[0:32, co + 32:co + 64], k32[po:po + 64, 0:32],
                                  r32[po:po + 64, 32:64],
                                  start=True, stop=True, tile_position=(po, 0))
                        te.matmul(pA[64:96, co + 96:co + 128], k32[po:po + 64, 64:96],
                                  r32[po:po + 64, 96:128],
                                  start=True, stop=True, tile_position=(po, 64))
                        for m in range(4):
                            j0 = 32 * m
                            te.matmul(pA[j0:j0 + 16, co + j0 + 16:co + j0 + 32],
                                      k16[po:po + 64, j0:j0 + 16],
                                      r16[po:po + 64, j0 + 16:j0 + 32],
                                      start=True, stop=True,
                                      tile_position=(po, j0))
                        inst = te.matmul(PSB[4 + c % 2][po:po + 64, 0:128],
                                         UM[hc % 2][po:po + 64, :],
                                         rk[po:po + 64, :],
                                         start=True, stop=True,
                                         tile_position=(po, po))
                    S["tt"].inc(inst)

                def emit_OM(te, hc, c):
                    Xc = X(hc, c)
                    sc = scl_of(Xc)
                    rt = sc[:, 0:128]
                    te.wait_ge(S["am"].h, b["am"] + Xc + 1)
                    te.wait_ge(S["ae"].h, b["ae"] + Xc + 1)
                    te.wait_ge(S["sbm"].h, b["sbm"] + hc * (NTC + 1) + 1 + c)
                    for h in range(2):
                        po = 64 * h
                        te.matmul(PSB[6][po:po + 64, 0:128],
                                  vTT[c % 2][:, po:po + 64],
                                  ASB[c % 2][:, h * 128:(h + 1) * 128],
                                  start=True, stop=False, tile_position=(0, po))
                        te.matmul(PSB[6][po:po + 64, 0:128],
                                  S0b[po:po + 64, :], rt[po:po + 64, :],
                                  start=False, stop=True, tile_position=(po, po))
                        inst = te.matmul(PSB[7][po:po + 64, 0:64],
                                         vTT[c % 2][:, 128 + po:128 + po + 64],
                                         vTT[c % 2][:, po:po + 64],
                                         start=True, stop=True,
                                         tile_position=(0, po))
                    S["om"].inc(inst)

                @blk.tensor
                def _(te):
                    prev = None
                    for hc in range(NHC):
                        for c in range(NTC):
                            emit_G1(te, hc, c)
                            if prev is not None:
                                emit_OM(te, *prev)
                            prev = (hc, c)
                    emit_OM(te, *prev)

                @blk.vector
                def _(ve):
                    def lnw_chain(h):
                        L = LNW[h % 2]
                        ve.wait_ge(S["ax2"].h, b["ax2"] + h + 1)
                        S["cd"].inc(ve.tensor_scalar_mul(L[:, 1:2], L[:, 0:1], -1.0))
                        ve.wait_ge(S["cd"].h, S["cd"].n)
                        S["cd"].inc(ve.tensor_scalar_max(L[:, 2:3], L[:, 1:2],
                                                         LNWMIN))
                        ve.wait_ge(S["cd"].h, S["cd"].n)
                        ve.tensor_scalar_mul(L[:, 3:4], L[:, 2:3], -1.0)
                        ve.tensor_scalar_mul(L[:, 4:5], L[:, 1:2], 127.0)
                        ve.tensor_scalar_mul(L[:, 5:6], L[:, 1:2], 63.0)
                        ve.tensor_scalar_mul(L[:, 6:7], L[:, 1:2], 31.0)
                        S["cd"].inc(ve.tensor_scalar_mul(L[:, 7:8], L[:, 1:2], 15.0))

                    def preps(Xc):
                        hc, c = divmod(Xc, NTC)
                        sc = SCL[Xc % 4]
                        pm = PM[hc % 2]
                        if c == 0:
                            ve.wait_ge(S["pmb"].h, b["pmb"] + hc + 1)
                            nmkv = "kv" if hc % 2 == 0 else "kv2"
                            ve.wait_ge(S[nmkv].h, b[nmkv] + 48 * (hc // 2 + 1))
                        if Xc >= 4:
                            ve.wait_ge(S["om"].h, b["om"] + Xc - 3)
                        rsl = kvb["r"][hc % 2][:, c * C:(c + 1) * C]
                        ksl = kvb["k"][hc % 2][:, c * C:(c + 1) * C]
                        ve.tensor_tensor(sc[:, 0:128], rsl, pm[:, 0:128], ALU.mult)
                        ve.tensor_tensor(sc[:, 128:256], rsl, pm[:, 128:256], ALU.mult)
                        ve.tensor_tensor(sc[:, 256:384], rsl, pm[:, 256:384], ALU.mult)
                        ve.tensor_tensor(sc[:, 384:512], rsl, pm[:, 384:512], ALU.mult)
                        ve.tensor_tensor(sc[:, 512:640], rsl, pm[:, 512:640], ALU.mult)
                        ve.tensor_tensor(sc[:, 1024:1152], ksl, pm[:, 1152:1280],
                                         ALU.mult)
                        S["dp"].inc(ve.tensor_tensor(sc[:, 1280:1408], rsl, ksl,
                                                     ALU.mult))

                    def tail(Xc):
                        hc, c = divmod(Xc, NTC)
                        if c == 0:
                            lnw_chain(hc) if False else None
                            ve.wait_ge(S["sbm"].h,
                                       b["sbm"] + hc * (NTC + 1) + 1)
                            if hc >= 1:
                                ve.wait_ge(S["ob"].h, b["ob"] + 16 * hc)
                            ve.memset(S0[:], 0.0)
                        wc = LNW[hc % 2][:, 8:9]
                        ve.wait_ge(S["tt"].h, b["tt"] + Xc + 1)
                        pA = PSB[2 + c % 2]
                        ve.tensor_tensor(ASB[c % 2][:, 0:128], pA[:, 0:128],
                                         self.masku, ALU.mult)
                        S["am"].inc(ve.tensor_tensor(ASB[c % 2][:, 128:256],
                                                     pA[:, 128:256],
                                                     self.masku, ALU.mult))
                        ve.wait_ge(S["om"].h, b["om"] + Xc + 1)
                        vsl = kvb["v"][hc % 2][:, c * C:(c + 1) * C]
                        S["cg"].inc(ve.tensor_tensor(TMPU[c % 2][:], vsl,
                                                     PSB[4 + c % 2][:, 0:128],
                                                     ALU.mult))
                        ve.wait_ge(S["cg"].h, S["cg"].n)
                        ve.tensor_tensor(obb[:, c * C:(c + 1) * C],
                                         PSB[6][:, 0:128], TMPU[c % 2][:],
                                         ALU.add)
                        S["uf"].inc(ve.scalar_tensor_tensor(
                            S0[:], S0[:], wc, PSB[7][:, 0:64],
                            ALU.mult, ALU.add))

                    NX = NHC * NTC
                    lnw_chain(0)
                    preps(0)
                    preps(1)
                    for Xc in range(NX):
                        hc, c = divmod(Xc, NTC)
                        if c == 1 and hc + 1 < NHC:
                            lnw_chain(hc + 1)
                        if Xc >= 1:
                            tail(Xc - 1)
                        if Xc + 2 < NX:
                            preps(Xc + 2)
                    tail(NX - 1)

    def p5b_gn(self):
        nc, S, PSB = self.nc, self.S, self.PSB
        with ExitStack() as ps:
            obb = [ps.enter_context(nc.sbuf_tensor(self.uname(f"g_ob{i}"), [P, T], BF16))
                   for i in range(2)]
            sqb = [ps.enter_context(nc.sbuf_tensor(self.uname(f"g_sq{i}"), [P, T], BF16))
                   for i in range(2)]
            gb = [ps.enter_context(nc.sbuf_tensor(self.uname(f"g_g{i}"), [P, T], BF16))
                  for i in range(2)]
            b = self.snap()
            with nc.Block() as blk:
                @blk.gpsimd
                def _(g_):
                    def t3(hc2):
                        g_.wait_ge(S["cg"].h, b["cg"] + 8 * hc2 + 8)
                        S["ax2"].inc(g_.tensor_scalar(
                            sqb[hc2 % 2][:], sqb[hc2 % 2][:],
                            self.gnw[:, hc2:hc2 + 1], self.gnb[:, hc2:hc2 + 1],
                            ALU.mult, ALU.add))
                    for hc in range(NHC):
                        if hc >= 2:
                            g_.wait_ge(S["gn"].h, b["gn"] + hc - 1)
                        self.dinc("xc", hc, g_.dma_start(obb[hc % 2][:],
                                                         self.obscr[hc, :, :]))
                        self.dinc("gv", hc, g_.dma_start(gb[hc % 2][:],
                                                         self.kvrg["g"][hc, :, :]))
                        if hc >= 1:
                            t3(hc - 1)
                    t3(NHC - 1)

                @blk.scalar
                def _(act):
                    for hc in range(NHC):
                        self.dwait(act, "xc", hc, b)
                        if hc >= 2:
                            act.wait_ge(S["sst"].h, b["sst"] + hc - 1)
                        S["aux"].inc(act.activation(sqb[hc % 2][:], obb[hc % 2][:],
                                                    AF.Square))
                        for ts in range(NTS):
                            slot = 4 * hc + ts
                            pr = slot % 2
                            act.wait_ge(S["sst"].h, b["sst"] + hc + 1)
                            if slot >= 2:
                                act.wait_ge(S["cd"].h, b["cd"] + 2 * (slot - 1))
                                act.wait_ge(S["ln"].h, b["ln"] + slot - 1)
                            S["ca"].inc(act.activation(self.RMUs[pr][0:2, :],
                                                       PSB[ts][0:2, :], AF.Copy,
                                                       scale=1.0 / 64))
                            act.wait_ge(S["ca"].h, S["ca"].n)
                            S["ca"].inc(act.activation(self.RSQs[pr][0:2, :],
                                                       self.RMUs[pr][0:2, :],
                                                       AF.Square))
                            act.wait_ge(S["cd"].h, b["cd"] + 2 * slot + 1)
                            S["ca"].inc(act.activation(self.RVAs[pr][0:2, :],
                                                       self.RVAs[pr][0:2, :], AF.Ln,
                                                       bias=self.CF[0:2, 640:641]))
                            act.wait_ge(S["ca"].h, S["ca"].n)
                            S["ca"].inc(act.activation(self.RSBs[pr][0:2, :],
                                                       self.RVAs[pr][0:2, :], AF.Exp,
                                                       scale=-0.5))

                @blk.tensor
                def _(te):
                    for hc in range(NHC):
                        self.dwait(te, "xc", hc, b)
                        te.wait_ge(S["aux"].h, b["aux"] + hc + 1)
                        if hc >= 1:
                            te.wait_ge(S["cg"].h, b["cg"] + 8 * hc)
                        for ts in range(NTS):
                            te.matmul(PSB[ts][0:2, :], self.hm2,
                                      obb[hc % 2][:, ts * TS:(ts + 1) * TS],
                                      start=True, stop=True)
                        for ts in range(NTS):
                            inst = te.matmul(PSB[4 + ts][0:2, :], self.hm2,
                                             sqb[hc % 2][:, ts * TS:(ts + 1) * TS],
                                             start=True, stop=True)
                        S["sst"].inc(inst)
                        for ts in range(NTS):
                            slot = 4 * hc + ts
                            pr = slot % 2
                            te.wait_ge(S["cd"].h, b["cd"] + 2 * slot + 2)
                            te.matmul(PSB[ts][:, 0:512], self.hm2T,
                                      self.RSBs[pr][0:2, :],
                                      start=True, stop=True)
                            inst = te.matmul(PSB[4 + ts][:, 0:512], self.hm2T,
                                             self.RMBs[pr][0:2, :],
                                             start=True, stop=True)
                            S["ln"].inc(inst)

                @blk.vector
                def _(ve):
                    for hc in range(NHC):
                        for ts in range(NTS):
                            slot = 4 * hc + ts
                            pr = slot % 2
                            ve.wait_ge(S["ca"].h, b["ca"] + 16 * hc + 4 * ts + 2)
                            S["cd"].inc(ve.scalar_tensor_tensor(
                                self.RVAs[pr][0:2, :], PSB[4 + ts][0:2, :], 1.0 / 64,
                                self.RSQs[pr][0:2, :], ALU.mult, ALU.subtract))
                            ve.wait_ge(S["ca"].h, b["ca"] + 16 * hc + 4 * ts + 4)
                            S["cd"].inc(ve.tensor_mul(self.RMBs[pr][0:2, :],
                                                      self.RMUs[pr][0:2, :],
                                                      self.RSBs[pr][0:2, :]))
                        for ts in range(NTS):
                            slot = 4 * hc + ts
                            ve.wait_ge(S["ln"].h, b["ln"] + slot + 1)
                            sl = slice(ts * TS, (ts + 1) * TS)
                            S["cg"].inc(ve.tensor_tensor(
                                sqb[hc % 2][:, sl], obb[hc % 2][:, sl],
                                PSB[ts][:, 0:512], ALU.mult))
                            ve.wait_ge(S["cg"].h, S["cg"].n)
                            S["cg"].inc(ve.tensor_tensor(
                                sqb[hc % 2][:, sl], sqb[hc % 2][:, sl],
                                PSB[4 + ts][:, 0:512], ALU.subtract))
                        ve.wait_ge(S["ax2"].h, b["ax2"] + hc + 1)
                        self.dwait(ve, "gv", hc, b)
                        S["gn"].inc(ve.tensor_mul(
                            self.ACT_A[:, hc * T:(hc + 1) * T],
                            sqb[hc % 2][:], gb[hc % 2][:]))

    def p_final(self):
        nc, S = self.nc, self.S
        with nc.Block() as blk:
            @blk.sync
            def _(sync):
                for nm in ["ys", "ys2", "st", "st2", "ws", "ws2", "ob", "qs", "bc",
                           "rbs", "xc", "xc2", "bt", "bt2", "gv", "gv2"]:
                    if S[nm].n:
                        sync.wait_ge(S[nm].h, S[nm].n)


def _prep_inputs(inputs):
    cbf, cf32, crow = _host_consts()
    vecs = np.zeros((128, 8 * 32), dtype=np.float32)
    order = [inputs["L0_attn_time_mix_key"], inputs["L0_attn_time_mix_value"],
             inputs["L0_attn_time_mix_receptance"], inputs["L0_attn_time_mix_gate"],
             inputs["L0_attn_gn_weight"], inputs["L0_attn_gn_bias"],
             np.asarray(inputs["L0_attn_time_decay"]).reshape(-1),
             np.asarray(inputs["L0_attn_time_faaaa"]).reshape(-1)]
    for i, v in enumerate(order):
        vecs[:, i * 32:(i + 1) * 32] = np.asarray(v, np.float32).reshape(32, 128).T
    A = np.ascontiguousarray(np.asarray(inputs["L0_ffn_A"], np.float32).reshape(128, D))
    Bm = np.ascontiguousarray(
        np.asarray(inputs["L0_ffn_B"], np.float32).transpose(1, 0, 2).reshape(D, 128))
    bvec = np.asarray(inputs["L0_ffn_b"], np.float32)
    base = {
        "wk": np.asarray(inputs["L0_attn_w_key"], np.float32),
        "wv": np.asarray(inputs["L0_attn_w_value"], np.float32),
        "wr": np.asarray(inputs["L0_attn_w_receptance"], np.float32),
        "wg": np.asarray(inputs["L0_attn_w_gate"], np.float32),
        "wo": np.asarray(inputs["L0_attn_w_output"], np.float32),
        "wf": np.asarray(inputs["L0_ffn_W"], np.float32),
        "amat": A, "bmat": Bm,
        "rwm": np.asarray(inputs["L0_ffn_router_W"], np.float32),
        "rbr": np.asarray(inputs["L0_ffn_router_b"], np.float32).reshape(1, 8),
        "brow0": np.ascontiguousarray(bvec[:T]).reshape(1, T),
        "brow1": np.ascontiguousarray(bvec[T:]).reshape(1, T),
        "vecs": vecs, "cbf": cbf, "cf32": cf32, "crow": crow,
    }
    xall = np.asarray(inputs["L0_attn_hidden"], np.float32)
    maps = []
    for core in range(8):
        m = dict(base)
        m["x0"] = np.ascontiguousarray(xall[core])
        maps.append(m)
    return maps


def kernel(**inputs):
    maps = _prep_inputs(inputs)
    nc = KB().build()
    res = run_bass_kernel_spmd(nc, maps, core_ids=list(range(8)))
    out = np.stack([np.asarray(r["y"], np.float32) for r in res.results], axis=0)
    return out


# version 08
# speedup vs baseline: 1.0992x; 1.0145x over previous
import numpy as np
import ml_dtypes
from contextlib import ExitStack

import concourse.bass as bass
import concourse.mybir as mybir
from concourse.bass_utils import run_bass_kernel_spmd

F32 = mybir.dt.float32
BF16 = mybir.dt.bfloat16
AF = mybir.ActivationFunctionType
ALU = mybir.AluOpType
AX = mybir.AxisListType

P = 128
T = 2048
TS = 512
NTS = 4
NCH = 32
D = 4096
C = 128
NTC = 16
NHC = 32
LNWMIN = -5.0
EPS = 1e-5
QL, QR = 611, 2047
NQ = QR - QL

BF = ml_dtypes.bfloat16


def _host_consts():
    j = np.arange(128)
    cbf = np.zeros((128, 579), dtype=np.float32)
    cbf[:, 0:128] = np.eye(128)
    cbf[:, 128:256] = np.triu(np.ones((128, 128)), 1)
    cbf[:, 256:320] = 1.0
    cbf[:64, 320] = 1.0
    cbf[64:, 321] = 1.0
    cbf[:, 322] = 1.0
    for e in range(8):
        cbf[e, 323 + e * 16:323 + (e + 1) * 16] = 1.0
    cbf[0, 451:515] = 1.0
    cbf[1, 515:579] = 1.0
    cf32 = np.zeros((128, 642), dtype=np.float32)
    cf32[:, 0:128] = np.eye(128)
    cf32[:, 128:256] = j[None, :]
    cf32[:, 256:384] = (j % 64)[None, :]
    cf32[:, 384:512] = (j % 32)[None, :]
    cf32[:, 512:640] = (j % 16)[None, :]
    cf32[:, 640] = 1e-5
    return cbf.astype(BF), cf32, np.ones((1, 512), dtype=BF)


class Sem:
    def __init__(self, es, nc, name):
        self.h = es.enter_context(nc.semaphore(name))
        self.n = 0

    def inc(self, inst, k=1):
        inst.then_inc(self.h, k)
        self.n += k
        return self.n


class KB:
    def __init__(self):
        nc = bass.Bass("TRN2", target_bir_lowering=False)
        self.nc = nc
        self.x0 = nc.dram_tensor("x0", [T, D], F32, kind="ExternalInput").ap()
        self.w = {nm: nc.dram_tensor(nm, [D, D], F32, kind="ExternalInput").ap()
                  for nm in ["wk", "wv", "wr", "wg", "wo", "wf"]}
        self.amat = nc.dram_tensor("amat", [128, D], F32, kind="ExternalInput").ap()
        self.bmat = nc.dram_tensor("bmat", [D, 128], F32, kind="ExternalInput").ap()
        self.rwm = nc.dram_tensor("rwm", [8, D], F32, kind="ExternalInput").ap()
        self.rbr = nc.dram_tensor("rbr", [1, 8], F32, kind="ExternalInput").ap()
        self.brow0 = nc.dram_tensor("brow0", [1, T], F32, kind="ExternalInput").ap()
        self.brow1 = nc.dram_tensor("brow1", [1, T], F32, kind="ExternalInput").ap()
        self.vecs = nc.dram_tensor("vecs", [128, 8 * 32], F32, kind="ExternalInput").ap()
        self.cbf = nc.dram_tensor("cbf", [128, 579], BF16, kind="ExternalInput").ap()
        self.cf32 = nc.dram_tensor("cf32", [128, 642], F32, kind="ExternalInput").ap()
        self.crow = nc.dram_tensor("crow", [1, 512], BF16, kind="ExternalInput").ap()
        self.y = nc.dram_tensor("y", [T, D], F32, kind="ExternalOutput").ap()

        self.xfm = [nc.dram_tensor(f"xfm{i}", [NCH, P, T], BF16, kind="Internal").ap()
                    for i in range(3)]
        self.kvrg = {nm: nc.dram_tensor(nm + "scr", [NCH, P, T], BF16, kind="Internal").ap()
                     for nm in "kvrg"}
        self.obscr = nc.dram_tensor("obscr", [NHC, P, T], BF16, kind="Internal").ap()
        self.wtf = nc.dram_tensor("wtf_scr", [NCH, P, D], BF16, kind="Internal").ap()
        self.atil = nc.dram_tensor("atil_scr", [NCH, P, 128], BF16, kind="Internal").ap()
        self.btil = nc.dram_tensor("btil_scr", [NCH, P, 128], BF16, kind="Internal").ap()
        self.rows_scr = nc.dram_tensor("rows_scr", [4, T], BF16, kind="Internal").ap()

    def snap(self):
        return {k: s.n for k, s in self.S.items()}

    HW_DMA_SEMS = ["qs", "wl", "wl2", "bc", "ob", "kv", "kv2", "wta", "wta2"]
    SW_DMA_SEMS = ["st", "st2", "xc", "xc2", "bt", "bt2", "ws", "ws2", "rbs",
                   "gv", "gv2", "ys", "ys2"]

    def _drain_dmas(self):
        S = self.S
        with self.nc.Block() as blk:
            @blk.sync
            def _(sync):
                for nm in self.HW_DMA_SEMS:
                    if S[nm].n:
                        sync.wait_ge(S[nm].h, S[nm].n)

            @blk.gpsimd
            def _(g_):
                for nm in self.SW_DMA_SEMS:
                    if S[nm].n:
                        g_.wait_ge(S[nm].h, S[nm].n)

    def dinc(self, name, idx, inst):
        self.S[name if idx % 2 == 0 else name + "2"].inc(inst, 16)

    def dwait(self, eng, name, idx, b):
        nm = name if idx % 2 == 0 else name + "2"
        eng.wait_ge(self.S[nm].h, b[nm] + 16 * (idx // 2 + 1))

    def uname(self, base):
        self._uc = getattr(self, "_uc", 0) + 1
        return f"{base}_{self._uc}"

    def build(self, upto=None, debug=()):
        self._upto = upto
        self._debug = tuple(debug)
        nc = self.nc
        if debug:
            self.dbg_out = {}
            scr = {"xfm0": self.xfm[0], "xfm1": self.xfm[1], "xfm2": self.xfm[2],
                   "kscr": self.kvrg["k"], "vscr": self.kvrg["v"],
                   "rscr": self.kvrg["r"], "gscr": self.kvrg["g"],
                   "obscr": self.obscr, "rows": self.rows_scr,
                   "atil": self.atil, "btil": self.btil, "wtf": self.wtf}
            for nm in debug:
                t = scr[nm]
                self.dbg_out[nm] = nc.dram_tensor(
                    "dbg_" + nm, list(t.shape), BF16, kind="ExternalOutput").ap()
        es = ExitStack()
        with es:
            sb = lambda name, shape, dt: es.enter_context(nc.sbuf_tensor(name, shape, dt))
            self.ACT_A = sb("ACT_A", [P, NCH * T], BF16)
            self.RSTD = sb("RSTD", [P, T], BF16)
            self.MURS = sb("MURS", [P, T], BF16)
            self.TSC = sb("TSC", [P, T], BF16)
            self.VECS = sb("VECS", [P, 8 * 32], F32)
            self.CBF = sb("CBF", [P, 579], BF16)
            self.CF = sb("CF", [P, 642], F32)
            self.CROW = sb("CROW", [1, 512], BF16)
            self.RWT = sb("RWT", [P, 256], BF16)
            self.BRB0 = sb("BRB0", [1, T], BF16)
            self.BRB1 = sb("BRB1", [1, T], BF16)
            self.RBR8 = sb("RBR8", [1, 8], F32)
            self.RTBC = sb("RTBC", [P, 1], F32)
            self.QSUM = sb("QSUM", [P, NCH], F32)
            self.QBF = sb("QBF", [P, NCH], BF16)
            self.SROW = sb("SROW", [1, 32], F32)
            self.PBF = sb("PBF", [1, 8], BF16)
            self.PCOL = sb("PCOL", [8, 1], BF16)
            self.RMUs = [sb(f"RMU{i}", [2, TS], F32) for i in range(2)]
            self.RSQs = [sb(f"RSQ{i}", [2, TS], F32) for i in range(2)]
            self.RVAs = [sb(f"RVA{i}", [2, TS], F32) for i in range(2)]
            self.RSBs = [sb(f"RSB{i}", [2, TS], BF16) for i in range(2)]
            self.RMBs = [sb(f"RMB{i}", [2, TS], BF16) for i in range(2)]
            self.RMU, self.RSQ, self.RVA = self.RMUs[0], self.RSQs[0], self.RVAs[0]
            self.RSB, self.RMB = self.RSBs[0], self.RMBs[0]

            self.ident = self.CBF[:, 0:128]
            self.masku = self.CBF[:, 128:256]
            self.ones64 = self.CBF[:, 256:320]
            self.hm2 = self.CBF[:, 320:322]
            self.onescol = self.CBF[:, 322:323]
            self.exp8 = self.CBF[0:8, 323:451]
            self.hm2T = self.CBF[0:2, 451:579]
            self.identf = self.CF[:, 0:128]
            self.iota = {128: self.CF[:, 128:256], 64: self.CF[:, 256:384],
                         32: self.CF[:, 384:512], 16: self.CF[:, 512:640]}
            self.tm = {pn: self.VECS[:, i * 32:(i + 1) * 32]
                       for i, pn in enumerate("kvrg")}
            self.gnw = self.VECS[:, 4 * 32:5 * 32]
            self.gnb = self.VECS[:, 5 * 32:6 * 32]
            self.tdv = self.VECS[:, 6 * 32:7 * 32]
            self.faaaa = self.VECS[:, 7 * 32:8 * 32]

            self.PSB = [es.enter_context(nc.psum_tensor(f"PS{i}", [P, 512], F32))
                        for i in range(8)]
            self.S = {nm: Sem(es, nc, "s_" + nm) for nm in
                      ["qs", "wl", "cv", "tpg", "sst", "wt", "mm", "ev", "xc",
                       "st", "bt", "ws", "xt", "yr", "ys", "pp", "tt", "ae",
                       "am", "om", "uf", "sbm", "kv", "pmb", "cd", "ca", "cp",
                       "cg", "ln", "gn", "aux", "ax2", "bc", "ob",
                       "wl2", "xc2", "st2", "bt2", "ys2", "ws2", "kv2",
                       "wta", "wta2", "rbs", "gv", "gv2", "dp"]}

            seq = [
                ("consts", self.p0_consts),
                ("entry", self.p0_entry),
                ("small", self.p0_small),
                ("lnq1", lambda: self.p1_lnq(self.xfm[0])),
                ("router1", self.p1_router),
                ("tsc1", self.p1_tsc),
                ("ffn1", lambda: self.p2_main("ffn", self.w["wf"],
                                              xres=self.xfm[0], xdst=self.xfm[1],
                                              save_wt=True)),
                ("stats1", lambda: self.p_stats(self.xfm[1])),
                ("bcast1", self._bcast_rows),
            ]
            for pn in "kvrg":
                seq.append(("mix_" + pn,
                            (lambda p: lambda: self.p4_mix(p, self.xfm[1]))(pn)))
                seq.append(("proj_" + pn,
                            (lambda p: lambda: self.p2_main(
                                "proj", self.w["w" + p], dst_scr=self.kvrg[p],
                                silu=(p == "g")))(pn)))
            seq += [
                ("wkv", self.p5a_wkv),
                ("gn", self.p5b_gn),
                ("outproj", lambda: self.p2_main("outproj", self.w["wo"],
                                                 xres=self.xfm[1],
                                                 xdst=self.xfm[2])),
                ("stats2", lambda: self.p_stats(self.xfm[2])),
                ("lnq2", lambda: self.p1_lnq(self.xfm[2])),
                ("router2", self.p1_router),
                ("tsc2", self.p1_tsc),
                ("ffn2", lambda: self.p2_main("ffn2", None, xres=self.xfm[2])),
            ]
            for name, fn in seq:
                fn()
                self._drain_dmas()
                if upto is not None and name == upto:
                    break
            if self._debug:
                self._emit_debug()
            self.p_final()
        return nc

    def _emit_debug(self):
        nc, S = self.nc, self.S
        scr = {"xfm0": self.xfm[0], "xfm1": self.xfm[1], "xfm2": self.xfm[2],
               "kscr": self.kvrg["k"], "vscr": self.kvrg["v"],
               "rscr": self.kvrg["r"], "gscr": self.kvrg["g"],
               "obscr": self.obscr, "rows": self.rows_scr,
               "atil": self.atil, "btil": self.btil, "wtf": self.wtf}
        with nc.Block() as blk:
            @blk.sync
            def _(sync):
                for snm in ["qs", "wl", "wl2", "st", "st2", "xc", "xc2", "bt",
                            "bt2", "ws", "ws2", "ys2", "kv", "kv2", "gv", "gv2",
                            "bc", "ob", "rbs", "wta", "wta2"]:
                    if S[snm].n:
                        sync.wait_ge(S[snm].h, S[snm].n)
                for nm in self._debug:
                    srcap = scr[nm]
                    dstap = self.dbg_out[nm]
                    if nm == "rows":
                        srcap = srcap[0:2, :]
                        dstap = dstap[0:2, :]
                    S["qs"].inc(sync.dma_start(dstap, srcap), 16)
                sync.wait_ge(S["qs"].h, S["qs"].n)

    def p0_consts(self):
        nc, S = self.nc, self.S
        with nc.Block() as blk:
            @blk.sync
            def _(sync):
                for dst, src in [(self.CBF[:], self.cbf), (self.CF[:], self.cf32),
                                 (self.CROW[:], self.crow), (self.VECS[:], self.vecs),
                                 (self.RBR8[:], self.rbr)]:
                    S["qs"].inc(sync.dma_start(dst, src), 16)
                sync.wait_ge(S["qs"].h, S["qs"].n)

    def p0_entry(self):
        nc, S, PSB = self.nc, self.S, self.PSB
        NT = 16
        with ExitStack() as ps:
            xtf = [ps.enter_context(nc.sbuf_tensor(self.uname(f"e_xtf{i}"), [P, 2048], F32))
                   for i in range(2)]
            xtb = ps.enter_context(nc.sbuf_tensor(self.uname("e_xtb"), [P, D], BF16))
            stg = [ps.enter_context(nc.sbuf_tensor(self.uname(f"e_stg{i}"), [P, 512], BF16))
                   for i in range(2)]
            sqg = [ps.enter_context(nc.sbuf_tensor(self.uname(f"e_sqg{i}"), [P, 512], BF16))
                   for i in range(2)]
            b = self.snap()
            with nc.Block() as blk:
                @blk.sync
                def _(sync):
                    for i in range(2 * NT):
                        if i >= 2:
                            sync.wait_ge(S["cv"].h, b["cv"] + i - 1)
                        tt, pc = divmod(i, 2)
                        self.dinc("wl", i, sync.dma_start(
                            xtf[i % 2][:], self.x0[tt * 128:(tt + 1) * 128,
                                                   pc * 2048:(pc + 1) * 2048]))

                @blk.scalar
                def _(act):
                    for tt in range(NT):
                        if tt >= 1:
                            act.wait_ge(S["tpg"].h, b["tpg"] + 8 * tt)
                        for pc in range(2):
                            i = tt * 2 + pc
                            self.dwait(act, "wl", i, b)
                            S["cv"].inc(act.activation(
                                xtb[:, pc * 2048:(pc + 1) * 2048],
                                xtf[i % 2][:], AF.Copy))
                        for g in range(8):
                            G = tt * 8 + g
                            act.wait_ge(S["wt"].h, b["wt"] + G + 1)
                            if G >= 2:
                                act.wait_ge(S["sst"].h, b["sst"] + G - 1)
                            S["aux"].inc(act.activation(
                                sqg[G % 2][:], stg[G % 2][:], AF.Square))
                        if tt % 4 == 3:
                            ts = tt // 4
                            act.wait_ge(S["sst"].h, b["sst"] + 8 * (tt + 1))
                            if ts >= 1:
                                act.wait_ge(S["cg"].h, b["cg"] + ts)
                                act.wait_ge(S["rbs"].h, b["rbs"] + 16 * 2 * ts)
                            S["ca"].inc(act.activation(self.RMU[0:1, :],
                                                       PSB[2][0:1, :], AF.Copy,
                                                       scale=1.0 / D))
                            act.wait_ge(S["ca"].h, S["ca"].n)
                            S["ca"].inc(act.activation(self.RSQ[0:1, :],
                                                       self.RMU[0:1, :], AF.Square))
                            act.wait_ge(S["cd"].h, b["cd"] + 1 * (ts + 1))
                            S["ca"].inc(act.activation(self.RVA[0:1, :],
                                                       self.RVA[0:1, :], AF.Ln,
                                                       bias=self.CF[0:1, 640:641]))
                            act.wait_ge(S["ca"].h, S["ca"].n)
                            S["ca"].inc(act.activation(self.RSB[0:1, :],
                                                       self.RVA[0:1, :], AF.Exp,
                                                       scale=-0.5))

                @blk.tensor
                def _(te):
                    for tt in range(NT):
                        te.wait_ge(S["cv"].h, b["cv"] + 2 * tt + 2)
                        if tt % 4 == 0 and tt > 0:
                            te.wait_ge(S["ca"].h, b["ca"] + 4 * (tt // 4))
                            te.wait_ge(S["cd"].h, b["cd"] + (tt // 4))
                        for g in range(8):
                            G = tt * 8 + g
                            if G >= 2:
                                te.wait_ge(S["wt"].h, b["wt"] + G - 1)
                            for m in range(4):
                                jj = g * 4 + m
                                inst = te.matmul(
                                    PSB[G % 2][:, m * 128:(m + 1) * 128],
                                    xtb[:, jj * 128:(jj + 1) * 128], self.ident,
                                    start=True, stop=True)
                            S["tpg"].inc(inst)
                            if g >= 1:
                                self._entry_stats(te, tt, g - 1, b, stg, sqg)
                        self._entry_stats(te, tt, 7, b, stg, sqg)

                @blk.vector
                def _(ve):
                    for tt in range(NT):
                        for g in range(8):
                            G = tt * 8 + g
                            ve.wait_ge(S["tpg"].h, b["tpg"] + G + 1)
                            if G >= 2:
                                ve.wait_ge(S["aux"].h, b["aux"] + G - 1)
                                self.dwait(ve, "st", G - 2, b)
                            S["wt"].inc(ve.tensor_copy(stg[G % 2][:], PSB[G % 2][:]))
                        if tt % 4 == 3:
                            ts = tt // 4
                            ve.wait_ge(S["ca"].h, b["ca"] + 4 * ts + 2)
                            S["cd"].inc(ve.scalar_tensor_tensor(
                                self.RVA[0:1, :], PSB[3][0:1, :], 1.0 / D,
                                self.RSQ[0:1, :], ALU.mult, ALU.subtract))
                            ve.wait_ge(S["ca"].h, b["ca"] + 4 * ts + 4)
                            S["cg"].inc(ve.tensor_mul(self.RMB[0:1, :],
                                                      self.RMU[0:1, :],
                                                      self.RSB[0:1, :]))

                @blk.gpsimd
                def _(g_):
                    for G in range(8 * NT):
                        g_.wait_ge(S["wt"].h, b["wt"] + G + 1)
                        tt, g = divmod(G, 8)
                        dst = self.xfm[0][4 * g:4 * g + 4, :,
                                          tt * 128:(tt + 1) * 128].transpose([1, 0, 2])
                        self.dinc("st", G, g_.dma_start(dst, stg[G % 2][:]))
                        if tt % 4 == 3 and g == 7:
                            ts = tt // 4
                            g_.wait_ge(S["cg"].h, b["cg"] + ts + 1)
                            S["rbs"].inc(g_.dma_start(
                                self.rows_scr[0:1, ts * TS:(ts + 1) * TS],
                                self.RSB[0:1, :]), 16)
                            S["rbs"].inc(g_.dma_start(
                                self.rows_scr[1:2, ts * TS:(ts + 1) * TS],
                                self.RMB[0:1, :]), 16)
                    g_.wait_ge(S["st"].h, S["st"].n)
                    g_.wait_ge(S["st2"].h, S["st2"].n)

    def _entry_stats(self, te, tt, g, b, stg, sqg):
        S, PSB = self.S, self.PSB
        G = tt * 8 + g
        te.wait_ge(S["wt"].h, b["wt"] + G + 1)
        te.wait_ge(S["aux"].h, b["aux"] + G + 1)
        col = (tt % 4) * 128
        first = (g == 0)
        last = (g == 7)
        for m in range(4):
            te.matmul(PSB[2][0:1, col:col + 128], self.onescol,
                      stg[G % 2][:, m * 128:(m + 1) * 128],
                      start=(first and m == 0 and tt % 4 == 0),
                      stop=(last and m == 3 and tt % 4 == 3))
        for m in range(4):
            inst = te.matmul(PSB[3][0:1, col:col + 128], self.onescol,
                             sqg[G % 2][:, m * 128:(m + 1) * 128],
                             start=(first and m == 0 and tt % 4 == 0),
                             stop=(last and m == 3 and tt % 4 == 3))
        S["sst"].inc(inst)

    def p0_small(self):
        nc, S, PSB = self.nc, self.S, self.PSB
        with ExitStack() as ps:
            stf = ps.enter_context(nc.sbuf_tensor(self.uname("s_stf"), [P, 1024], F32))
            stb = ps.enter_context(nc.sbuf_tensor(self.uname("s_stb"), [P, 1024], BF16))
            otb = ps.enter_context(nc.sbuf_tensor(self.uname("s_otb"), [P, 512], BF16))
            brf = ps.enter_context(nc.sbuf_tensor(self.uname("s_brf"), [1, T], F32))
            for strip in range(4):
                self._small_strip(self.amat[:, strip * 1024:(strip + 1) * 1024],
                                  stf, stb, None,
                                  self.atil[strip * 8:(strip + 1) * 8, :, :], 8)
            for strip in range(8):
                src = self.bmat[strip * 512:(strip + 1) * 512, :].rearrange(
                    "(a p) e -> p a e", p=128)
                self._small_strip(src, stf, stb, otb,
                                  self.btil[strip * 4:(strip + 1) * 4, :, :], 4)
            for s4 in range(4):
                with nc.Block() as blk:
                    @blk.sync
                    def _(sync):
                        S["qs"].inc(sync.dma_start(
                            stf[0:8, :], self.rwm[:, s4 * 1024:(s4 + 1) * 1024]), 16)
                        sync.wait_ge(S["qs"].h, S["qs"].n)
                with nc.Block() as blk:
                    @blk.scalar
                    def _(act):
                        act.activation(stb[0:8, :], stf[0:8, :], AF.Copy)
                with nc.Block() as blk:
                    @blk.tensor
                    def _(te):
                        for m in range(8):
                            te.matmul(PSB[0][:, m * 64:m * 64 + 8],
                                      stb[0:8, m * 128:(m + 1) * 128],
                                      self.ident[0:8, 0:8], start=True, stop=True)
                with nc.Block() as blk:
                    @blk.vector
                    def _(ve):
                        for m in range(8):
                            kc = s4 * 8 + m
                            ve.tensor_copy(self.RWT[:, kc * 8:(kc + 1) * 8],
                                           PSB[0][:, m * 64:m * 64 + 8])
            with nc.Block() as blk:
                @blk.sync
                def _(sync):
                    S["qs"].inc(sync.dma_start(brf[:], self.brow0), 16)
                    sync.wait_ge(S["qs"].h, S["qs"].n)
            with nc.Block() as blk:
                @blk.scalar
                def _(act):
                    act.activation(self.BRB0[:], brf[:], AF.Copy)
            with nc.Block() as blk:
                @blk.sync
                def _(sync):
                    S["qs"].inc(sync.dma_start(brf[:], self.brow1), 16)
                    sync.wait_ge(S["qs"].h, S["qs"].n)
            with nc.Block() as blk:
                @blk.scalar
                def _(act):
                    act.activation(self.BRB1[:], brf[:], AF.Copy)

    def _small_strip(self, src, stf, stb, otb, dst_scr, ntile):
        nc, S, PSB = self.nc, self.S, self.PSB
        w = ntile * 128
        out = stb if otb is None else otb
        with nc.Block() as blk:
            @blk.sync
            def _(sync):
                S["qs"].inc(sync.dma_start(stf[:, 0:w], src), 16)
                sync.wait_ge(S["qs"].h, S["qs"].n)
        with nc.Block() as blk:
            @blk.scalar
            def _(act):
                act.activation(stb[:, 0:w], stf[:, 0:w], AF.Copy)
        with nc.Block() as blk:
            @blk.tensor
            def _(te):
                for m in range(ntile):
                    te.matmul(PSB[m // 4 % 2][:, (m % 4) * 128:(m % 4 + 1) * 128],
                              stb[:, m * 128:(m + 1) * 128], self.ident,
                              start=True, stop=True)
        with nc.Block() as blk:
            @blk.vector
            def _(ve):
                for q in range((ntile + 3) // 4):
                    n = min(4, ntile - q * 4) * 128
                    ve.tensor_copy(out[:, q * 512:q * 512 + n], PSB[q % 2][:, 0:n])
        with nc.Block() as blk:
            @blk.sync
            def _(sync):
                S["qs"].inc(sync.dma_start(
                    dst_scr.transpose([1, 0, 2]),
                    out[:, 0:w].rearrange("p (a c) -> p a c", c=128)), 16)
                sync.wait_ge(S["qs"].h, S["qs"].n)

    def _bcast_rows(self):
        nc, S = self.nc, self.S
        with nc.Block() as blk:
            @blk.sync
            def _(sync):
                if S["rbs"].n:
                    sync.wait_ge(S["rbs"].h, S["rbs"].n)
                if S["qs"].n:
                    sync.wait_ge(S["qs"].h, S["qs"].n)
                S["bc"].inc(sync.dma_start(
                    self.RSTD[:], self.rows_scr[0:1, :].broadcast_to([P, T])), 16)
                S["bc"].inc(sync.dma_start(
                    self.MURS[:], self.rows_scr[1:2, :].broadcast_to([P, T])), 16)
                sync.wait_ge(S["bc"].h, S["bc"].n)

    def p1_lnq(self, xsrc):
        nc, S = self.nc, self.S
        self._bcast_rows()
        with ExitStack() as ps:
            xst = [ps.enter_context(nc.sbuf_tensor(self.uname(f"l_x{i}"), [P, T], BF16))
                   for i in range(2)]
            b = self.snap()
            with nc.Block() as blk:
                @blk.gpsimd
                def _(g_):
                    for ch in range(NCH):
                        if ch >= 2:
                            g_.wait_ge(S["cd"].h, b["cd"] + 3 * (ch - 1))
                        self.dinc("xc", ch, g_.dma_start(xst[ch % 2][:],
                                                         xsrc[ch, :, :]))

                @blk.vector
                def _(ve):
                    for ch in range(NCH):
                        a = self.ACT_A[:, ch * T:(ch + 1) * T]
                        self.dwait(ve, "xc", ch, b)
                        S["cd"].inc(ve.tensor_mul(a, xst[ch % 2][:], self.RSTD[:]))
                        ve.wait_ge(S["cd"].h, S["cd"].n)
                        S["cd"].inc(ve.tensor_sub(a, a, self.MURS[:]))
                        ve.wait_ge(S["cd"].h, S["cd"].n)
                        S["cd"].inc(ve.reduce_sum(
                            self.QSUM[:, ch:ch + 1],
                            self.ACT_A[:, ch * T + QL:ch * T + QR], axis=AX.X))

    def p1_router(self):
        nc, S, PSB = self.nc, self.S, self.PSB
        with nc.Block() as blk:
            @blk.scalar
            def _(act):
                act.activation(self.QBF[:], self.QSUM[:], AF.Copy)
        with nc.Block() as blk:
            @blk.tensor
            def _(te):
                for kc in range(NCH):
                    te.matmul(PSB[0][0:1, 0:8], self.QBF[:, kc:kc + 1],
                              self.RWT[:, kc * 8:(kc + 1) * 8],
                              start=(kc == 0), stop=(kc == NCH - 1))
        with nc.Block() as blk:
            @blk.vector
            def _(ve):
                S["cd"].inc(ve.scalar_tensor_tensor(
                    self.SROW[0:1, 0:8], PSB[0][0:1, 0:8], 1.0 / NQ,
                    self.RBR8[:], ALU.mult, ALU.add))
                ve.wait_ge(S["cd"].h, S["cd"].n)
                S["cd"].inc(ve.reduce_max(self.SROW[0:1, 8:9],
                                          self.SROW[0:1, 0:8], axis=AX.X))
                ve.wait_ge(S["cd"].h, S["cd"].n)
                S["cd"].inc(ve.tensor_scalar_mul(self.SROW[0:1, 9:10],
                                                 self.SROW[0:1, 8:9], -1.0))
        with nc.Block() as blk:
            @blk.scalar
            def _(act):
                act.activation(self.SROW[0:1, 16:24], self.SROW[0:1, 0:8],
                               AF.Exp, bias=self.SROW[0:1, 9:10])
        with nc.Block() as blk:
            @blk.vector
            def _(ve):
                S["cd"].inc(ve.reduce_sum(self.SROW[0:1, 10:11],
                                          self.SROW[0:1, 16:24], axis=AX.X))
                ve.wait_ge(S["cd"].h, S["cd"].n)
                S["cd"].inc(ve.reciprocal(self.SROW[0:1, 11:12],
                                          self.SROW[0:1, 10:11]))
                ve.wait_ge(S["cd"].h, S["cd"].n)
                S["cd"].inc(ve.tensor_scalar(self.SROW[0:1, 24:32],
                                             self.SROW[0:1, 16:24],
                                             self.SROW[0:1, 11:12], 2.0,
                                             ALU.mult, ALU.mult))
        with nc.Block() as blk:
            @blk.scalar
            def _(act):
                act.activation(self.PBF[:], self.SROW[0:1, 24:32], AF.Copy)
        with nc.Block() as blk:
            @blk.tensor
            def _(te):
                te.matmul(PSB[1][0:8, 0:1], self.PBF[:], self.CROW[0:1, 0:1],
                          start=True, stop=True)
        with nc.Block() as blk:
            @blk.vector
            def _(ve):
                ve.tensor_copy(self.PCOL[:], PSB[1][0:8, 0:1])
        with nc.Block() as blk:
            @blk.tensor
            def _(te):
                te.matmul(PSB[2][:, 0:1], self.exp8, self.PCOL[:],
                          start=True, stop=True)
        with nc.Block() as blk:
            @blk.vector
            def _(ve):
                ve.tensor_copy(self.RTBC[:], PSB[2][:, 0:1])

    def p1_tsc(self):
        nc, S, PSB = self.nc, self.S, self.PSB
        with ExitStack() as ps:
            atl = ps.enter_context(nc.sbuf_tensor(self.uname("t_atl"), [P, NCH * 128], BF16))
            b = self.snap()
            with nc.Block() as blk:
                @blk.sync
                def _(sync):
                    S["qs"].inc(sync.dma_start(
                        atl[:], self.atil[:, :, :].transpose([1, 0, 2])), 16)
                    sync.wait_ge(S["qs"].h, S["qs"].n)
            with nc.Block() as blk:
                @blk.tensor
                def _(te):
                    for ts in range(NTS):
                        if ts >= 2:
                            te.wait_ge(S["ev"].h, b["ev"] + ts - 1)
                        for kc in range(NCH):
                            inst = te.matmul(
                                PSB[3 + ts % 2][:],
                                atl[:, kc * 128:(kc + 1) * 128],
                                self.ACT_A[:, kc * T + ts * TS:kc * T + (ts + 1) * TS],
                                start=(kc == 0), stop=(kc == NCH - 1))
                        S["mm"].inc(inst)

                @blk.vector
                def _(ve):
                    for ts in range(NTS):
                        ve.wait_ge(S["mm"].h, b["mm"] + ts + 1)
                        S["ev"].inc(ve.tensor_scalar_mul(
                            self.TSC[:, ts * TS:(ts + 1) * TS],
                            PSB[3 + ts % 2][:], self.RTBC[:, 0:1]))

    def p_stats(self, xsrc):
        nc, S, PSB = self.nc, self.S, self.PSB
        with ExitStack() as ps:
            xst = [ps.enter_context(nc.sbuf_tensor(self.uname(f"st_x{i}"), [P, T], BF16))
                   for i in range(2)]
            sqs = [ps.enter_context(nc.sbuf_tensor(self.uname(f"st_q{i}"), [P, T], BF16))
                   for i in range(2)]
            b = self.snap()
            with nc.Block() as blk:
                @blk.gpsimd
                def _(g_):
                    for ch in range(NCH):
                        if ch >= 2:
                            g_.wait_ge(S["sst"].h, b["sst"] + ch - 1)
                        self.dinc("xc", ch, g_.dma_start(xst[ch % 2][:], xsrc[ch, :, :]))

                @blk.scalar
                def _(act):
                    for ch in range(NCH):
                        self.dwait(act, "xc", ch, b)
                        if ch >= 2:
                            act.wait_ge(S["sst"].h, b["sst"] + ch - 1)
                        S["aux"].inc(act.activation(sqs[ch % 2][:], xst[ch % 2][:],
                                                    AF.Square))
                    for ts in range(NTS):
                        act.wait_ge(S["sst"].h, b["sst"] + NCH)
                        if ts >= 1:
                            act.wait_ge(S["cd"].h, b["cd"] + 2 * ts)
                            act.wait_ge(S["qs"].h, b["qs"] + 16 * 2 * ts)
                        S["ca"].inc(act.activation(self.RMU[0:1, :],
                                                   PSB[ts][0:1, :], AF.Copy,
                                                   scale=1.0 / D))
                        act.wait_ge(S["ca"].h, S["ca"].n)
                        S["ca"].inc(act.activation(self.RSQ[0:1, :],
                                                   self.RMU[0:1, :], AF.Square))
                        act.wait_ge(S["cd"].h, b["cd"] + 2 * ts + 1)
                        S["ca"].inc(act.activation(self.RVA[0:1, :],
                                                   self.RVA[0:1, :], AF.Ln,
                                                   bias=self.CF[0:1, 640:641]))
                        act.wait_ge(S["ca"].h, S["ca"].n)
                        S["ca"].inc(act.activation(self.RSB[0:1, :],
                                                   self.RVA[0:1, :], AF.Exp,
                                                   scale=-0.5))

                @blk.tensor
                def _(te):
                    for ch in range(NCH):
                        self.dwait(te, "xc", ch, b)
                        te.wait_ge(S["aux"].h, b["aux"] + ch + 1)
                        for ts in range(NTS):
                            te.matmul(PSB[ts][0:1, :], self.onescol,
                                      xst[ch % 2][:, ts * TS:(ts + 1) * TS],
                                      start=(ch == 0), stop=(ch == NCH - 1))
                        for ts in range(NTS):
                            inst = te.matmul(PSB[4 + ts][0:1, :], self.onescol,
                                             sqs[ch % 2][:, ts * TS:(ts + 1) * TS],
                                             start=(ch == 0), stop=(ch == NCH - 1))
                        S["sst"].inc(inst)

                @blk.vector
                def _(ve):
                    for ts in range(NTS):
                        ve.wait_ge(S["ca"].h, b["ca"] + 4 * ts + 2)
                        S["cd"].inc(ve.scalar_tensor_tensor(
                            self.RVA[0:1, :], PSB[4 + ts][0:1, :], 1.0 / D,
                            self.RSQ[0:1, :], ALU.mult, ALU.subtract))
                        ve.wait_ge(S["ca"].h, b["ca"] + 4 * ts + 4)
                        S["cd"].inc(ve.tensor_mul(self.RMB[0:1, :],
                                                  self.RMU[0:1, :],
                                                  self.RSB[0:1, :]))

                @blk.sync
                def _(sync):
                    for ts in range(NTS):
                        sync.wait_ge(S["cd"].h, b["cd"] + 2 * ts + 2)
                        S["qs"].inc(sync.dma_start(
                            self.rows_scr[0:1, ts * TS:(ts + 1) * TS],
                            self.RSB[0:1, :]), 16)
                        S["qs"].inc(sync.dma_start(
                            self.rows_scr[1:2, ts * TS:(ts + 1) * TS],
                            self.RMB[0:1, :]), 16)
                    sync.wait_ge(S["qs"].h, S["qs"].n)

    def p4_mix(self, pn, xsrc):
        nc, S = self.nc, self.S
        with ExitStack() as ps:
            xst = [ps.enter_context(nc.sbuf_tensor(self.uname(f"m_x{i}"), [P, T], BF16))
                   for i in range(2)]
            lnst = [ps.enter_context(nc.sbuf_tensor(self.uname(f"m_l{i}"), [P, T], BF16))
                    for i in range(2)]
            mab = [ps.enter_context(nc.sbuf_tensor(self.uname(f"m_a{i}"), [P, T], BF16))
                   for i in range(2)]
            mbb = [ps.enter_context(nc.sbuf_tensor(self.uname(f"m_b{i}"), [P, T], BF16))
                   for i in range(2)]
            tmm = ps.enter_context(nc.sbuf_tensor(self.uname("m_tm"), [P, 32], F32))
            b = self.snap()
            tmc = self.tm[pn]
            with nc.Block() as blk:
                @blk.gpsimd
                def _(g_):
                    for ch in range(NCH):
                        if ch >= 2:
                            g_.wait_ge(S["cd"].h, b["cd"] + 3 * (ch - 2) + 2)
                        self.dinc("xc", ch, g_.dma_start(xst[ch % 2][:],
                                                         xsrc[ch, :, :]))

                @blk.vector
                def _(ve):
                    S["cd"].inc(ve.tensor_scalar(tmm[:], tmc[:], -1.0, 1.0,
                                                 ALU.mult, ALU.add))

                    def combine(ch2):
                        a = self.ACT_A[:, ch2 * T:(ch2 + 1) * T]
                        ve.wait_ge(S["ca"].h, b["ca"] + ch2 + 1)
                        ma, mb = mab[ch2 % 2], mbb[ch2 % 2]
                        ve.tensor_add(a[:, 1:T], ma[:, 1:T], mb[:, 0:T - 1])
                        S["cd"].inc(ve.tensor_copy(a[:, 0:1], ma[:, 0:1]))

                    for ch in range(NCH):
                        ln = lnst[ch % 2]
                        self.dwait(ve, "xc", ch, b)
                        if ch >= 2:
                            ve.wait_ge(S["ca"].h, b["ca"] + ch - 1)
                        S["cd"].inc(ve.tensor_mul(ln[:], xst[ch % 2][:],
                                                  self.RSTD[:]))
                        ve.wait_ge(S["cd"].h, S["cd"].n)
                        S["cd"].inc(ve.tensor_sub(ln[:], ln[:], self.MURS[:]))
                        if ch >= 1:
                            combine(ch - 1)
                    combine(NCH - 1)

                @blk.scalar
                def _(act):
                    for ch in range(NCH):
                        act.wait_ge(S["cd"].h, b["cd"] + 3 * ch + 3)
                        if ch >= 2:
                            act.wait_ge(S["cd"].h, b["cd"] + 3 * (ch - 1) + 1)
                        act.activation(mab[ch % 2][:], lnst[ch % 2][:], AF.Copy,
                                       scale=tmc[:, ch:ch + 1])
                        S["ca"].inc(act.activation(mbb[ch % 2][:], lnst[ch % 2][:],
                                                   AF.Copy, scale=tmm[:, ch:ch + 1]))

    def p2_main(self, mode, Wsrc, xres=None, xdst=None, dst_scr=None,
                silu=False, save_wt=False):
        nc, S, PSB = self.nc, self.S, self.PSB
        lora = mode in ("ffn", "ffn2")
        from_scr = (mode == "ffn2")
        with ExitStack() as ps:
            wT = [ps.enter_context(nc.sbuf_tensor(self.uname(f"w_t{i}"), [P, D], BF16))
                  for i in range(2)]
            if not from_scr:
                wfp = [ps.enter_context(nc.sbuf_tensor(self.uname(f"w_f{i}"), [P, 1024], F32))
                       for i in range(2)]
                wbf = ps.enter_context(nc.sbuf_tensor(self.uname("w_b"), [P, D], BF16))
            if lora:
                btl = [ps.enter_context(nc.sbuf_tensor(self.uname(f"w_bt{i}"), [P, 128], BF16))
                       for i in range(2)]
            if mode != "proj":
                xcb = [ps.enter_context(nc.sbuf_tensor(self.uname(f"w_xc{i}"), [P, TS], BF16))
                       for i in range(2)]
            if mode == "ffn2":
                fin = [ps.enter_context(nc.sbuf_tensor(self.uname(f"w_fi{i}"), [P, TS], F32))
                       for i in range(2)]
                yro = [ps.enter_context(nc.sbuf_tensor(self.uname(f"w_yr{i}"), [P, TS], F32))
                       for i in range(2)]
            else:
                ob = [ps.enter_context(nc.sbuf_tensor(self.uname(f"w_ob{i}"), [P, TS], BF16))
                      for i in range(2)]
            b = self.snap()

            with nc.Block() as blk:
                @blk.sync
                def _(sync):
                    if from_scr:
                        for och in range(NCH):
                            if och >= 2:
                                sync.wait_ge(S["mm"].h, b["mm"] + 4 * (och - 1))
                            self.dinc("wta", och, sync.dma_start(
                                wT[och % 2][:], self.wtf[och, :, :]))
                    else:
                        for och in range(NCH):
                            for pc in range(4):
                                i = och * 4 + pc
                                if i >= 2:
                                    sync.wait_ge(S["cv"].h, b["cv"] + i - 1)
                                self.dinc("wl", i, sync.dma_start(
                                    wfp[i % 2][:],
                                    Wsrc[och * 128:(och + 1) * 128,
                                         pc * 1024:(pc + 1) * 1024]))

                if not from_scr:
                    @blk.scalar
                    def _(act):
                        for och in range(NCH):
                            if och >= 1:
                                act.wait_ge(S["tpg"].h, b["tpg"] + 8 * och)
                            for pc in range(4):
                                i = och * 4 + pc
                                self.dwait(act, "wl", i, b)
                                S["cv"].inc(act.activation(
                                    wbf[:, pc * 1024:(pc + 1) * 1024],
                                    wfp[i % 2][:], AF.Copy))
                            if silu and och >= 1:
                                self._silu_evac(act, och - 1, b, ob)
                        if silu:
                            self._silu_evac(act, NCH - 1, b, ob)

                @blk.tensor
                def _(te):
                    def emit_tp(och):
                        te.wait_ge(S["cv"].h, b["cv"] + 4 * och + 4)
                        for g in range(8):
                            G = och * 8 + g
                            if G >= 2:
                                te.wait_ge(S["wt"].h, b["wt"] + G - 1)
                            for m in range(4):
                                jj = g * 4 + m
                                inst = te.matmul(
                                    PSB[2 + G % 2][:, m * 128:(m + 1) * 128],
                                    wbf[:, jj * 128:(jj + 1) * 128], self.ident,
                                    start=True, stop=True)
                            S["tpg"].inc(inst)

                    def emit_mm(och):
                        if not from_scr:
                            te.wait_ge(S["wt"].h, b["wt"] + 8 * (och + 1))
                        else:
                            self.dwait(te, "wta", och, b)
                        if lora:
                            self.dwait(te, "bt", och, b)
                        for ts in range(NTS):
                            I = och * 4 + ts
                            if I >= 2:
                                te.wait_ge(S["ev"].h, b["ev"] + I - 1)
                            for kc in range(NCH):
                                inst = te.matmul(
                                    PSB[I % 2][:],
                                    wT[och % 2][:, kc * 128:(kc + 1) * 128],
                                    self.ACT_A[:, kc * T + ts * TS:kc * T + (ts + 1) * TS],
                                    start=(kc == 0),
                                    stop=(kc == NCH - 1 and not lora))
                            if lora:
                                te.matmul(PSB[I % 2][:], btl[och % 2][:],
                                          self.TSC[:, ts * TS:(ts + 1) * TS],
                                          start=False, stop=False)
                                brow_src = self.BRB0 if och < 16 else self.BRB1
                                cc = (och % 16) * 128
                                inst = te.matmul(PSB[I % 2][:],
                                                 brow_src[0:1, cc:cc + 128],
                                                 self.CROW[0:1, :],
                                                 start=False, stop=True)
                            S["mm"].inc(inst)
                            if mode == "ffn2" and I >= 1:
                                self._exit_tp(te, I - 1, b, fin)

                    if from_scr:
                        for och in range(NCH):
                            emit_mm(och)
                    else:
                        emit_tp(0)
                        for och in range(NCH):
                            if och + 1 < NCH:
                                emit_tp(och + 1)
                            emit_mm(och)
                    if mode == "ffn2":
                        self._exit_tp(te, 4 * NCH - 1, b, fin)

                @blk.vector
                def _(ve):
                    def wt_evacs(och):
                        for g in range(8):
                            G = och * 8 + g
                            ve.wait_ge(S["tpg"].h, b["tpg"] + G + 1)
                            if och >= 2 and g == 0:
                                ve.wait_ge(S["mm"].h, b["mm"] + 4 * (och - 1))
                                if save_wt:
                                    self.dwait(ve, "ws", och - 2, b)
                            S["wt"].inc(ve.tensor_copy(
                                wT[och % 2][:, g * 512:(g + 1) * 512],
                                PSB[2 + G % 2][:]))
                    if not from_scr:
                        wt_evacs(0)
                    for och in range(NCH):
                        if not from_scr and och + 1 < NCH:
                            wt_evacs(och + 1)
                        if silu:
                            continue
                        for ts in range(NTS):
                            I = och * 4 + ts
                            ve.wait_ge(S["mm"].h, b["mm"] + I + 1)
                            if mode != "proj":
                                self.dwait(ve, "xc", I, b)
                            if I >= 2 and mode != "ffn2":
                                self.dwait(ve, "st", I - 2, b)
                            if mode == "ffn":
                                S["ev"].inc(ve.scalar_tensor_tensor(
                                    ob[I % 2][:], PSB[I % 2][:], 0.5, xcb[I % 2][:],
                                    ALU.mult, ALU.add))
                            elif mode == "outproj":
                                S["ev"].inc(ve.tensor_add(
                                    ob[I % 2][:], PSB[I % 2][:], xcb[I % 2][:]))
                            elif mode == "proj":
                                S["ev"].inc(ve.tensor_copy(ob[I % 2][:],
                                                           PSB[I % 2][:]))
                            else:
                                if I >= 2:
                                    ve.wait_ge(S["xt"].h, b["xt"] + I - 1)
                                S["ev"].inc(ve.scalar_tensor_tensor(
                                    fin[I % 2][:], PSB[I % 2][:], 0.5, xcb[I % 2][:],
                                    ALU.mult, ALU.add))
                        if mode == "ffn2":
                            for ts in range(NTS):
                                I = och * 4 + ts
                                ve.wait_ge(S["xt"].h, b["xt"] + I + 1)
                                if I >= 2:
                                    self.dwait(ve, "ys", I - 2, b)
                                S["yr"].inc(ve.tensor_copy(yro[I % 2][:],
                                                           PSB[4 + I % 2][:]))

                @blk.gpsimd
                def _(g_):
                    def ystore(I2):
                        och2, ts2 = divmod(I2, 4)
                        g_.wait_ge(S["yr"].h, b["yr"] + I2 + 1)
                        dsty = self.y[ts2 * TS:(ts2 + 1) * TS,
                                      och2 * 128:(och2 + 1) * 128].rearrange(
                                          "(jj p) o -> p jj o", p=128)
                        self.dinc("ys", I2, g_.dma_start(
                            dsty, yro[I2 % 2][:].rearrange(
                                "p (jj o) -> p jj o", o=128)))
                    for och in range(NCH):
                        if lora:
                            if och >= 2:
                                g_.wait_ge(S["mm"].h, b["mm"] + 4 * (och - 1))
                            self.dinc("bt", och, g_.dma_start(btl[och % 2][:],
                                                              self.btil[och, :, :]))
                        if save_wt:
                            g_.wait_ge(S["wt"].h, b["wt"] + 8 * (och + 1))
                            self.dinc("ws", och, g_.dma_start(self.wtf[och, :, :],
                                                              wT[och % 2][:]))
                        for ts in range(NTS):
                            I = och * 4 + ts
                            if mode != "proj":
                                if I >= 2:
                                    g_.wait_ge(S["ev"].h, b["ev"] + I - 1)
                                self.dinc("xc", I, g_.dma_start(
                                    xcb[I % 2][:],
                                    xres[och, :, ts * TS:(ts + 1) * TS]))
                            if mode in ("ffn", "outproj"):
                                g_.wait_ge(S["ev"].h, b["ev"] + I + 1)
                                self.dinc("st", I, g_.dma_start(
                                    xdst[och, :, ts * TS:(ts + 1) * TS],
                                    ob[I % 2][:]))
                            elif mode == "proj":
                                g_.wait_ge(S["ev"].h, b["ev"] + I + 1)
                                self.dinc("st", I, g_.dma_start(
                                    dst_scr[och, :, ts * TS:(ts + 1) * TS],
                                    ob[I % 2][:]))
                            else:
                                if och >= 1:
                                    ystore((och - 1) * 4 + ts)
                    if mode == "ffn2":
                        for ts in range(NTS):
                            ystore((NCH - 1) * 4 + ts)

    def _silu_evac(self, act, och, b, ob):
        S, PSB = self.S, self.PSB
        for ts in range(NTS):
            I = och * 4 + ts
            act.wait_ge(S["mm"].h, b["mm"] + I + 1)
            if I >= 2:
                self.dwait(act, "st", I - 2, b)
            S["ev"].inc(act.activation(ob[I % 2][:], PSB[I % 2][:], AF.Silu))

    def _exit_tp(self, te, I, b, fin):
        S, PSB = self.S, self.PSB
        te.wait_ge(S["ev"].h, b["ev"] + I + 1)
        if I >= 2:
            te.wait_ge(S["yr"].h, b["yr"] + I - 1)
        for m in range(4):
            inst = te.matmul(PSB[4 + I % 2][:, m * 128:(m + 1) * 128],
                             fin[I % 2][:, m * 128:(m + 1) * 128], self.identf,
                             start=True, stop=True)
        S["xt"].inc(inst)

    def p5a_wkv(self):
        nc, S, PSB = self.nc, self.S, self.PSB
        with ExitStack() as ps:
            kvb = {nm: [self.ACT_A[:, (2 * j + i) * T:(2 * j + i + 1) * T]
                        for i in range(2)]
                   for j, nm in enumerate("kvr")}
            obb = ps.enter_context(nc.sbuf_tensor(self.uname("k_ob"), [P, T], BF16))
            PM = [ps.enter_context(nc.sbuf_tensor(self.uname(f"k_pm{i}"), [P, 10 * 128], BF16))
                  for i in range(2)]
            def frow(hc, s):
                base = 6 + (hc % 2) * 11 + s
                return self.ACT_A[:, base * T:(base + 1) * T]
            UM = [ps.enter_context(nc.sbuf_tensor(self.uname(f"k_um{i}"), [P, 64], BF16))
                  for i in range(2)]
            vTT = [ps.enter_context(nc.sbuf_tensor(self.uname(f"k_vt{i}"), [P, 256], BF16))
                   for i in range(2)]
            ASB = [ps.enter_context(nc.sbuf_tensor(self.uname(f"k_as{i}"), [P, 256], BF16))
                   for i in range(2)]
            TMPU = [ps.enter_context(nc.sbuf_tensor(self.uname(f"k_tu{i}"), [P, 128], BF16))
                    for i in range(2)]
            S0 = ps.enter_context(nc.sbuf_tensor(self.uname("k_s0"), [P, 64], F32))
            S0b = ps.enter_context(nc.sbuf_tensor(self.uname("k_s0b"), [P, 64], BF16))
            LNW = [ps.enter_context(nc.sbuf_tensor(self.uname(f"k_lnw{i}"), [P, 10], F32))
                   for i in range(2)]
            b = self.snap()

            def X(hc, c):
                return hc * NTC + c

            with nc.Block() as blk:
                @blk.sync
                def _(sync):
                    for hc in range(NHC):
                        if hc >= 2:
                            sync.wait_ge(S["uf"].h, b["uf"] + (hc - 1) * NTC)
                        for nm in "kvr":
                            self.dinc("kv", hc, sync.dma_start(
                                kvb[nm][hc % 2][:], self.kvrg[nm][hc, :, :]))
                        if hc >= 1:
                            sync.wait_ge(S["uf"].h, b["uf"] + hc * NTC)
                            S["ob"].inc(sync.dma_start(
                                self.obscr[hc - 1, :, :], obb[:]), 16)
                    sync.wait_ge(S["uf"].h, b["uf"] + NHC * NTC)
                    S["ob"].inc(sync.dma_start(self.obscr[NHC - 1, :, :], obb[:]), 16)
                    sync.wait_ge(S["ob"].h, S["ob"].n)

                @blk.scalar
                def _(act):
                    def pm_build(h):
                        L = LNW[h % 2]
                        S["ax2"].inc(act.activation(L[:, 0:1],
                                                    self.tdv[:, h:h + 1], AF.Exp))
                        act.wait_ge(S["cd"].h, b["cd"] + 3 * (h + 1))
                        if h >= 2:
                            act.wait_ge(S["pp"].h, b["pp"] + h - 1)
                            act.wait_ge(S["dp"].h, b["dp"] + h - 1)
                        pm = PM[h % 2]
                        act.activation(pm[:, 0:128], self.iota[128], AF.Exp,
                                       scale=L[:, 1:2])
                        act.activation(pm[:, 128:256], self.iota[64], AF.Exp,
                                       scale=L[:, 1:2])
                        act.activation(pm[:, 256:384], self.iota[32], AF.Exp,
                                       scale=L[:, 1:2])
                        act.activation(pm[:, 384:512], self.iota[16], AF.Exp,
                                       scale=L[:, 1:2])
                        act.activation(pm[:, 512:640], self.iota[16], AF.Exp,
                                       scale=L[:, 2:3])
                        act.activation(pm[:, 640:768], self.iota[128], AF.Exp,
                                       scale=L[:, 0:1], bias=L[:, 4:5])
                        act.activation(pm[:, 768:896], self.iota[64], AF.Exp,
                                       scale=L[:, 0:1], bias=L[:, 5:6])
                        act.activation(pm[:, 896:1024], self.iota[32], AF.Exp,
                                       scale=L[:, 0:1], bias=L[:, 6:7])
                        act.activation(pm[:, 1024:1152], self.iota[16], AF.Exp,
                                       scale=L[:, 0:1], bias=L[:, 7:8])
                        act.activation(pm[:, 1152:1280], self.iota[16], AF.Exp,
                                       scale=L[:, 3:4], bias=L[:, 3:4])
                        S["pmb"].inc(act.activation(L[:, 8:9], L[:, 1:2],
                                                    AF.Exp, scale=128.0))
                    pm_build(0)
                    for hc in range(NHC):
                        act.wait_ge(S["sbm"].h, b["sbm"] + hc * (NTC + 1))
                        S["sbm"].inc(act.memzero(S0b[:]))
                        for c in range(NTC):
                            Xc = X(hc, c)
                            act.wait_ge(S["tt"].h, b["tt"] + Xc + 1)
                            act.activation(vTT[c % 2][:, 0:128],
                                           PSB[c % 2][:, 0:128], AF.Copy)
                            S["ae"].inc(act.activation(vTT[c % 2][:, 128:256],
                                                       PSB[c % 2][:, 128:256],
                                                       AF.Copy))
                            if c == 0 and hc + 1 < NHC:
                                pm_build(hc + 1)
                            act.wait_ge(S["uf"].h, b["uf"] + Xc + 1)
                            S["sbm"].inc(act.activation(S0b[:], S0[:], AF.Copy))

                @blk.gpsimd
                def _(g_):
                    for hc in range(NHC):
                        g_.wait_ge(S["pmb"].h, b["pmb"] + hc + 1)
                        nmkv = "kv" if hc % 2 == 0 else "kv2"
                        g_.wait_ge(S[nmkv].h, b[nmkv] + 48 * (hc // 2 + 1))
                        if hc >= 2:
                            g_.wait_ge(S["om"].h, b["om"] + (hc - 1) * NTC)
                        pm = PM[hc % 2]
                        g_.tensor_scalar_mul(UM[hc % 2][:], self.ones64,
                                             self.faaaa[:, hc:hc + 1])
                        kfull = kvb["k"][hc % 2][:]
                        for s, pmo in [(5, 768), (6, 896), (7, 1024), (9, 640)]:
                            big = pm[:, pmo:pmo + 128].unsqueeze(1).broadcast_to(
                                [P, NTC, 128])
                            ov = frow(hc, s).rearrange("p (a c) -> p a c", c=128)
                            kv_ = kfull.rearrange("p (a c) -> p a c", c=128)
                            inst = g_.tensor_tensor(ov, kv_, big, ALU.mult)
                        S["pp"].inc(inst)

                def emit_G1(te, hc, c):
                    Xc = X(hc, c)
                    cs = slice(c * C, (c + 1) * C)
                    if c == 0:
                        te.wait_ge(S["pp"].h, b["pp"] + hc + 1)
                        te.wait_ge(S["dp"].h, b["dp"] + hc + 1)
                    if Xc >= 2:
                        te.wait_ge(S["ae"].h, b["ae"] + Xc - 1)
                        te.wait_ge(S["cg"].h, b["cg"] + Xc - 1)
                        te.wait_ge(S["am"].h, b["am"] + Xc - 1)
                    vsl = kvb["v"][hc % 2][:, cs]
                    rt = frow(hc, 0)[:, cs]
                    r64, r32, r16 = (frow(hc, 1)[:, cs], frow(hc, 2)[:, cs],
                                     frow(hc, 3)[:, cs])
                    rcl, kcl = frow(hc, 4)[:, cs], frow(hc, 8)[:, cs]
                    k64, k32, k16 = (frow(hc, 5)[:, cs], frow(hc, 6)[:, cs],
                                     frow(hc, 7)[:, cs])
                    kg = frow(hc, 9)[:, cs]
                    rk = frow(hc, 10)[:, cs]
                    te.matmul(PSB[c % 2][:, 0:128], vsl, self.ident,
                              start=True, stop=True)
                    te.matmul(PSB[c % 2][:, 128:256], kg, self.ident,
                              start=True, stop=True)
                    pA = PSB[2 + c % 2]
                    for h in range(2):
                        po = 64 * h
                        co = 128 * h
                        te.matmul(pA[:, co:co + 128], kcl[po:po + 64, :],
                                  rcl[po:po + 64, :], start=True, stop=True,
                                  tile_position=(po, 0))
                        te.matmul(pA[0:64, co + 64:co + 128], k64[po:po + 64, 0:64],
                                  r64[po:po + 64, 64:128],
                                  start=True, stop=True, tile_position=(po, 0))
                        te.matmul(pA[0:32, co + 32:co + 64], k32[po:po + 64, 0:32],
                                  r32[po:po + 64, 32:64],
                                  start=True, stop=True, tile_position=(po, 0))
                        te.matmul(pA[64:96, co + 96:co + 128], k32[po:po + 64, 64:96],
                                  r32[po:po + 64, 96:128],
                                  start=True, stop=True, tile_position=(po, 64))
                        for m in range(4):
                            j0 = 32 * m
                            te.matmul(pA[j0:j0 + 16, co + j0 + 16:co + j0 + 32],
                                      k16[po:po + 64, j0:j0 + 16],
                                      r16[po:po + 64, j0 + 16:j0 + 32],
                                      start=True, stop=True,
                                      tile_position=(po, j0))
                        inst = te.matmul(PSB[4 + c % 2][po:po + 64, 0:128],
                                         UM[hc % 2][po:po + 64, :],
                                         rk[po:po + 64, :],
                                         start=True, stop=True,
                                         tile_position=(po, po))
                    S["tt"].inc(inst)

                def emit_OM(te, hc, c):
                    Xc = X(hc, c)
                    rt = frow(hc, 0)[:, c * C:(c + 1) * C]
                    te.wait_ge(S["am"].h, b["am"] + Xc + 1)
                    te.wait_ge(S["ae"].h, b["ae"] + Xc + 1)
                    te.wait_ge(S["sbm"].h, b["sbm"] + hc * (NTC + 1) + 1 + c)
                    for h in range(2):
                        po = 64 * h
                        te.matmul(PSB[6][po:po + 64, 0:128],
                                  vTT[c % 2][:, po:po + 64],
                                  ASB[c % 2][:, h * 128:(h + 1) * 128],
                                  start=True, stop=False, tile_position=(0, po))
                        te.matmul(PSB[6][po:po + 64, 0:128],
                                  S0b[po:po + 64, :], rt[po:po + 64, :],
                                  start=False, stop=True, tile_position=(po, po))
                        inst = te.matmul(PSB[7][po:po + 64, 0:64],
                                         vTT[c % 2][:, 128 + po:128 + po + 64],
                                         vTT[c % 2][:, po:po + 64],
                                         start=True, stop=True,
                                         tile_position=(0, po))
                    S["om"].inc(inst)

                @blk.tensor
                def _(te):
                    prev = None
                    for hc in range(NHC):
                        for c in range(NTC):
                            emit_G1(te, hc, c)
                            if prev is not None:
                                emit_OM(te, *prev)
                            prev = (hc, c)
                    emit_OM(te, *prev)

                @blk.vector
                def _(ve):
                    def lnw_chain(h):
                        L = LNW[h % 2]
                        ve.wait_ge(S["ax2"].h, b["ax2"] + h + 1)
                        S["cd"].inc(ve.tensor_scalar_mul(L[:, 1:2], L[:, 0:1], -1.0))
                        ve.wait_ge(S["cd"].h, S["cd"].n)
                        S["cd"].inc(ve.tensor_scalar_max(L[:, 2:3], L[:, 1:2],
                                                         LNWMIN))
                        ve.wait_ge(S["cd"].h, S["cd"].n)
                        ve.tensor_scalar_mul(L[:, 3:4], L[:, 2:3], -1.0)
                        ve.tensor_scalar_mul(L[:, 4:5], L[:, 1:2], 127.0)
                        ve.tensor_scalar_mul(L[:, 5:6], L[:, 1:2], 63.0)
                        ve.tensor_scalar_mul(L[:, 6:7], L[:, 1:2], 31.0)
                        S["cd"].inc(ve.tensor_scalar_mul(L[:, 7:8], L[:, 1:2], 15.0))

                    def preps(hc):
                        pm = PM[hc % 2]
                        ve.wait_ge(S["pmb"].h, b["pmb"] + hc + 1)
                        nmkv = "kv" if hc % 2 == 0 else "kv2"
                        ve.wait_ge(S[nmkv].h, b[nmkv] + 48 * (hc // 2 + 1))
                        if hc >= 2:
                            ve.wait_ge(S["om"].h, b["om"] + (hc - 1) * NTC)
                        rfull = kvb["r"][hc % 2][:]
                        kfull = kvb["k"][hc % 2][:]
                        rv = rfull.rearrange("p (a c) -> p a c", c=128)
                        for s, pmo in [(0, 0), (1, 128), (2, 256), (3, 384),
                                       (4, 512), (8, 1152)]:
                            big = pm[:, pmo:pmo + 128].unsqueeze(1).broadcast_to(
                                [P, NTC, 128])
                            ov = frow(hc, s).rearrange("p (a c) -> p a c", c=128)
                            iv = rv if s <= 4 else kfull.rearrange(
                                "p (a c) -> p a c", c=128)
                            ve.tensor_tensor(ov, iv, big, ALU.mult)
                        S["dp"].inc(ve.tensor_tensor(frow(hc, 10), rfull, kfull,
                                                     ALU.mult))

                    def tail(Xc):
                        hc, c = divmod(Xc, NTC)
                        if c == 0:
                            lnw_chain(hc) if False else None
                            ve.wait_ge(S["sbm"].h,
                                       b["sbm"] + hc * (NTC + 1) + 1)
                            if hc >= 1:
                                ve.wait_ge(S["ob"].h, b["ob"] + 16 * hc)
                            ve.memset(S0[:], 0.0)
                        wc = LNW[hc % 2][:, 8:9]
                        ve.wait_ge(S["tt"].h, b["tt"] + Xc + 1)
                        pA = PSB[2 + c % 2]
                        ve.tensor_tensor(ASB[c % 2][:, 0:128], pA[:, 0:128],
                                         self.masku, ALU.mult)
                        S["am"].inc(ve.tensor_tensor(ASB[c % 2][:, 128:256],
                                                     pA[:, 128:256],
                                                     self.masku, ALU.mult))
                        ve.wait_ge(S["om"].h, b["om"] + Xc + 1)
                        vsl = kvb["v"][hc % 2][:, c * C:(c + 1) * C]
                        S["cg"].inc(ve.tensor_tensor(TMPU[c % 2][:], vsl,
                                                     PSB[4 + c % 2][:, 0:128],
                                                     ALU.mult))
                        ve.wait_ge(S["cg"].h, S["cg"].n)
                        ve.tensor_tensor(obb[:, c * C:(c + 1) * C],
                                         PSB[6][:, 0:128], TMPU[c % 2][:],
                                         ALU.add)
                        S["uf"].inc(ve.scalar_tensor_tensor(
                            S0[:], S0[:], wc, PSB[7][:, 0:64],
                            ALU.mult, ALU.add))

                    NX = NHC * NTC
                    lnw_chain(0)
                    preps(0)
                    for Xc in range(NX):
                        hc, c = divmod(Xc, NTC)
                        if c == 1 and hc + 1 < NHC:
                            lnw_chain(hc + 1)
                        if Xc >= 1:
                            tail(Xc - 1)
                        if c == 4 and hc + 1 < NHC:
                            preps(hc + 1)
                    tail(NX - 1)

    def p5b_gn(self):
        nc, S, PSB = self.nc, self.S, self.PSB
        with ExitStack() as ps:
            obb = [ps.enter_context(nc.sbuf_tensor(self.uname(f"g_ob{i}"), [P, T], BF16))
                   for i in range(2)]
            sqb = [ps.enter_context(nc.sbuf_tensor(self.uname(f"g_sq{i}"), [P, T], BF16))
                   for i in range(2)]
            gb = [ps.enter_context(nc.sbuf_tensor(self.uname(f"g_g{i}"), [P, T], BF16))
                  for i in range(2)]
            b = self.snap()
            with nc.Block() as blk:
                @blk.gpsimd
                def _(g_):
                    def t3(hc2):
                        g_.wait_ge(S["cg"].h, b["cg"] + 8 * hc2 + 8)
                        S["ax2"].inc(g_.tensor_scalar(
                            sqb[hc2 % 2][:], sqb[hc2 % 2][:],
                            self.gnw[:, hc2:hc2 + 1], self.gnb[:, hc2:hc2 + 1],
                            ALU.mult, ALU.add))
                    for hc in range(NHC):
                        if hc >= 2:
                            g_.wait_ge(S["gn"].h, b["gn"] + hc - 1)
                        self.dinc("xc", hc, g_.dma_start(obb[hc % 2][:],
                                                         self.obscr[hc, :, :]))
                        self.dinc("gv", hc, g_.dma_start(gb[hc % 2][:],
                                                         self.kvrg["g"][hc, :, :]))
                        if hc >= 1:
                            t3(hc - 1)
                    t3(NHC - 1)

                @blk.scalar
                def _(act):
                    for hc in range(NHC):
                        self.dwait(act, "xc", hc, b)
                        if hc >= 2:
                            act.wait_ge(S["sst"].h, b["sst"] + hc - 1)
                        S["aux"].inc(act.activation(sqb[hc % 2][:], obb[hc % 2][:],
                                                    AF.Square))
                        for ts in range(NTS):
                            slot = 4 * hc + ts
                            pr = slot % 2
                            act.wait_ge(S["sst"].h, b["sst"] + hc + 1)
                            if slot >= 2:
                                act.wait_ge(S["cd"].h, b["cd"] + 2 * (slot - 1))
                                act.wait_ge(S["ln"].h, b["ln"] + slot - 1)
                            S["ca"].inc(act.activation(self.RMUs[pr][0:2, :],
                                                       PSB[ts][0:2, :], AF.Copy,
                                                       scale=1.0 / 64))
                            act.wait_ge(S["ca"].h, S["ca"].n)
                            S["ca"].inc(act.activation(self.RSQs[pr][0:2, :],
                                                       self.RMUs[pr][0:2, :],
                                                       AF.Square))
                            act.wait_ge(S["cd"].h, b["cd"] + 2 * slot + 1)
                            S["ca"].inc(act.activation(self.RVAs[pr][0:2, :],
                                                       self.RVAs[pr][0:2, :], AF.Ln,
                                                       bias=self.CF[0:2, 640:641]))
                            act.wait_ge(S["ca"].h, S["ca"].n)
                            S["ca"].inc(act.activation(self.RSBs[pr][0:2, :],
                                                       self.RVAs[pr][0:2, :], AF.Exp,
                                                       scale=-0.5))

                @blk.tensor
                def _(te):
                    for hc in range(NHC):
                        self.dwait(te, "xc", hc, b)
                        te.wait_ge(S["aux"].h, b["aux"] + hc + 1)
                        if hc >= 1:
                            te.wait_ge(S["cg"].h, b["cg"] + 8 * hc)
                        for ts in range(NTS):
                            te.matmul(PSB[ts][0:2, :], self.hm2,
                                      obb[hc % 2][:, ts * TS:(ts + 1) * TS],
                                      start=True, stop=True)
                        for ts in range(NTS):
                            inst = te.matmul(PSB[4 + ts][0:2, :], self.hm2,
                                             sqb[hc % 2][:, ts * TS:(ts + 1) * TS],
                                             start=True, stop=True)
                        S["sst"].inc(inst)
                        for ts in range(NTS):
                            slot = 4 * hc + ts
                            pr = slot % 2
                            te.wait_ge(S["cd"].h, b["cd"] + 2 * slot + 2)
                            te.matmul(PSB[ts][:, 0:512], self.hm2T,
                                      self.RSBs[pr][0:2, :],
                                      start=True, stop=True)
                            inst = te.matmul(PSB[4 + ts][:, 0:512], self.hm2T,
                                             self.RMBs[pr][0:2, :],
                                             start=True, stop=True)
                            S["ln"].inc(inst)

                @blk.vector
                def _(ve):
                    for hc in range(NHC):
                        for ts in range(NTS):
                            slot = 4 * hc + ts
                            pr = slot % 2
                            ve.wait_ge(S["ca"].h, b["ca"] + 16 * hc + 4 * ts + 2)
                            S["cd"].inc(ve.scalar_tensor_tensor(
                                self.RVAs[pr][0:2, :], PSB[4 + ts][0:2, :], 1.0 / 64,
                                self.RSQs[pr][0:2, :], ALU.mult, ALU.subtract))
                            ve.wait_ge(S["ca"].h, b["ca"] + 16 * hc + 4 * ts + 4)
                            S["cd"].inc(ve.tensor_mul(self.RMBs[pr][0:2, :],
                                                      self.RMUs[pr][0:2, :],
                                                      self.RSBs[pr][0:2, :]))
                        for ts in range(NTS):
                            slot = 4 * hc + ts
                            ve.wait_ge(S["ln"].h, b["ln"] + slot + 1)
                            sl = slice(ts * TS, (ts + 1) * TS)
                            S["cg"].inc(ve.tensor_tensor(
                                sqb[hc % 2][:, sl], obb[hc % 2][:, sl],
                                PSB[ts][:, 0:512], ALU.mult))
                            ve.wait_ge(S["cg"].h, S["cg"].n)
                            S["cg"].inc(ve.tensor_tensor(
                                sqb[hc % 2][:, sl], sqb[hc % 2][:, sl],
                                PSB[4 + ts][:, 0:512], ALU.subtract))
                        ve.wait_ge(S["ax2"].h, b["ax2"] + hc + 1)
                        self.dwait(ve, "gv", hc, b)
                        S["gn"].inc(ve.tensor_mul(
                            self.ACT_A[:, hc * T:(hc + 1) * T],
                            sqb[hc % 2][:], gb[hc % 2][:]))

    def p_final(self):
        nc, S = self.nc, self.S
        with nc.Block() as blk:
            @blk.sync
            def _(sync):
                for nm in ["ys", "ys2", "st", "st2", "ws", "ws2", "ob", "qs", "bc",
                           "rbs", "xc", "xc2", "bt", "bt2", "gv", "gv2"]:
                    if S[nm].n:
                        sync.wait_ge(S[nm].h, S[nm].n)


def _prep_inputs(inputs):
    cbf, cf32, crow = _host_consts()
    vecs = np.zeros((128, 8 * 32), dtype=np.float32)
    order = [inputs["L0_attn_time_mix_key"], inputs["L0_attn_time_mix_value"],
             inputs["L0_attn_time_mix_receptance"], inputs["L0_attn_time_mix_gate"],
             inputs["L0_attn_gn_weight"], inputs["L0_attn_gn_bias"],
             np.asarray(inputs["L0_attn_time_decay"]).reshape(-1),
             np.asarray(inputs["L0_attn_time_faaaa"]).reshape(-1)]
    for i, v in enumerate(order):
        vecs[:, i * 32:(i + 1) * 32] = np.asarray(v, np.float32).reshape(32, 128).T
    A = np.ascontiguousarray(np.asarray(inputs["L0_ffn_A"], np.float32).reshape(128, D))
    Bm = np.ascontiguousarray(
        np.asarray(inputs["L0_ffn_B"], np.float32).transpose(1, 0, 2).reshape(D, 128))
    bvec = np.asarray(inputs["L0_ffn_b"], np.float32)
    base = {
        "wk": np.asarray(inputs["L0_attn_w_key"], np.float32),
        "wv": np.asarray(inputs["L0_attn_w_value"], np.float32),
        "wr": np.asarray(inputs["L0_attn_w_receptance"], np.float32),
        "wg": np.asarray(inputs["L0_attn_w_gate"], np.float32),
        "wo": np.asarray(inputs["L0_attn_w_output"], np.float32),
        "wf": np.asarray(inputs["L0_ffn_W"], np.float32),
        "amat": A, "bmat": Bm,
        "rwm": np.asarray(inputs["L0_ffn_router_W"], np.float32),
        "rbr": np.asarray(inputs["L0_ffn_router_b"], np.float32).reshape(1, 8),
        "brow0": np.ascontiguousarray(bvec[:T]).reshape(1, T),
        "brow1": np.ascontiguousarray(bvec[T:]).reshape(1, T),
        "vecs": vecs, "cbf": cbf, "cf32": cf32, "crow": crow,
    }
    xall = np.asarray(inputs["L0_attn_hidden"], np.float32)
    maps = []
    for core in range(8):
        m = dict(base)
        m["x0"] = np.ascontiguousarray(xall[core])
        maps.append(m)
    return maps


def kernel(**inputs):
    maps = _prep_inputs(inputs)
    nc = KB().build()
    res = run_bass_kernel_spmd(nc, maps, core_ids=list(range(8)))
    out = np.stack([np.asarray(r["y"], np.float32) for r in res.results], axis=0)
    return out


# version 10
# speedup vs baseline: 1.1319x; 1.0298x over previous
import numpy as np
import ml_dtypes
from contextlib import ExitStack

import concourse.bass as bass
import concourse.mybir as mybir
from concourse.bass_utils import run_bass_kernel_spmd

F32 = mybir.dt.float32
BF16 = mybir.dt.bfloat16
AF = mybir.ActivationFunctionType
ALU = mybir.AluOpType
AX = mybir.AxisListType

P = 128
T = 2048
TS = 512
NTS = 4
NCH = 32
D = 4096
C = 128
NTC = 16
NHC = 32
LNWMIN = -5.0
EPS = 1e-5
QL, QR = 611, 2047
NQ = QR - QL

BF = ml_dtypes.bfloat16


def _host_consts():
    j = np.arange(128)
    cbf = np.zeros((128, 579), dtype=np.float32)
    cbf[:, 0:128] = np.eye(128)
    cbf[:, 128:256] = np.triu(np.ones((128, 128)), 1)
    cbf[:, 256:320] = 1.0
    cbf[:64, 320] = 1.0
    cbf[64:, 321] = 1.0
    cbf[:, 322] = 1.0
    for e in range(8):
        cbf[e, 323 + e * 16:323 + (e + 1) * 16] = 1.0
    cbf[0, 451:515] = 1.0
    cbf[1, 515:579] = 1.0
    cf32 = np.zeros((128, 642), dtype=np.float32)
    cf32[:, 0:128] = np.eye(128)
    cf32[:, 128:256] = j[None, :]
    cf32[:, 256:384] = (j % 64)[None, :]
    cf32[:, 384:512] = (j % 32)[None, :]
    cf32[:, 512:640] = (j % 16)[None, :]
    cf32[:, 640] = 1e-5
    return cbf.astype(BF), cf32, np.ones((1, 512), dtype=BF)


class Sem:
    def __init__(self, es, nc, name):
        self.h = es.enter_context(nc.semaphore(name))
        self.n = 0

    def inc(self, inst, k=1):
        inst.then_inc(self.h, k)
        self.n += k
        return self.n


class KB:
    def __init__(self):
        nc = bass.Bass("TRN2", target_bir_lowering=False)
        self.nc = nc
        self.x0 = nc.dram_tensor("x0", [T, D], F32, kind="ExternalInput").ap()
        self.w = {nm: nc.dram_tensor(nm, [D, D], F32, kind="ExternalInput").ap()
                  for nm in ["wk", "wv", "wr", "wg", "wo", "wf"]}
        self.amat = nc.dram_tensor("amat", [128, D], F32, kind="ExternalInput").ap()
        self.bmat = nc.dram_tensor("bmat", [D, 128], F32, kind="ExternalInput").ap()
        self.rwm = nc.dram_tensor("rwm", [8, D], F32, kind="ExternalInput").ap()
        self.rbr = nc.dram_tensor("rbr", [1, 8], F32, kind="ExternalInput").ap()
        self.brow0 = nc.dram_tensor("brow0", [1, T], F32, kind="ExternalInput").ap()
        self.brow1 = nc.dram_tensor("brow1", [1, T], F32, kind="ExternalInput").ap()
        self.vecs = nc.dram_tensor("vecs", [128, 8 * 32], F32, kind="ExternalInput").ap()
        self.cbf = nc.dram_tensor("cbf", [128, 579], BF16, kind="ExternalInput").ap()
        self.cf32 = nc.dram_tensor("cf32", [128, 642], F32, kind="ExternalInput").ap()
        self.crow = nc.dram_tensor("crow", [1, 512], BF16, kind="ExternalInput").ap()
        self.y = nc.dram_tensor("y", [T, D], F32, kind="ExternalOutput").ap()

        self.xfm = [nc.dram_tensor(f"xfm{i}", [NCH, P, T], BF16, kind="Internal").ap()
                    for i in range(3)]
        self.kvrg = {nm: nc.dram_tensor(nm + "scr", [NCH, P, T], BF16, kind="Internal").ap()
                     for nm in "kvrg"}
        self.obscr = nc.dram_tensor("obscr", [NHC, P, T], BF16, kind="Internal").ap()
        self.wtf = nc.dram_tensor("wtf_scr", [NCH, P, D], BF16, kind="Internal").ap()
        self.atil = nc.dram_tensor("atil_scr", [NCH, P, 128], BF16, kind="Internal").ap()
        self.btil = nc.dram_tensor("btil_scr", [NCH, P, 128], BF16, kind="Internal").ap()
        self.rows_scr = nc.dram_tensor("rows_scr", [4, T], BF16, kind="Internal").ap()

    def snap(self):
        return {k: s.n for k, s in self.S.items()}

    HW_DMA_SEMS = ["qs", "wl", "wl2", "bc", "ob", "kv", "kv2", "wta", "wta2"]
    SW_DMA_SEMS = ["st", "st2", "xc", "xc2", "bt", "bt2", "ws", "ws2", "rbs",
                   "gv", "gv2", "ys", "ys2"]

    def _drain_dmas(self):
        S = self.S
        with self.nc.Block() as blk:
            @blk.sync
            def _(sync):
                for nm in self.HW_DMA_SEMS:
                    if S[nm].n:
                        sync.wait_ge(S[nm].h, S[nm].n)

            @blk.gpsimd
            def _(g_):
                for nm in self.SW_DMA_SEMS:
                    if S[nm].n:
                        g_.wait_ge(S[nm].h, S[nm].n)

    def dinc(self, name, idx, inst):
        self.S[name if idx % 2 == 0 else name + "2"].inc(inst, 16)

    def dwait(self, eng, name, idx, b):
        nm = name if idx % 2 == 0 else name + "2"
        eng.wait_ge(self.S[nm].h, b[nm] + 16 * (idx // 2 + 1))

    def uname(self, base):
        self._uc = getattr(self, "_uc", 0) + 1
        return f"{base}_{self._uc}"

    def build(self, upto=None, debug=()):
        self._upto = upto
        self._debug = tuple(debug)
        nc = self.nc
        if debug:
            self.dbg_out = {}
            scr = {"xfm0": self.xfm[0], "xfm1": self.xfm[1], "xfm2": self.xfm[2],
                   "kscr": self.kvrg["k"], "vscr": self.kvrg["v"],
                   "rscr": self.kvrg["r"], "gscr": self.kvrg["g"],
                   "obscr": self.obscr, "rows": self.rows_scr,
                   "atil": self.atil, "btil": self.btil, "wtf": self.wtf}
            for nm in debug:
                t = scr[nm]
                self.dbg_out[nm] = nc.dram_tensor(
                    "dbg_" + nm, list(t.shape), BF16, kind="ExternalOutput").ap()
        es = ExitStack()
        with es:
            sb = lambda name, shape, dt: es.enter_context(nc.sbuf_tensor(name, shape, dt))
            self.ACT_A = sb("ACT_A", [P, NCH * T], BF16)
            self.RSTD = sb("RSTD", [P, T], BF16)
            self.MURS = sb("MURS", [P, T], BF16)
            self.TSC = sb("TSC", [P, T], BF16)
            self.VECS = sb("VECS", [P, 8 * 32], F32)
            self.CBF = sb("CBF", [P, 579], BF16)
            self.CF = sb("CF", [P, 642], F32)
            self.CROW = sb("CROW", [1, 512], BF16)
            self.RWT = sb("RWT", [P, 256], BF16)
            self.BRB0 = sb("BRB0", [1, T], BF16)
            self.BRB1 = sb("BRB1", [1, T], BF16)
            self.RBR8 = sb("RBR8", [1, 8], F32)
            self.RTBC = sb("RTBC", [P, 1], F32)
            self.QSUM = sb("QSUM", [P, NCH], F32)
            self.QBF = sb("QBF", [P, NCH], BF16)
            self.SROW = sb("SROW", [1, 32], F32)
            self.PBF = sb("PBF", [1, 8], BF16)
            self.PCOL = sb("PCOL", [8, 1], BF16)
            self.RMUs = [sb(f"RMU{i}", [2, TS], F32) for i in range(2)]
            self.RSQs = [sb(f"RSQ{i}", [2, TS], F32) for i in range(2)]
            self.RVAs = [sb(f"RVA{i}", [2, TS], F32) for i in range(2)]
            self.RSBs = [sb(f"RSB{i}", [2, TS], BF16) for i in range(2)]
            self.RMBs = [sb(f"RMB{i}", [2, TS], BF16) for i in range(2)]
            self.RMU, self.RSQ, self.RVA = self.RMUs[0], self.RSQs[0], self.RVAs[0]
            self.RSB, self.RMB = self.RSBs[0], self.RMBs[0]

            self.ident = self.CBF[:, 0:128]
            self.masku = self.CBF[:, 128:256]
            self.ones64 = self.CBF[:, 256:320]
            self.hm2 = self.CBF[:, 320:322]
            self.onescol = self.CBF[:, 322:323]
            self.exp8 = self.CBF[0:8, 323:451]
            self.hm2T = self.CBF[0:2, 451:579]
            self.identf = self.CF[:, 0:128]
            self.iota = {128: self.CF[:, 128:256], 64: self.CF[:, 256:384],
                         32: self.CF[:, 384:512], 16: self.CF[:, 512:640]}
            self.tm = {pn: self.VECS[:, i * 32:(i + 1) * 32]
                       for i, pn in enumerate("kvrg")}
            self.gnw = self.VECS[:, 4 * 32:5 * 32]
            self.gnb = self.VECS[:, 5 * 32:6 * 32]
            self.tdv = self.VECS[:, 6 * 32:7 * 32]
            self.faaaa = self.VECS[:, 7 * 32:8 * 32]

            self.PSB = [es.enter_context(nc.psum_tensor(f"PS{i}", [P, 512], F32))
                        for i in range(8)]
            self.S = {nm: Sem(es, nc, "s_" + nm) for nm in
                      ["qs", "wl", "cv", "tpg", "sst", "wt", "mm", "ev", "xc",
                       "st", "bt", "ws", "xt", "yr", "ys", "pp", "tt", "ae",
                       "am", "om", "uf", "sbm", "kv", "pmb", "cd", "ca", "cp",
                       "cg", "ln", "gn", "aux", "ax2", "bc", "ob",
                       "wl2", "xc2", "st2", "bt2", "ys2", "ws2", "kv2",
                       "wta", "wta2", "rbs", "gv", "gv2", "dp"]}

            seq = [
                ("consts", self.p0_consts),
                ("entry", self.p0_entry),
                ("small", self.p0_small),
                ("lnq1", lambda: self.p1_lnq(self.xfm[0])),
                ("router1", self.p1_router),
                ("tsc1", self.p1_tsc),
                ("ffn1", lambda: self.p2_main("ffn", self.w["wf"],
                                              xres=self.xfm[0], xdst=self.xfm[1],
                                              save_wt=True)),
                ("stats1", lambda: self.p_stats(self.xfm[1])),
                ("bcast1", self._bcast_rows),
            ]
            for pn in "kvrg":
                seq.append(("mix_" + pn,
                            (lambda p: lambda: self.p4_mix(p, self.xfm[1]))(pn)))
                seq.append(("proj_" + pn,
                            (lambda p: lambda: self.p2_main(
                                "proj", self.w["w" + p], dst_scr=self.kvrg[p],
                                silu=(p == "g")))(pn)))
            seq += [
                ("wkv", self.p5a_wkv),
                ("gn", self.p5b_gn),
                ("outproj", lambda: self.p2_main("outproj", self.w["wo"],
                                                 xres=self.xfm[1],
                                                 xdst=self.xfm[2])),
                ("stats2", lambda: self.p_stats(self.xfm[2])),
                ("lnq2", lambda: self.p1_lnq(self.xfm[2])),
                ("router2", self.p1_router),
                ("tsc2", self.p1_tsc),
                ("ffn2", lambda: self.p2_main("ffn2", None, xres=self.xfm[2])),
            ]
            for name, fn in seq:
                fn()
                self._drain_dmas()
                if upto is not None and name == upto:
                    break
            if self._debug:
                self._emit_debug()
            self.p_final()
        return nc

    def _emit_debug(self):
        nc, S = self.nc, self.S
        scr = {"xfm0": self.xfm[0], "xfm1": self.xfm[1], "xfm2": self.xfm[2],
               "kscr": self.kvrg["k"], "vscr": self.kvrg["v"],
               "rscr": self.kvrg["r"], "gscr": self.kvrg["g"],
               "obscr": self.obscr, "rows": self.rows_scr,
               "atil": self.atil, "btil": self.btil, "wtf": self.wtf}
        with nc.Block() as blk:
            @blk.sync
            def _(sync):
                for snm in ["qs", "wl", "wl2", "st", "st2", "xc", "xc2", "bt",
                            "bt2", "ws", "ws2", "ys2", "kv", "kv2", "gv", "gv2",
                            "bc", "ob", "rbs", "wta", "wta2"]:
                    if S[snm].n:
                        sync.wait_ge(S[snm].h, S[snm].n)
                for nm in self._debug:
                    srcap = scr[nm]
                    dstap = self.dbg_out[nm]
                    if nm == "rows":
                        srcap = srcap[0:2, :]
                        dstap = dstap[0:2, :]
                    S["qs"].inc(sync.dma_start(dstap, srcap), 16)
                sync.wait_ge(S["qs"].h, S["qs"].n)

    def p0_consts(self):
        nc, S = self.nc, self.S
        with nc.Block() as blk:
            @blk.sync
            def _(sync):
                for dst, src in [(self.CBF[:], self.cbf), (self.CF[:], self.cf32),
                                 (self.CROW[:], self.crow), (self.VECS[:], self.vecs),
                                 (self.RBR8[:], self.rbr)]:
                    S["qs"].inc(sync.dma_start(dst, src), 16)
                sync.wait_ge(S["qs"].h, S["qs"].n)

    def p0_entry(self):
        nc, S, PSB = self.nc, self.S, self.PSB
        NT = 16
        with ExitStack() as ps:
            xtf = [ps.enter_context(nc.sbuf_tensor(self.uname(f"e_xtf{i}"), [P, 2048], F32))
                   for i in range(2)]
            xtb = ps.enter_context(nc.sbuf_tensor(self.uname("e_xtb"), [P, D], BF16))
            stg = [ps.enter_context(nc.sbuf_tensor(self.uname(f"e_stg{i}"), [P, 512], BF16))
                   for i in range(2)]
            sqg = [ps.enter_context(nc.sbuf_tensor(self.uname(f"e_sqg{i}"), [P, 512], BF16))
                   for i in range(2)]
            b = self.snap()
            with nc.Block() as blk:
                @blk.sync
                def _(sync):
                    for i in range(2 * NT):
                        if i >= 2:
                            sync.wait_ge(S["cv"].h, b["cv"] + i - 1)
                        tt, pc = divmod(i, 2)
                        self.dinc("wl", i, sync.dma_start(
                            xtf[i % 2][:], self.x0[tt * 128:(tt + 1) * 128,
                                                   pc * 2048:(pc + 1) * 2048]))

                @blk.scalar
                def _(act):
                    for tt in range(NT):
                        if tt >= 1:
                            act.wait_ge(S["tpg"].h, b["tpg"] + 8 * tt)
                        for pc in range(2):
                            i = tt * 2 + pc
                            self.dwait(act, "wl", i, b)
                            S["cv"].inc(act.activation(
                                xtb[:, pc * 2048:(pc + 1) * 2048],
                                xtf[i % 2][:], AF.Copy))
                        for g in range(8):
                            G = tt * 8 + g
                            act.wait_ge(S["wt"].h, b["wt"] + G + 1)
                            if G >= 2:
                                act.wait_ge(S["sst"].h, b["sst"] + G - 1)
                            S["aux"].inc(act.activation(
                                sqg[G % 2][:], stg[G % 2][:], AF.Square))
                        if tt % 4 == 3:
                            ts = tt // 4
                            act.wait_ge(S["sst"].h, b["sst"] + 8 * (tt + 1))
                            if ts >= 1:
                                act.wait_ge(S["cg"].h, b["cg"] + ts)
                                act.wait_ge(S["rbs"].h, b["rbs"] + 16 * 2 * ts)
                            S["ca"].inc(act.activation(self.RMU[0:1, :],
                                                       PSB[2][0:1, :], AF.Copy,
                                                       scale=1.0 / D))
                            act.wait_ge(S["ca"].h, S["ca"].n)
                            S["ca"].inc(act.activation(self.RSQ[0:1, :],
                                                       self.RMU[0:1, :], AF.Square))
                            act.wait_ge(S["cd"].h, b["cd"] + 1 * (ts + 1))
                            S["ca"].inc(act.activation(self.RVA[0:1, :],
                                                       self.RVA[0:1, :], AF.Ln,
                                                       bias=self.CF[0:1, 640:641]))
                            act.wait_ge(S["ca"].h, S["ca"].n)
                            S["ca"].inc(act.activation(self.RSB[0:1, :],
                                                       self.RVA[0:1, :], AF.Exp,
                                                       scale=-0.5))

                @blk.tensor
                def _(te):
                    for tt in range(NT):
                        te.wait_ge(S["cv"].h, b["cv"] + 2 * tt + 2)
                        if tt % 4 == 0 and tt > 0:
                            te.wait_ge(S["ca"].h, b["ca"] + 4 * (tt // 4))
                            te.wait_ge(S["cd"].h, b["cd"] + (tt // 4))
                        for g in range(8):
                            G = tt * 8 + g
                            if G >= 2:
                                te.wait_ge(S["wt"].h, b["wt"] + G - 1)
                            for m in range(4):
                                jj = g * 4 + m
                                inst = te.matmul(
                                    PSB[G % 2][:, m * 128:(m + 1) * 128],
                                    xtb[:, jj * 128:(jj + 1) * 128], self.ident,
                                    start=True, stop=True)
                            S["tpg"].inc(inst)
                            if g >= 1:
                                self._entry_stats(te, tt, g - 1, b, stg, sqg)
                        self._entry_stats(te, tt, 7, b, stg, sqg)

                @blk.vector
                def _(ve):
                    for tt in range(NT):
                        for g in range(8):
                            G = tt * 8 + g
                            ve.wait_ge(S["tpg"].h, b["tpg"] + G + 1)
                            if G >= 2:
                                ve.wait_ge(S["aux"].h, b["aux"] + G - 1)
                                self.dwait(ve, "st", G - 2, b)
                            S["wt"].inc(ve.tensor_copy(stg[G % 2][:], PSB[G % 2][:]))
                        if tt % 4 == 3:
                            ts = tt // 4
                            ve.wait_ge(S["ca"].h, b["ca"] + 4 * ts + 2)
                            S["cd"].inc(ve.scalar_tensor_tensor(
                                self.RVA[0:1, :], PSB[3][0:1, :], 1.0 / D,
                                self.RSQ[0:1, :], ALU.mult, ALU.subtract))
                            ve.wait_ge(S["ca"].h, b["ca"] + 4 * ts + 4)
                            S["cg"].inc(ve.tensor_mul(self.RMB[0:1, :],
                                                      self.RMU[0:1, :],
                                                      self.RSB[0:1, :]))

                @blk.gpsimd
                def _(g_):
                    for G in range(8 * NT):
                        g_.wait_ge(S["wt"].h, b["wt"] + G + 1)
                        tt, g = divmod(G, 8)
                        dst = self.xfm[0][4 * g:4 * g + 4, :,
                                          tt * 128:(tt + 1) * 128].transpose([1, 0, 2])
                        self.dinc("st", G, g_.dma_start(dst, stg[G % 2][:]))
                        if tt % 4 == 3 and g == 7:
                            ts = tt // 4
                            g_.wait_ge(S["cg"].h, b["cg"] + ts + 1)
                            S["rbs"].inc(g_.dma_start(
                                self.rows_scr[0:1, ts * TS:(ts + 1) * TS],
                                self.RSB[0:1, :]), 16)
                            S["rbs"].inc(g_.dma_start(
                                self.rows_scr[1:2, ts * TS:(ts + 1) * TS],
                                self.RMB[0:1, :]), 16)
                    g_.wait_ge(S["st"].h, S["st"].n)
                    g_.wait_ge(S["st2"].h, S["st2"].n)

    def _entry_stats(self, te, tt, g, b, stg, sqg):
        S, PSB = self.S, self.PSB
        G = tt * 8 + g
        te.wait_ge(S["wt"].h, b["wt"] + G + 1)
        te.wait_ge(S["aux"].h, b["aux"] + G + 1)
        col = (tt % 4) * 128
        first = (g == 0)
        last = (g == 7)
        for m in range(4):
            te.matmul(PSB[2][0:1, col:col + 128], self.onescol,
                      stg[G % 2][:, m * 128:(m + 1) * 128],
                      start=(first and m == 0 and tt % 4 == 0),
                      stop=(last and m == 3 and tt % 4 == 3))
        for m in range(4):
            inst = te.matmul(PSB[3][0:1, col:col + 128], self.onescol,
                             sqg[G % 2][:, m * 128:(m + 1) * 128],
                             start=(first and m == 0 and tt % 4 == 0),
                             stop=(last and m == 3 and tt % 4 == 3))
        S["sst"].inc(inst)

    def p0_small(self):
        nc, S, PSB = self.nc, self.S, self.PSB
        with ExitStack() as ps:
            stf = ps.enter_context(nc.sbuf_tensor(self.uname("s_stf"), [P, 1024], F32))
            stb = ps.enter_context(nc.sbuf_tensor(self.uname("s_stb"), [P, 1024], BF16))
            otb = ps.enter_context(nc.sbuf_tensor(self.uname("s_otb"), [P, 512], BF16))
            brf = ps.enter_context(nc.sbuf_tensor(self.uname("s_brf"), [1, T], F32))
            for strip in range(4):
                self._small_strip(self.amat[:, strip * 1024:(strip + 1) * 1024],
                                  stf, stb, None,
                                  self.atil[strip * 8:(strip + 1) * 8, :, :], 8)
            for strip in range(8):
                src = self.bmat[strip * 512:(strip + 1) * 512, :].rearrange(
                    "(a p) e -> p a e", p=128)
                self._small_strip(src, stf, stb, otb,
                                  self.btil[strip * 4:(strip + 1) * 4, :, :], 4)
            for s4 in range(4):
                with nc.Block() as blk:
                    @blk.sync
                    def _(sync):
                        S["qs"].inc(sync.dma_start(
                            stf[0:8, :], self.rwm[:, s4 * 1024:(s4 + 1) * 1024]), 16)
                        sync.wait_ge(S["qs"].h, S["qs"].n)
                with nc.Block() as blk:
                    @blk.scalar
                    def _(act):
                        act.activation(stb[0:8, :], stf[0:8, :], AF.Copy)
                with nc.Block() as blk:
                    @blk.tensor
                    def _(te):
                        for m in range(8):
                            te.matmul(PSB[0][:, m * 64:m * 64 + 8],
                                      stb[0:8, m * 128:(m + 1) * 128],
                                      self.ident[0:8, 0:8], start=True, stop=True)
                with nc.Block() as blk:
                    @blk.vector
                    def _(ve):
                        for m in range(8):
                            kc = s4 * 8 + m
                            ve.tensor_copy(self.RWT[:, kc * 8:(kc + 1) * 8],
                                           PSB[0][:, m * 64:m * 64 + 8])
            with nc.Block() as blk:
                @blk.sync
                def _(sync):
                    S["qs"].inc(sync.dma_start(brf[:], self.brow0), 16)
                    sync.wait_ge(S["qs"].h, S["qs"].n)
            with nc.Block() as blk:
                @blk.scalar
                def _(act):
                    act.activation(self.BRB0[:], brf[:], AF.Copy)
            with nc.Block() as blk:
                @blk.sync
                def _(sync):
                    S["qs"].inc(sync.dma_start(brf[:], self.brow1), 16)
                    sync.wait_ge(S["qs"].h, S["qs"].n)
            with nc.Block() as blk:
                @blk.scalar
                def _(act):
                    act.activation(self.BRB1[:], brf[:], AF.Copy)

    def _small_strip(self, src, stf, stb, otb, dst_scr, ntile):
        nc, S, PSB = self.nc, self.S, self.PSB
        w = ntile * 128
        out = stb if otb is None else otb
        with nc.Block() as blk:
            @blk.sync
            def _(sync):
                S["qs"].inc(sync.dma_start(stf[:, 0:w], src), 16)
                sync.wait_ge(S["qs"].h, S["qs"].n)
        with nc.Block() as blk:
            @blk.scalar
            def _(act):
                act.activation(stb[:, 0:w], stf[:, 0:w], AF.Copy)
        with nc.Block() as blk:
            @blk.tensor
            def _(te):
                for m in range(ntile):
                    te.matmul(PSB[m // 4 % 2][:, (m % 4) * 128:(m % 4 + 1) * 128],
                              stb[:, m * 128:(m + 1) * 128], self.ident,
                              start=True, stop=True)
        with nc.Block() as blk:
            @blk.vector
            def _(ve):
                for q in range((ntile + 3) // 4):
                    n = min(4, ntile - q * 4) * 128
                    ve.tensor_copy(out[:, q * 512:q * 512 + n], PSB[q % 2][:, 0:n])
        with nc.Block() as blk:
            @blk.sync
            def _(sync):
                S["qs"].inc(sync.dma_start(
                    dst_scr.transpose([1, 0, 2]),
                    out[:, 0:w].rearrange("p (a c) -> p a c", c=128)), 16)
                sync.wait_ge(S["qs"].h, S["qs"].n)

    def _bcast_rows(self):
        nc, S = self.nc, self.S
        with nc.Block() as blk:
            @blk.sync
            def _(sync):
                if S["rbs"].n:
                    sync.wait_ge(S["rbs"].h, S["rbs"].n)
                if S["qs"].n:
                    sync.wait_ge(S["qs"].h, S["qs"].n)
                S["bc"].inc(sync.dma_start(
                    self.RSTD[:], self.rows_scr[0:1, :].broadcast_to([P, T])), 16)
                S["bc"].inc(sync.dma_start(
                    self.MURS[:], self.rows_scr[1:2, :].broadcast_to([P, T])), 16)
                sync.wait_ge(S["bc"].h, S["bc"].n)

    def p1_lnq(self, xsrc):
        nc, S = self.nc, self.S
        self._bcast_rows()
        with ExitStack() as ps:
            xst = [ps.enter_context(nc.sbuf_tensor(self.uname(f"l_x{i}"), [P, T], BF16))
                   for i in range(2)]
            b = self.snap()
            with nc.Block() as blk:
                @blk.gpsimd
                def _(g_):
                    for ch in range(NCH):
                        if ch >= 2:
                            g_.wait_ge(S["cd"].h, b["cd"] + 3 * (ch - 1))
                        self.dinc("xc", ch, g_.dma_start(xst[ch % 2][:],
                                                         xsrc[ch, :, :]))

                @blk.vector
                def _(ve):
                    for ch in range(NCH):
                        a = self.ACT_A[:, ch * T:(ch + 1) * T]
                        self.dwait(ve, "xc", ch, b)
                        S["cd"].inc(ve.tensor_mul(a, xst[ch % 2][:], self.RSTD[:]))
                        ve.wait_ge(S["cd"].h, S["cd"].n)
                        S["cd"].inc(ve.tensor_sub(a, a, self.MURS[:]))
                        ve.wait_ge(S["cd"].h, S["cd"].n)
                        S["cd"].inc(ve.reduce_sum(
                            self.QSUM[:, ch:ch + 1],
                            self.ACT_A[:, ch * T + QL:ch * T + QR], axis=AX.X))

    def p1_router(self):
        nc, S, PSB = self.nc, self.S, self.PSB
        with nc.Block() as blk:
            @blk.scalar
            def _(act):
                act.activation(self.QBF[:], self.QSUM[:], AF.Copy)
        with nc.Block() as blk:
            @blk.tensor
            def _(te):
                for kc in range(NCH):
                    te.matmul(PSB[0][0:1, 0:8], self.QBF[:, kc:kc + 1],
                              self.RWT[:, kc * 8:(kc + 1) * 8],
                              start=(kc == 0), stop=(kc == NCH - 1))
        with nc.Block() as blk:
            @blk.vector
            def _(ve):
                S["cd"].inc(ve.scalar_tensor_tensor(
                    self.SROW[0:1, 0:8], PSB[0][0:1, 0:8], 1.0 / NQ,
                    self.RBR8[:], ALU.mult, ALU.add))
                ve.wait_ge(S["cd"].h, S["cd"].n)
                S["cd"].inc(ve.reduce_max(self.SROW[0:1, 8:9],
                                          self.SROW[0:1, 0:8], axis=AX.X))
                ve.wait_ge(S["cd"].h, S["cd"].n)
                S["cd"].inc(ve.tensor_scalar_mul(self.SROW[0:1, 9:10],
                                                 self.SROW[0:1, 8:9], -1.0))
        with nc.Block() as blk:
            @blk.scalar
            def _(act):
                act.activation(self.SROW[0:1, 16:24], self.SROW[0:1, 0:8],
                               AF.Exp, bias=self.SROW[0:1, 9:10])
        with nc.Block() as blk:
            @blk.vector
            def _(ve):
                S["cd"].inc(ve.reduce_sum(self.SROW[0:1, 10:11],
                                          self.SROW[0:1, 16:24], axis=AX.X))
                ve.wait_ge(S["cd"].h, S["cd"].n)
                S["cd"].inc(ve.reciprocal(self.SROW[0:1, 11:12],
                                          self.SROW[0:1, 10:11]))
                ve.wait_ge(S["cd"].h, S["cd"].n)
                S["cd"].inc(ve.tensor_scalar(self.SROW[0:1, 24:32],
                                             self.SROW[0:1, 16:24],
                                             self.SROW[0:1, 11:12], 2.0,
                                             ALU.mult, ALU.mult))
        with nc.Block() as blk:
            @blk.scalar
            def _(act):
                act.activation(self.PBF[:], self.SROW[0:1, 24:32], AF.Copy)
        with nc.Block() as blk:
            @blk.tensor
            def _(te):
                te.matmul(PSB[1][0:8, 0:1], self.PBF[:], self.CROW[0:1, 0:1],
                          start=True, stop=True)
        with nc.Block() as blk:
            @blk.vector
            def _(ve):
                ve.tensor_copy(self.PCOL[:], PSB[1][0:8, 0:1])
        with nc.Block() as blk:
            @blk.tensor
            def _(te):
                te.matmul(PSB[2][:, 0:1], self.exp8, self.PCOL[:],
                          start=True, stop=True)
        with nc.Block() as blk:
            @blk.vector
            def _(ve):
                ve.tensor_copy(self.RTBC[:], PSB[2][:, 0:1])

    def p1_tsc(self):
        nc, S, PSB = self.nc, self.S, self.PSB
        with ExitStack() as ps:
            atl = ps.enter_context(nc.sbuf_tensor(self.uname("t_atl"), [P, NCH * 128], BF16))
            b = self.snap()
            with nc.Block() as blk:
                @blk.sync
                def _(sync):
                    S["qs"].inc(sync.dma_start(
                        atl[:], self.atil[:, :, :].transpose([1, 0, 2])), 16)
                    sync.wait_ge(S["qs"].h, S["qs"].n)
            with nc.Block() as blk:
                @blk.tensor
                def _(te):
                    for ts in range(NTS):
                        if ts >= 2:
                            te.wait_ge(S["ev"].h, b["ev"] + ts - 1)
                        for kc in range(NCH):
                            inst = te.matmul(
                                PSB[3 + ts % 2][:],
                                atl[:, kc * 128:(kc + 1) * 128],
                                self.ACT_A[:, kc * T + ts * TS:kc * T + (ts + 1) * TS],
                                start=(kc == 0), stop=(kc == NCH - 1))
                        S["mm"].inc(inst)

                @blk.vector
                def _(ve):
                    for ts in range(NTS):
                        ve.wait_ge(S["mm"].h, b["mm"] + ts + 1)
                        S["ev"].inc(ve.tensor_scalar_mul(
                            self.TSC[:, ts * TS:(ts + 1) * TS],
                            PSB[3 + ts % 2][:], self.RTBC[:, 0:1]))

    def p_stats(self, xsrc):
        nc, S, PSB = self.nc, self.S, self.PSB
        with ExitStack() as ps:
            xst = [ps.enter_context(nc.sbuf_tensor(self.uname(f"st_x{i}"), [P, T], BF16))
                   for i in range(2)]
            sqs = [ps.enter_context(nc.sbuf_tensor(self.uname(f"st_q{i}"), [P, T], BF16))
                   for i in range(2)]
            b = self.snap()
            with nc.Block() as blk:
                @blk.gpsimd
                def _(g_):
                    for ch in range(NCH):
                        if ch >= 2:
                            g_.wait_ge(S["sst"].h, b["sst"] + ch - 1)
                        self.dinc("xc", ch, g_.dma_start(xst[ch % 2][:], xsrc[ch, :, :]))

                @blk.scalar
                def _(act):
                    for ch in range(NCH):
                        self.dwait(act, "xc", ch, b)
                        if ch >= 2:
                            act.wait_ge(S["sst"].h, b["sst"] + ch - 1)
                        S["aux"].inc(act.activation(sqs[ch % 2][:], xst[ch % 2][:],
                                                    AF.Square))
                    for ts in range(NTS):
                        act.wait_ge(S["sst"].h, b["sst"] + NCH)
                        if ts >= 1:
                            act.wait_ge(S["cd"].h, b["cd"] + 2 * ts)
                            act.wait_ge(S["qs"].h, b["qs"] + 16 * 2 * ts)
                        S["ca"].inc(act.activation(self.RMU[0:1, :],
                                                   PSB[ts][0:1, :], AF.Copy,
                                                   scale=1.0 / D))
                        act.wait_ge(S["ca"].h, S["ca"].n)
                        S["ca"].inc(act.activation(self.RSQ[0:1, :],
                                                   self.RMU[0:1, :], AF.Square))
                        act.wait_ge(S["cd"].h, b["cd"] + 2 * ts + 1)
                        S["ca"].inc(act.activation(self.RVA[0:1, :],
                                                   self.RVA[0:1, :], AF.Ln,
                                                   bias=self.CF[0:1, 640:641]))
                        act.wait_ge(S["ca"].h, S["ca"].n)
                        S["ca"].inc(act.activation(self.RSB[0:1, :],
                                                   self.RVA[0:1, :], AF.Exp,
                                                   scale=-0.5))

                @blk.tensor
                def _(te):
                    for ch in range(NCH):
                        self.dwait(te, "xc", ch, b)
                        te.wait_ge(S["aux"].h, b["aux"] + ch + 1)
                        for ts in range(NTS):
                            te.matmul(PSB[ts][0:1, :], self.onescol,
                                      xst[ch % 2][:, ts * TS:(ts + 1) * TS],
                                      start=(ch == 0), stop=(ch == NCH - 1))
                        for ts in range(NTS):
                            inst = te.matmul(PSB[4 + ts][0:1, :], self.onescol,
                                             sqs[ch % 2][:, ts * TS:(ts + 1) * TS],
                                             start=(ch == 0), stop=(ch == NCH - 1))
                        S["sst"].inc(inst)

                @blk.vector
                def _(ve):
                    for ts in range(NTS):
                        ve.wait_ge(S["ca"].h, b["ca"] + 4 * ts + 2)
                        S["cd"].inc(ve.scalar_tensor_tensor(
                            self.RVA[0:1, :], PSB[4 + ts][0:1, :], 1.0 / D,
                            self.RSQ[0:1, :], ALU.mult, ALU.subtract))
                        ve.wait_ge(S["ca"].h, b["ca"] + 4 * ts + 4)
                        S["cd"].inc(ve.tensor_mul(self.RMB[0:1, :],
                                                  self.RMU[0:1, :],
                                                  self.RSB[0:1, :]))

                @blk.sync
                def _(sync):
                    for ts in range(NTS):
                        sync.wait_ge(S["cd"].h, b["cd"] + 2 * ts + 2)
                        S["qs"].inc(sync.dma_start(
                            self.rows_scr[0:1, ts * TS:(ts + 1) * TS],
                            self.RSB[0:1, :]), 16)
                        S["qs"].inc(sync.dma_start(
                            self.rows_scr[1:2, ts * TS:(ts + 1) * TS],
                            self.RMB[0:1, :]), 16)
                    sync.wait_ge(S["qs"].h, S["qs"].n)

    def p4_mix(self, pn, xsrc):
        nc, S = self.nc, self.S
        with ExitStack() as ps:
            xst = [ps.enter_context(nc.sbuf_tensor(self.uname(f"m_x{i}"), [P, T], BF16))
                   for i in range(2)]
            lnst = [ps.enter_context(nc.sbuf_tensor(self.uname(f"m_l{i}"), [P, T], BF16))
                    for i in range(2)]
            mab = [ps.enter_context(nc.sbuf_tensor(self.uname(f"m_a{i}"), [P, T], BF16))
                   for i in range(2)]
            mbb = [ps.enter_context(nc.sbuf_tensor(self.uname(f"m_b{i}"), [P, T], BF16))
                   for i in range(2)]
            tmm = ps.enter_context(nc.sbuf_tensor(self.uname("m_tm"), [P, 32], F32))
            b = self.snap()
            tmc = self.tm[pn]
            with nc.Block() as blk:
                @blk.gpsimd
                def _(g_):
                    for ch in range(NCH):
                        if ch >= 2:
                            g_.wait_ge(S["cd"].h, b["cd"] + 3 * (ch - 2) + 2)
                        self.dinc("xc", ch, g_.dma_start(xst[ch % 2][:],
                                                         xsrc[ch, :, :]))

                @blk.vector
                def _(ve):
                    S["cd"].inc(ve.tensor_scalar(tmm[:], tmc[:], -1.0, 1.0,
                                                 ALU.mult, ALU.add))

                    def combine(ch2):
                        a = self.ACT_A[:, ch2 * T:(ch2 + 1) * T]
                        ve.wait_ge(S["ca"].h, b["ca"] + ch2 + 1)
                        ma, mb = mab[ch2 % 2], mbb[ch2 % 2]
                        ve.tensor_add(a[:, 1:T], ma[:, 1:T], mb[:, 0:T - 1])
                        S["cd"].inc(ve.tensor_copy(a[:, 0:1], ma[:, 0:1]))

                    for ch in range(NCH):
                        ln = lnst[ch % 2]
                        self.dwait(ve, "xc", ch, b)
                        if ch >= 2:
                            ve.wait_ge(S["ca"].h, b["ca"] + ch - 1)
                        S["cd"].inc(ve.tensor_mul(ln[:], xst[ch % 2][:],
                                                  self.RSTD[:]))
                        ve.wait_ge(S["cd"].h, S["cd"].n)
                        S["cd"].inc(ve.tensor_sub(ln[:], ln[:], self.MURS[:]))
                        if ch >= 1:
                            combine(ch - 1)
                    combine(NCH - 1)

                @blk.scalar
                def _(act):
                    for ch in range(NCH):
                        act.wait_ge(S["cd"].h, b["cd"] + 3 * ch + 3)
                        if ch >= 2:
                            act.wait_ge(S["cd"].h, b["cd"] + 3 * (ch - 1) + 1)
                        act.activation(mab[ch % 2][:], lnst[ch % 2][:], AF.Copy,
                                       scale=tmc[:, ch:ch + 1])
                        S["ca"].inc(act.activation(mbb[ch % 2][:], lnst[ch % 2][:],
                                                   AF.Copy, scale=tmm[:, ch:ch + 1]))

    def p2_main(self, mode, Wsrc, xres=None, xdst=None, dst_scr=None,
                silu=False, save_wt=False):
        nc, S, PSB = self.nc, self.S, self.PSB
        lora = mode in ("ffn", "ffn2")
        from_scr = (mode == "ffn2")
        with ExitStack() as ps:
            wT = [ps.enter_context(nc.sbuf_tensor(self.uname(f"w_t{i}"), [P, D], BF16))
                  for i in range(2)]
            if not from_scr:
                wfp = [ps.enter_context(nc.sbuf_tensor(self.uname(f"w_f{i}"), [P, 1024], F32))
                       for i in range(2)]
                wbf = ps.enter_context(nc.sbuf_tensor(self.uname("w_b"), [P, D], BF16))
            if lora:
                btl = [ps.enter_context(nc.sbuf_tensor(self.uname(f"w_bt{i}"), [P, 128], BF16))
                       for i in range(2)]
            if mode != "proj":
                xcb = [ps.enter_context(nc.sbuf_tensor(self.uname(f"w_xc{i}"), [P, TS], BF16))
                       for i in range(2)]
            if mode == "ffn2":
                fin = [ps.enter_context(nc.sbuf_tensor(self.uname(f"w_fi{i}"), [P, TS], F32))
                       for i in range(2)]
                yro = [ps.enter_context(nc.sbuf_tensor(self.uname(f"w_yr{i}"), [P, TS], F32))
                       for i in range(2)]
            else:
                ob = [ps.enter_context(nc.sbuf_tensor(self.uname(f"w_ob{i}"), [P, TS], BF16))
                      for i in range(2)]
            b = self.snap()

            with nc.Block() as blk:
                @blk.sync
                def _(sync):
                    if from_scr:
                        for och in range(NCH):
                            if och >= 2:
                                sync.wait_ge(S["mm"].h, b["mm"] + 4 * (och - 1))
                            self.dinc("wta", och, sync.dma_start(
                                wT[och % 2][:], self.wtf[och, :, :]))
                    else:
                        for och in range(NCH):
                            for pc in range(4):
                                i = och * 4 + pc
                                if i >= 2:
                                    sync.wait_ge(S["cv"].h, b["cv"] + i - 1)
                                self.dinc("wl", i, sync.dma_start(
                                    wfp[i % 2][:],
                                    Wsrc[och * 128:(och + 1) * 128,
                                         pc * 1024:(pc + 1) * 1024]))

                if not from_scr:
                    @blk.scalar
                    def _(act):
                        def silu1(och2, ts):
                            I = och2 * 4 + ts
                            act.wait_ge(S["mm"].h, b["mm"] + I + 1)
                            if I >= 2:
                                self.dwait(act, "st", I - 2, b)
                            S["ev"].inc(act.activation(ob[I % 2][:], PSB[I % 4][:],
                                                       AF.Silu))
                        for och in range(NCH):
                            if och >= 1:
                                act.wait_ge(S["tpg"].h, b["tpg"] + 8 * och)
                            for pc in range(4):
                                i = och * 4 + pc
                                self.dwait(act, "wl", i, b)
                                S["cv"].inc(act.activation(
                                    wbf[:, pc * 1024:(pc + 1) * 1024],
                                    wfp[i % 2][:], AF.Copy))
                                if silu and och >= 2:
                                    silu1(och - 2, pc)
                        if silu:
                            for ts in range(NTS):
                                silu1(NCH - 2, ts)
                            for ts in range(NTS):
                                silu1(NCH - 1, ts)

                @blk.tensor
                def _(te):
                    def emit_tp(och):
                        te.wait_ge(S["cv"].h, b["cv"] + 4 * och + 4)
                        for g in range(8):
                            G = och * 8 + g
                            if G >= 2:
                                te.wait_ge(S["wt"].h, b["wt"] + G - 1)
                            for m in range(4):
                                jj = g * 4 + m
                                inst = te.matmul(
                                    PSB[4 + G % 2][:, m * 128:(m + 1) * 128],
                                    wbf[:, jj * 128:(jj + 1) * 128], self.ident,
                                    start=True, stop=True)
                            S["tpg"].inc(inst)

                    def emit_mm(och):
                        if not from_scr:
                            te.wait_ge(S["wt"].h, b["wt"] + 8 * (och + 1))
                        else:
                            self.dwait(te, "wta", och, b)
                        if lora:
                            self.dwait(te, "bt", och, b)
                        for ts in range(NTS):
                            I = och * 4 + ts
                            if I >= 4:
                                te.wait_ge(S["ev"].h, b["ev"] + I - 3)
                            for kc in range(NCH):
                                inst = te.matmul(
                                    PSB[I % 4][:],
                                    wT[och % 2][:, kc * 128:(kc + 1) * 128],
                                    self.ACT_A[:, kc * T + ts * TS:kc * T + (ts + 1) * TS],
                                    start=(kc == 0),
                                    stop=(kc == NCH - 1 and not lora))
                            if lora:
                                te.matmul(PSB[I % 4][:], btl[och % 2][:],
                                          self.TSC[:, ts * TS:(ts + 1) * TS],
                                          start=False, stop=False)
                                brow_src = self.BRB0 if och < 16 else self.BRB1
                                cc = (och % 16) * 128
                                inst = te.matmul(PSB[I % 4][:],
                                                 brow_src[0:1, cc:cc + 128],
                                                 self.CROW[0:1, :],
                                                 start=False, stop=True)
                            S["mm"].inc(inst)
                            if mode == "ffn2" and I >= 1:
                                self._exit_tp(te, I - 1, b, fin)

                    if from_scr:
                        for och in range(NCH):
                            emit_mm(och)
                    else:
                        emit_tp(0)
                        for och in range(NCH):
                            if och + 1 < NCH:
                                emit_tp(och + 1)
                            emit_mm(och)
                    if mode == "ffn2":
                        self._exit_tp(te, 4 * NCH - 1, b, fin)

                @blk.vector
                def _(ve):
                    def wt_evacs(och):
                        for g in range(8):
                            G = och * 8 + g
                            ve.wait_ge(S["tpg"].h, b["tpg"] + G + 1)
                            if och >= 2 and g == 0:
                                ve.wait_ge(S["mm"].h, b["mm"] + 4 * (och - 1))
                                if save_wt:
                                    self.dwait(ve, "ws", och - 2, b)
                            S["wt"].inc(ve.tensor_copy(
                                wT[och % 2][:, g * 512:(g + 1) * 512],
                                PSB[4 + G % 2][:]))
                    if not from_scr:
                        wt_evacs(0)
                    for och in range(NCH):
                        if not from_scr and och + 1 < NCH:
                            wt_evacs(och + 1)
                        if silu:
                            continue
                        for ts in range(NTS):
                            I = och * 4 + ts
                            ve.wait_ge(S["mm"].h, b["mm"] + I + 1)
                            if mode != "proj":
                                self.dwait(ve, "xc", I, b)
                            if I >= 2 and mode != "ffn2":
                                self.dwait(ve, "st", I - 2, b)
                            if mode == "ffn":
                                S["ev"].inc(ve.scalar_tensor_tensor(
                                    ob[I % 2][:], PSB[I % 4][:], 0.5, xcb[I % 2][:],
                                    ALU.mult, ALU.add))
                            elif mode == "outproj":
                                S["ev"].inc(ve.tensor_add(
                                    ob[I % 2][:], PSB[I % 4][:], xcb[I % 2][:]))
                            elif mode == "proj":
                                S["ev"].inc(ve.tensor_copy(ob[I % 2][:],
                                                           PSB[I % 4][:]))
                            else:
                                if I >= 2:
                                    ve.wait_ge(S["xt"].h, b["xt"] + I - 1)
                                S["ev"].inc(ve.scalar_tensor_tensor(
                                    fin[I % 2][:], PSB[I % 4][:], 0.5, xcb[I % 2][:],
                                    ALU.mult, ALU.add))
                        if mode == "ffn2":
                            for ts in range(NTS):
                                I = och * 4 + ts
                                ve.wait_ge(S["xt"].h, b["xt"] + I + 1)
                                if I >= 2:
                                    self.dwait(ve, "ys", I - 2, b)
                                S["yr"].inc(ve.tensor_copy(yro[I % 2][:],
                                                           PSB[4 + I % 2][:]))

                @blk.gpsimd
                def _(g_):
                    def ystore(I2):
                        och2, ts2 = divmod(I2, 4)
                        g_.wait_ge(S["yr"].h, b["yr"] + I2 + 1)
                        dsty = self.y[ts2 * TS:(ts2 + 1) * TS,
                                      och2 * 128:(och2 + 1) * 128].rearrange(
                                          "(jj p) o -> p jj o", p=128)
                        self.dinc("ys", I2, g_.dma_start(
                            dsty, yro[I2 % 2][:].rearrange(
                                "p (jj o) -> p jj o", o=128)))
                    for och in range(NCH):
                        if lora:
                            if och >= 2:
                                g_.wait_ge(S["mm"].h, b["mm"] + 4 * (och - 1))
                            self.dinc("bt", och, g_.dma_start(btl[och % 2][:],
                                                              self.btil[och, :, :]))
                        if save_wt:
                            g_.wait_ge(S["wt"].h, b["wt"] + 8 * (och + 1))
                            self.dinc("ws", och, g_.dma_start(self.wtf[och, :, :],
                                                              wT[och % 2][:]))
                        for ts in range(NTS):
                            I = och * 4 + ts
                            if mode != "proj":
                                if I >= 2:
                                    g_.wait_ge(S["ev"].h, b["ev"] + I - 1)
                                self.dinc("xc", I, g_.dma_start(
                                    xcb[I % 2][:],
                                    xres[och, :, ts * TS:(ts + 1) * TS]))
                            if mode in ("ffn", "outproj"):
                                g_.wait_ge(S["ev"].h, b["ev"] + I + 1)
                                self.dinc("st", I, g_.dma_start(
                                    xdst[och, :, ts * TS:(ts + 1) * TS],
                                    ob[I % 2][:]))
                            elif mode == "proj":
                                g_.wait_ge(S["ev"].h, b["ev"] + I + 1)
                                self.dinc("st", I, g_.dma_start(
                                    dst_scr[och, :, ts * TS:(ts + 1) * TS],
                                    ob[I % 2][:]))
                            else:
                                if och >= 1:
                                    ystore((och - 1) * 4 + ts)
                    if mode == "ffn2":
                        for ts in range(NTS):
                            ystore((NCH - 1) * 4 + ts)

    def _silu_evac(self, act, och, b, ob):
        S, PSB = self.S, self.PSB
        for ts in range(NTS):
            I = och * 4 + ts
            act.wait_ge(S["mm"].h, b["mm"] + I + 1)
            if I >= 2:
                self.dwait(act, "st", I - 2, b)
            S["ev"].inc(act.activation(ob[I % 2][:], PSB[I % 2][:], AF.Silu))

    def _exit_tp(self, te, I, b, fin):
        S, PSB = self.S, self.PSB
        te.wait_ge(S["ev"].h, b["ev"] + I + 1)
        if I >= 2:
            te.wait_ge(S["yr"].h, b["yr"] + I - 1)
        for m in range(4):
            inst = te.matmul(PSB[4 + I % 2][:, m * 128:(m + 1) * 128],
                             fin[I % 2][:, m * 128:(m + 1) * 128], self.identf,
                             start=True, stop=True)
        S["xt"].inc(inst)

    def p5a_wkv(self):
        nc, S, PSB = self.nc, self.S, self.PSB
        with ExitStack() as ps:
            kvb = {nm: [self.ACT_A[:, (2 * j + i) * T:(2 * j + i + 1) * T]
                        for i in range(2)]
                   for j, nm in enumerate("kvr")}
            obb = ps.enter_context(nc.sbuf_tensor(self.uname("k_ob"), [P, T], BF16))
            PM = [ps.enter_context(nc.sbuf_tensor(self.uname(f"k_pm{i}"), [P, 10 * 128], BF16))
                  for i in range(2)]
            def frow(hc, s):
                base = 6 + (hc % 2) * 11 + s
                return self.ACT_A[:, base * T:(base + 1) * T]
            UM = [ps.enter_context(nc.sbuf_tensor(self.uname(f"k_um{i}"), [P, 64], BF16))
                  for i in range(2)]
            vTT = [ps.enter_context(nc.sbuf_tensor(self.uname(f"k_vt{i}"), [P, 256], BF16))
                   for i in range(2)]
            ASB = [ps.enter_context(nc.sbuf_tensor(self.uname(f"k_as{i}"), [P, 256], BF16))
                   for i in range(2)]
            TMPU = [ps.enter_context(nc.sbuf_tensor(self.uname(f"k_tu{i}"), [P, 128], BF16))
                    for i in range(2)]
            S0 = ps.enter_context(nc.sbuf_tensor(self.uname("k_s0"), [P, 64], F32))
            S0b = ps.enter_context(nc.sbuf_tensor(self.uname("k_s0b"), [P, 64], BF16))
            LNW = [ps.enter_context(nc.sbuf_tensor(self.uname(f"k_lnw{i}"), [P, 10], F32))
                   for i in range(2)]
            b = self.snap()

            def X(hc, c):
                return hc * NTC + c

            with nc.Block() as blk:
                @blk.sync
                def _(sync):
                    for hc in range(NHC):
                        if hc >= 2:
                            sync.wait_ge(S["uf"].h, b["uf"] + (hc - 1) * NTC)
                        for nm in "kvr":
                            self.dinc("kv", hc, sync.dma_start(
                                kvb[nm][hc % 2][:], self.kvrg[nm][hc, :, :]))
                        if hc >= 1:
                            sync.wait_ge(S["uf"].h, b["uf"] + hc * NTC)
                            S["ob"].inc(sync.dma_start(
                                self.obscr[hc - 1, :, :], obb[:]), 16)
                    sync.wait_ge(S["uf"].h, b["uf"] + NHC * NTC)
                    S["ob"].inc(sync.dma_start(self.obscr[NHC - 1, :, :], obb[:]), 16)
                    sync.wait_ge(S["ob"].h, S["ob"].n)

                @blk.scalar
                def _(act):
                    def pm_build(h):
                        L = LNW[h % 2]
                        S["ax2"].inc(act.activation(L[:, 0:1],
                                                    self.tdv[:, h:h + 1], AF.Exp))
                        act.wait_ge(S["cd"].h, b["cd"] + 3 * (h + 1))
                        if h >= 2:
                            act.wait_ge(S["pp"].h, b["pp"] + h - 1)
                            act.wait_ge(S["dp"].h, b["dp"] + h - 1)
                        pm = PM[h % 2]
                        act.activation(pm[:, 0:128], self.iota[128], AF.Exp,
                                       scale=L[:, 1:2])
                        act.activation(pm[:, 128:256], self.iota[64], AF.Exp,
                                       scale=L[:, 1:2])
                        act.activation(pm[:, 256:384], self.iota[32], AF.Exp,
                                       scale=L[:, 1:2])
                        act.activation(pm[:, 384:512], self.iota[16], AF.Exp,
                                       scale=L[:, 1:2])
                        act.activation(pm[:, 512:640], self.iota[16], AF.Exp,
                                       scale=L[:, 2:3])
                        act.activation(pm[:, 640:768], self.iota[128], AF.Exp,
                                       scale=L[:, 0:1], bias=L[:, 4:5])
                        act.activation(pm[:, 768:896], self.iota[64], AF.Exp,
                                       scale=L[:, 0:1], bias=L[:, 5:6])
                        act.activation(pm[:, 896:1024], self.iota[32], AF.Exp,
                                       scale=L[:, 0:1], bias=L[:, 6:7])
                        act.activation(pm[:, 1024:1152], self.iota[16], AF.Exp,
                                       scale=L[:, 0:1], bias=L[:, 7:8])
                        act.activation(pm[:, 1152:1280], self.iota[16], AF.Exp,
                                       scale=L[:, 3:4], bias=L[:, 3:4])
                        S["pmb"].inc(act.activation(L[:, 8:9], L[:, 1:2],
                                                    AF.Exp, scale=128.0))
                    pm_build(0)
                    for hc in range(NHC):
                        act.wait_ge(S["sbm"].h, b["sbm"] + hc * (NTC + 1))
                        S["sbm"].inc(act.memzero(S0b[:]))
                        for c in range(NTC):
                            Xc = X(hc, c)
                            act.wait_ge(S["tt"].h, b["tt"] + Xc + 1)
                            act.activation(vTT[c % 2][:, 0:128],
                                           PSB[c % 2][:, 0:128], AF.Copy)
                            S["ae"].inc(act.activation(vTT[c % 2][:, 128:256],
                                                       PSB[c % 2][:, 128:256],
                                                       AF.Copy))
                            if c == 0 and hc + 1 < NHC:
                                pm_build(hc + 1)
                            act.wait_ge(S["uf"].h, b["uf"] + Xc + 1)
                            S["sbm"].inc(act.activation(S0b[:], S0[:], AF.Copy))

                @blk.gpsimd
                def _(g_):
                    for hc in range(NHC):
                        g_.wait_ge(S["pmb"].h, b["pmb"] + hc + 1)
                        nmkv = "kv" if hc % 2 == 0 else "kv2"
                        g_.wait_ge(S[nmkv].h, b[nmkv] + 48 * (hc // 2 + 1))
                        if hc >= 2:
                            g_.wait_ge(S["om"].h, b["om"] + (hc - 1) * NTC)
                        pm = PM[hc % 2]
                        g_.tensor_scalar_mul(UM[hc % 2][:], self.ones64,
                                             self.faaaa[:, hc:hc + 1])
                        kfull = kvb["k"][hc % 2][:]
                        for s, pmo in [(5, 768), (6, 896), (7, 1024), (9, 640)]:
                            big = pm[:, pmo:pmo + 128].unsqueeze(1).broadcast_to(
                                [P, NTC, 128])
                            ov = frow(hc, s).rearrange("p (a c) -> p a c", c=128)
                            kv_ = kfull.rearrange("p (a c) -> p a c", c=128)
                            inst = g_.tensor_tensor(ov, kv_, big, ALU.mult)
                        S["pp"].inc(inst)

                def emit_G1(te, hc, c):
                    Xc = X(hc, c)
                    cs = slice(c * C, (c + 1) * C)
                    if c == 0:
                        te.wait_ge(S["pp"].h, b["pp"] + hc + 1)
                        te.wait_ge(S["dp"].h, b["dp"] + hc + 1)
                    if Xc >= 2:
                        te.wait_ge(S["ae"].h, b["ae"] + Xc - 1)
                        te.wait_ge(S["cg"].h, b["cg"] + Xc - 1)
                        te.wait_ge(S["am"].h, b["am"] + Xc - 1)
                    vsl = kvb["v"][hc % 2][:, cs]
                    rt = frow(hc, 0)[:, cs]
                    r64, r32, r16 = (frow(hc, 1)[:, cs], frow(hc, 2)[:, cs],
                                     frow(hc, 3)[:, cs])
                    rcl, kcl = frow(hc, 4)[:, cs], frow(hc, 8)[:, cs]
                    k64, k32, k16 = (frow(hc, 5)[:, cs], frow(hc, 6)[:, cs],
                                     frow(hc, 7)[:, cs])
                    kg = frow(hc, 9)[:, cs]
                    rk = frow(hc, 10)[:, cs]
                    te.matmul(PSB[c % 2][:, 0:128], vsl, self.ident,
                              start=True, stop=True)
                    te.matmul(PSB[c % 2][:, 128:256], kg, self.ident,
                              start=True, stop=True)
                    pA = PSB[2 + c % 2]
                    for h in range(2):
                        po = 64 * h
                        co = 128 * h
                        te.matmul(pA[:, co:co + 128], kcl[po:po + 64, :],
                                  rcl[po:po + 64, :], start=True, stop=True,
                                  tile_position=(po, 0))
                        te.matmul(pA[0:64, co + 64:co + 128], k64[po:po + 64, 0:64],
                                  r64[po:po + 64, 64:128],
                                  start=True, stop=True, tile_position=(po, 0))
                        te.matmul(pA[0:32, co + 32:co + 64], k32[po:po + 64, 0:32],
                                  r32[po:po + 64, 32:64],
                                  start=True, stop=True, tile_position=(po, 0))
                        te.matmul(pA[64:96, co + 96:co + 128], k32[po:po + 64, 64:96],
                                  r32[po:po + 64, 96:128],
                                  start=True, stop=True, tile_position=(po, 64))
                        for m in range(4):
                            j0 = 32 * m
                            te.matmul(pA[j0:j0 + 16, co + j0 + 16:co + j0 + 32],
                                      k16[po:po + 64, j0:j0 + 16],
                                      r16[po:po + 64, j0 + 16:j0 + 32],
                                      start=True, stop=True,
                                      tile_position=(po, j0))
                        inst = te.matmul(PSB[4 + c % 2][po:po + 64, 0:128],
                                         UM[hc % 2][po:po + 64, :],
                                         rk[po:po + 64, :],
                                         start=True, stop=True,
                                         tile_position=(po, po))
                    S["tt"].inc(inst)

                def emit_OM(te, hc, c):
                    Xc = X(hc, c)
                    rt = frow(hc, 0)[:, c * C:(c + 1) * C]
                    te.wait_ge(S["am"].h, b["am"] + Xc + 1)
                    te.wait_ge(S["ae"].h, b["ae"] + Xc + 1)
                    te.wait_ge(S["sbm"].h, b["sbm"] + hc * (NTC + 1) + 1 + c)
                    for h in range(2):
                        po = 64 * h
                        te.matmul(PSB[6][po:po + 64, 0:128],
                                  vTT[c % 2][:, po:po + 64],
                                  ASB[c % 2][:, h * 128:(h + 1) * 128],
                                  start=True, stop=False, tile_position=(0, po))
                        te.matmul(PSB[6][po:po + 64, 0:128],
                                  S0b[po:po + 64, :], rt[po:po + 64, :],
                                  start=False, stop=True, tile_position=(po, po))
                        inst = te.matmul(PSB[7][po:po + 64, 0:64],
                                         vTT[c % 2][:, 128 + po:128 + po + 64],
                                         vTT[c % 2][:, po:po + 64],
                                         start=True, stop=True,
                                         tile_position=(0, po))
                    S["om"].inc(inst)

                @blk.tensor
                def _(te):
                    prev = None
                    for hc in range(NHC):
                        for c in range(NTC):
                            emit_G1(te, hc, c)
                            if prev is not None:
                                emit_OM(te, *prev)
                            prev = (hc, c)
                    emit_OM(te, *prev)

                @blk.vector
                def _(ve):
                    def lnw_chain(h):
                        L = LNW[h % 2]
                        ve.wait_ge(S["ax2"].h, b["ax2"] + h + 1)
                        S["cd"].inc(ve.tensor_scalar_mul(L[:, 1:2], L[:, 0:1], -1.0))
                        ve.wait_ge(S["cd"].h, S["cd"].n)
                        S["cd"].inc(ve.tensor_scalar_max(L[:, 2:3], L[:, 1:2],
                                                         LNWMIN))
                        ve.wait_ge(S["cd"].h, S["cd"].n)
                        ve.tensor_scalar_mul(L[:, 3:4], L[:, 2:3], -1.0)
                        ve.tensor_scalar_mul(L[:, 4:5], L[:, 1:2], 127.0)
                        ve.tensor_scalar_mul(L[:, 5:6], L[:, 1:2], 63.0)
                        ve.tensor_scalar_mul(L[:, 6:7], L[:, 1:2], 31.0)
                        S["cd"].inc(ve.tensor_scalar_mul(L[:, 7:8], L[:, 1:2], 15.0))

                    def preps(hc):
                        pm = PM[hc % 2]
                        ve.wait_ge(S["pmb"].h, b["pmb"] + hc + 1)
                        nmkv = "kv" if hc % 2 == 0 else "kv2"
                        ve.wait_ge(S[nmkv].h, b[nmkv] + 48 * (hc // 2 + 1))
                        if hc >= 2:
                            ve.wait_ge(S["om"].h, b["om"] + (hc - 1) * NTC)
                        rfull = kvb["r"][hc % 2][:]
                        kfull = kvb["k"][hc % 2][:]
                        rv = rfull.rearrange("p (a c) -> p a c", c=128)
                        for s, pmo in [(0, 0), (1, 128), (2, 256), (3, 384),
                                       (4, 512), (8, 1152)]:
                            big = pm[:, pmo:pmo + 128].unsqueeze(1).broadcast_to(
                                [P, NTC, 128])
                            ov = frow(hc, s).rearrange("p (a c) -> p a c", c=128)
                            iv = rv if s <= 4 else kfull.rearrange(
                                "p (a c) -> p a c", c=128)
                            ve.tensor_tensor(ov, iv, big, ALU.mult)
                        S["dp"].inc(ve.tensor_tensor(frow(hc, 10), rfull, kfull,
                                                     ALU.mult))

                    def tail(Xc):
                        hc, c = divmod(Xc, NTC)
                        if c == 0:
                            lnw_chain(hc) if False else None
                            ve.wait_ge(S["sbm"].h,
                                       b["sbm"] + hc * (NTC + 1) + 1)
                            if hc >= 1:
                                ve.wait_ge(S["ob"].h, b["ob"] + 16 * hc)
                            ve.memset(S0[:], 0.0)
                        wc = LNW[hc % 2][:, 8:9]
                        ve.wait_ge(S["tt"].h, b["tt"] + Xc + 1)
                        pA = PSB[2 + c % 2]
                        ve.tensor_tensor(ASB[c % 2][:, 0:128], pA[:, 0:128],
                                         self.masku, ALU.mult)
                        S["am"].inc(ve.tensor_tensor(ASB[c % 2][:, 128:256],
                                                     pA[:, 128:256],
                                                     self.masku, ALU.mult))
                        ve.wait_ge(S["om"].h, b["om"] + Xc + 1)
                        vsl = kvb["v"][hc % 2][:, c * C:(c + 1) * C]
                        S["cg"].inc(ve.tensor_tensor(TMPU[c % 2][:], vsl,
                                                     PSB[4 + c % 2][:, 0:128],
                                                     ALU.mult))
                        ve.wait_ge(S["cg"].h, S["cg"].n)
                        ve.tensor_tensor(obb[:, c * C:(c + 1) * C],
                                         PSB[6][:, 0:128], TMPU[c % 2][:],
                                         ALU.add)
                        S["uf"].inc(ve.scalar_tensor_tensor(
                            S0[:], S0[:], wc, PSB[7][:, 0:64],
                            ALU.mult, ALU.add))

                    NX = NHC * NTC
                    lnw_chain(0)
                    preps(0)
                    for Xc in range(NX):
                        hc, c = divmod(Xc, NTC)
                        if c == 1 and hc + 1 < NHC:
                            lnw_chain(hc + 1)
                        if Xc >= 1:
                            tail(Xc - 1)
                        if c == 4 and hc + 1 < NHC:
                            preps(hc + 1)
                    tail(NX - 1)

    def p5b_gn(self):
        nc, S, PSB = self.nc, self.S, self.PSB
        with ExitStack() as ps:
            obb = [ps.enter_context(nc.sbuf_tensor(self.uname(f"g_ob{i}"), [P, T], BF16))
                   for i in range(2)]
            sqb = [ps.enter_context(nc.sbuf_tensor(self.uname(f"g_sq{i}"), [P, T], BF16))
                   for i in range(2)]
            gb = [ps.enter_context(nc.sbuf_tensor(self.uname(f"g_g{i}"), [P, T], BF16))
                  for i in range(2)]
            b = self.snap()
            with nc.Block() as blk:
                @blk.gpsimd
                def _(g_):
                    def t3(hc2):
                        g_.wait_ge(S["cg"].h, b["cg"] + 8 * hc2 + 8)
                        S["ax2"].inc(g_.tensor_scalar(
                            sqb[hc2 % 2][:], sqb[hc2 % 2][:],
                            self.gnw[:, hc2:hc2 + 1], self.gnb[:, hc2:hc2 + 1],
                            ALU.mult, ALU.add))
                    for hc in range(NHC):
                        if hc >= 2:
                            g_.wait_ge(S["gn"].h, b["gn"] + hc - 1)
                        self.dinc("xc", hc, g_.dma_start(obb[hc % 2][:],
                                                         self.obscr[hc, :, :]))
                        self.dinc("gv", hc, g_.dma_start(gb[hc % 2][:],
                                                         self.kvrg["g"][hc, :, :]))
                        if hc >= 1:
                            t3(hc - 1)
                    t3(NHC - 1)

                @blk.scalar
                def _(act):
                    for hc in range(NHC):
                        self.dwait(act, "xc", hc, b)
                        if hc >= 2:
                            act.wait_ge(S["sst"].h, b["sst"] + hc - 1)
                        S["aux"].inc(act.activation(sqb[hc % 2][:], obb[hc % 2][:],
                                                    AF.Square))
                        for ts in range(NTS):
                            slot = 4 * hc + ts
                            pr = slot % 2
                            act.wait_ge(S["sst"].h, b["sst"] + hc + 1)
                            if slot >= 2:
                                act.wait_ge(S["cd"].h, b["cd"] + 2 * (slot - 1))
                                act.wait_ge(S["ln"].h, b["ln"] + slot - 1)
                            S["ca"].inc(act.activation(self.RMUs[pr][0:2, :],
                                                       PSB[ts][0:2, :], AF.Copy,
                                                       scale=1.0 / 64))
                            act.wait_ge(S["ca"].h, S["ca"].n)
                            S["ca"].inc(act.activation(self.RSQs[pr][0:2, :],
                                                       self.RMUs[pr][0:2, :],
                                                       AF.Square))
                            act.wait_ge(S["cd"].h, b["cd"] + 2 * slot + 1)
                            S["ca"].inc(act.activation(self.RVAs[pr][0:2, :],
                                                       self.RVAs[pr][0:2, :], AF.Ln,
                                                       bias=self.CF[0:2, 640:641]))
                            act.wait_ge(S["ca"].h, S["ca"].n)
                            S["ca"].inc(act.activation(self.RSBs[pr][0:2, :],
                                                       self.RVAs[pr][0:2, :], AF.Exp,
                                                       scale=-0.5))

                @blk.tensor
                def _(te):
                    for hc in range(NHC):
                        self.dwait(te, "xc", hc, b)
                        te.wait_ge(S["aux"].h, b["aux"] + hc + 1)
                        if hc >= 1:
                            te.wait_ge(S["cg"].h, b["cg"] + 8 * hc)
                        for ts in range(NTS):
                            te.matmul(PSB[ts][0:2, :], self.hm2,
                                      obb[hc % 2][:, ts * TS:(ts + 1) * TS],
                                      start=True, stop=True)
                        for ts in range(NTS):
                            inst = te.matmul(PSB[4 + ts][0:2, :], self.hm2,
                                             sqb[hc % 2][:, ts * TS:(ts + 1) * TS],
                                             start=True, stop=True)
                        S["sst"].inc(inst)
                        for ts in range(NTS):
                            slot = 4 * hc + ts
                            pr = slot % 2
                            te.wait_ge(S["cd"].h, b["cd"] + 2 * slot + 2)
                            te.matmul(PSB[ts][:, 0:512], self.hm2T,
                                      self.RSBs[pr][0:2, :],
                                      start=True, stop=True)
                            inst = te.matmul(PSB[4 + ts][:, 0:512], self.hm2T,
                                             self.RMBs[pr][0:2, :],
                                             start=True, stop=True)
                            S["ln"].inc(inst)

                @blk.vector
                def _(ve):
                    for hc in range(NHC):
                        for ts in range(NTS):
                            slot = 4 * hc + ts
                            pr = slot % 2
                            ve.wait_ge(S["ca"].h, b["ca"] + 16 * hc + 4 * ts + 2)
                            S["cd"].inc(ve.scalar_tensor_tensor(
                                self.RVAs[pr][0:2, :], PSB[4 + ts][0:2, :], 1.0 / 64,
                                self.RSQs[pr][0:2, :], ALU.mult, ALU.subtract))
                            ve.wait_ge(S["ca"].h, b["ca"] + 16 * hc + 4 * ts + 4)
                            S["cd"].inc(ve.tensor_mul(self.RMBs[pr][0:2, :],
                                                      self.RMUs[pr][0:2, :],
                                                      self.RSBs[pr][0:2, :]))
                        for ts in range(NTS):
                            slot = 4 * hc + ts
                            ve.wait_ge(S["ln"].h, b["ln"] + slot + 1)
                            sl = slice(ts * TS, (ts + 1) * TS)
                            S["cg"].inc(ve.tensor_tensor(
                                sqb[hc % 2][:, sl], obb[hc % 2][:, sl],
                                PSB[ts][:, 0:512], ALU.mult))
                            ve.wait_ge(S["cg"].h, S["cg"].n)
                            S["cg"].inc(ve.tensor_tensor(
                                sqb[hc % 2][:, sl], sqb[hc % 2][:, sl],
                                PSB[4 + ts][:, 0:512], ALU.subtract))
                        ve.wait_ge(S["ax2"].h, b["ax2"] + hc + 1)
                        self.dwait(ve, "gv", hc, b)
                        S["gn"].inc(ve.tensor_mul(
                            self.ACT_A[:, hc * T:(hc + 1) * T],
                            sqb[hc % 2][:], gb[hc % 2][:]))

    def p_final(self):
        nc, S = self.nc, self.S
        with nc.Block() as blk:
            @blk.sync
            def _(sync):
                for nm in ["ys", "ys2", "st", "st2", "ws", "ws2", "ob", "qs", "bc",
                           "rbs", "xc", "xc2", "bt", "bt2", "gv", "gv2"]:
                    if S[nm].n:
                        sync.wait_ge(S[nm].h, S[nm].n)


def _prep_inputs(inputs):
    cbf, cf32, crow = _host_consts()
    vecs = np.zeros((128, 8 * 32), dtype=np.float32)
    order = [inputs["L0_attn_time_mix_key"], inputs["L0_attn_time_mix_value"],
             inputs["L0_attn_time_mix_receptance"], inputs["L0_attn_time_mix_gate"],
             inputs["L0_attn_gn_weight"], inputs["L0_attn_gn_bias"],
             np.asarray(inputs["L0_attn_time_decay"]).reshape(-1),
             np.asarray(inputs["L0_attn_time_faaaa"]).reshape(-1)]
    for i, v in enumerate(order):
        vecs[:, i * 32:(i + 1) * 32] = np.asarray(v, np.float32).reshape(32, 128).T
    A = np.ascontiguousarray(np.asarray(inputs["L0_ffn_A"], np.float32).reshape(128, D))
    Bm = np.ascontiguousarray(
        np.asarray(inputs["L0_ffn_B"], np.float32).transpose(1, 0, 2).reshape(D, 128))
    bvec = np.asarray(inputs["L0_ffn_b"], np.float32)
    base = {
        "wk": np.asarray(inputs["L0_attn_w_key"], np.float32),
        "wv": np.asarray(inputs["L0_attn_w_value"], np.float32),
        "wr": np.asarray(inputs["L0_attn_w_receptance"], np.float32),
        "wg": np.asarray(inputs["L0_attn_w_gate"], np.float32),
        "wo": np.asarray(inputs["L0_attn_w_output"], np.float32),
        "wf": np.asarray(inputs["L0_ffn_W"], np.float32),
        "amat": A, "bmat": Bm,
        "rwm": np.asarray(inputs["L0_ffn_router_W"], np.float32),
        "rbr": np.asarray(inputs["L0_ffn_router_b"], np.float32).reshape(1, 8),
        "brow0": np.ascontiguousarray(bvec[:T]).reshape(1, T),
        "brow1": np.ascontiguousarray(bvec[T:]).reshape(1, T),
        "vecs": vecs, "cbf": cbf, "cf32": cf32, "crow": crow,
    }
    xall = np.asarray(inputs["L0_attn_hidden"], np.float32)
    maps = []
    for core in range(8):
        m = dict(base)
        m["x0"] = np.ascontiguousarray(xall[core])
        maps.append(m)
    return maps


def kernel(**inputs):
    maps = _prep_inputs(inputs)
    nc = KB().build()
    res = run_bass_kernel_spmd(nc, maps, core_ids=list(range(8)))
    out = np.stack([np.asarray(r["y"], np.float32) for r in res.results], axis=0)
    return out
